# version 18
# baseline (speedup 1.0000x reference)
import contextlib
import numpy as np
import ml_dtypes
import concourse.bass as bass
import concourse.mybir as mybir
from concourse.bass_utils import run_bass_kernel_spmd
from concourse.aot_env import allow_ici_all_to_all

F32 = mybir.dt.float32
BF16 = mybir.dt.bfloat16
AF = mybir.ActivationFunctionType
ALU = mybir.AluOpType
AX = mybir.AxisListType

NCORES = 8
B, T, C = 2, 8192, 2048
H, DH = 16, 128
HID, RH, PH = 5632, 704, 768
HPAD = NCORES * PH
MYT = 2048
NT = MYT // 128
SM_SCALE = float(DH) ** -0.5
RND_C = 1.5 * 2.0 ** 23
EPS_LN = 1e-5
EPS_Q = 1e-8
WCNT = float(HID * C)
WIN = 1024
NW = T // WIN
NTILE_W12 = 6


class Ctr:
    __slots__ = ("h", "v")
    def __init__(self, h):
        self.h = h
        self.v = 0

    def inc(self, ins, k=1):
        ins.then_inc(self.h, k)
        self.v += k
        return self.v


def w12_rows(j):
    r0 = j * 128
    return r0, min(RH, r0 + 128) - r0


def build_nc(debug_taps=False):
    nc = bass.Bass("TRN2", target_bir_lowering=False, detect_race_conditions=True)

    x_my = nc.dram_tensor("x_my", [MYT, C], F32, kind="ExternalInput")
    wq_my = nc.dram_tensor("wq_my", [2, C, DH], F32, kind="ExternalInput")
    wk_my = nc.dram_tensor("wk_my", [2, C, DH], F32, kind="ExternalInput")
    wv_my = nc.dram_tensor("wv_my", [2, C, DH], F32, kind="ExternalInput")
    bq_my = nc.dram_tensor("bq_my", [DH, 2], F32, kind="ExternalInput")
    bk_my = nc.dram_tensor("bk_my", [DH, 2], F32, kind="ExternalInput")
    bv_my = nc.dram_tensor("bv_my", [DH, 2], F32, kind="ExternalInput")
    wo_in = nc.dram_tensor("wo_in", [C, C], F32, kind="ExternalInput")
    bo_in = nc.dram_tensor("bo_in", [1, C], F32, kind="ExternalInput")
    w1_my = nc.dram_tensor("w1_my", [RH, C], F32, kind="ExternalInput")
    w2_my = nc.dram_tensor("w2_my", [RH, C], F32, kind="ExternalInput")
    w3_my = nc.dram_tensor("w3_my", [256, HID], F32, kind="ExternalInput")
    ident_in = nc.dram_tensor("ident_in", [128, 128], F32, kind="ExternalInput")
    ones_in = nc.dram_tensor("ones_in", [128, 128], F32, kind="ExternalInput")
    mask_in = nc.dram_tensor("mask_in", [128, 2048], F32, kind="ExternalInput")
    out_my = nc.dram_tensor("out_my", [MYT, C], F32, kind="ExternalOutput")

    xn_cq = [nc.dram_tensor(f"xn_cq{q}", [C, 512], BF16, kind="Internal") for q in range(4)]
    xn_aq = [nc.dram_tensor(f"xn_aq{q}", [NCORES * C, 512], BF16, kind="Internal") for q in range(4)]
    gam_c = nc.dram_tensor("gam_c", [1, 4], F32, kind="Internal")
    gam_s = nc.dram_tensor("gam_s", [1, 4], F32, kind="Internal")
    w1qc = nc.dram_tensor("w1qc", [RH, C], BF16, kind="Internal")
    w2qc = nc.dram_tensor("w2qc", [RH, C], BF16, kind="Internal")
    w3qc = nc.dram_tensor("w3qc", [256, HID], BF16, kind="Internal")
    w1q_all = nc.dram_tensor("w1q_all", [NCORES * RH, C], BF16, kind="Internal")
    w2q_all = nc.dram_tensor("w2q_all", [NCORES * RH, C], BF16, kind="Internal")
    w3q_all = nc.dram_tensor("w3q_all", [NCORES * 256, HID], BF16, kind="Internal")
    wo_bf = nc.dram_tensor("wo_bf", [C, C], BF16, kind="Internal")
    yt_c = nc.dram_tensor("yt_c", [2048, 2048], BF16, kind="Internal")
    yt_my = nc.dram_tensor("yt_my", [2048, 2048], BF16, kind="Internal")
    g_sc = nc.dram_tensor("g_sc", [MYT, C], BF16, kind="Internal")

    rg = [list(range(NCORES))]
    ctx = contextlib.ExitStack()

    def sem(name):
        return Ctr(ctx.enter_context(nc.semaphore(name)))

    def sb(name, shape, dt):
        return ctx.enter_context(nc.sbuf_tensor(name, shape, dt))

    ident_bf = sb("ident_bf", [128, 128], BF16)
    ident_f = sb("ident_f", [128, 128], F32)
    ones_f = sb("ones_f", [128, 128], F32)
    ones_bf = sb("ones_bf", [128, 1], BF16)
    stat = sb("stat", [128, 64], F32)
    grow = sb("grow", [1, 16], F32)
    rq_sb = sb("rq_sb", [128, 4], F32)
    negb = sb("negb", [128, 4], F32)
    bmax = sb("bmax", [1, 64], F32)
    biasT = sb("biasT", [128, 6], F32)

    psA = ctx.enter_context(nc.psum_tensor("psA", [128, 2048], F32))
    psB = ctx.enter_context(nc.psum_tensor("psB", [128, 2048], F32))

    s_ld = sem("s_ld")
    g_ld = sem("g_ld")
    s_tp = sem("s_tp")
    v_ms = sem("v_ms")
    a_ms = sem("a_ms")
    t_ms = sem("t_ms")
    g_st = sem("g_st")
    cc0 = sem("cc0"); cc2 = sem("cc2")
    ccq = [sem(f"ccq{q}") for q in range(4)]
    cc3a = sem("cc3a"); cc3b = sem("cc3b"); cc3c = sem("cc3c")
    g_c = sem("g_c")
    v_sf = sem("v_sf")
    a_sf = sem("a_sf")
    sx = [sem("sx0"), sem("sx1")]
    gxn = [sem("gxn0"), sem("gxn1")]
    gy = [sem("gy0"), sem("gy1")]
    gg = [sem("gg0"), sem("gg1")]
    sw3 = [sem(f"sw3{i}") for i in range(4)]
    g_yw = sem("g_yw")
    swp = [sem("swp0"), sem("swp1")]
    stw = [sem("stw0"), sem("stw1")]
    gq = [sem("gq0"), sem("gq1")]
    vq = sem("vq")
    g_wl = sem("g_wl")
    swq = [sem("swq0"), sem("swq1")]
    gt3s = [sem("gt3s0"), sem("gt3s1")]
    sx3 = [sem("sx3a"), sem("sx3b")]
    go = [sem("go0"), sem("go1")]
    bar = sem("bar")

    blk = ctx.enter_context(nc.Block())
    barn = [0]

    def barrier():
        barn[0] += 1
        target = 5 * barn[0]
        caps = dict(sync=(s_ld, s_tp, sx[0], sx[1], sw3[0], sw3[1], sw3[2], sw3[3], sx3[0], sx3[1], swp[0], swp[1], stw[0], stw[1]),
                    gpsimd=(g_ld, g_st, g_c, gxn[0], gxn[1], gy[0], gy[1], gg[0], gg[1],
                            g_yw, gt3s[0], gt3s[1], go[0], go[1], gq[0], gq[1], g_wl),
                    tensor=(t_ms,), scalar=(a_ms, a_sf, swq[0], swq[1]), vector=(v_ms, v_sf, vq))
        for name, dec in (("sync", blk.sync), ("gpsimd", blk.gpsimd), ("tensor", blk.tensor),
                          ("scalar", blk.scalar), ("vector", blk.vector)):
            def body(e, name=name):
                for s in caps[name]:
                    e.wait_ge(s.h, s.v)
                e.sem_inc(bar.h, 1)
                e.wait_ge(bar.h, target)
            dec(body)
        bar.v = target

    WQW = 2816
    WT2 = []
    for j in range(NTILE_W12):
        r0, nr = w12_rows(j)
        WT2.append((w1_my, r0, nr, 0, C, w1qc, r0, 0))
    for j in range(NTILE_W12):
        r0, nr = w12_rows(j)
        WT2.append((w2_my, r0, nr, 0, C, w2qc, r0, 1))
    for j in range(2):
        for hf in range(2):
            WT2.append((w3_my, j * 128, 128, hf * WQW, HID - hf * WQW if hf else WQW,
                        w3qc, j * 128, 2))
    NWT2 = len(WT2)

    pW = contextlib.ExitStack()
    def sbW(name, shape, dt):
        return pW.enter_context(nc.sbuf_tensor(name, shape, dt))
    wqa = sbW("wqa", [128, WQW], F32)
    qbf = [sbW("qbf0", [128, WQW], BF16), sbW("qbf1", [128, WQW], BF16)]
    wqkv = sbW("wqkv", [128, 6 * C], BF16)
    mask_bf = sbW("mask_bf", [128, 2048], BF16)

    pLN = contextlib.ExitStack()
    def sbL(name, shape, dt):
        return pLN.enter_context(nc.sbuf_tensor(name, shape, dt))
    xt = [sbL("xt0", [128, C], F32), sbL("xt1", [128, C], F32)]
    xsq = sbL("xsq", [128, C], F32)
    xnf = [sbL("xnf0", [128, C], F32), sbL("xnf1", [128, C], F32)]
    xn_st = [sbL("xnst0", [128, 16 * 512], BF16), sbL("xnst1", [128, 16 * 512], BF16)]

    def body(e):
        e.dma_start(out=ident_bf[:], in_=ident_in[:, :]).then_inc(g_c.h, 16)
        e.dma_start(out=ones_f[:], in_=ones_in[:, :]).then_inc(g_c.h, 16)
        e.dma_start(out=ones_bf[:], in_=ones_in[:, 0:1], allow_slow_non_contiguous=True).then_inc(g_c.h, 16)
        e.dma_start(out=biasT[:, 0:2], in_=bq_my[:, :]).then_inc(g_c.h, 16)
        e.dma_start(out=biasT[:, 2:4], in_=bk_my[:, :]).then_inc(g_c.h, 16)
        e.dma_start(out=biasT[:, 4:6], in_=bv_my[:, :]).then_inc(g_c.h, 16)
        e.dma_start(out=ident_f[:], in_=ident_in[:, :]).then_inc(g_c.h, 16)
        e.dma_start(out=mask_bf[:], in_=mask_in[:, :]).then_inc(g_ld.h, 16)
        g_ld.v += 16
        for idx, wsrc in enumerate((wq_my, wk_my, wv_my)):
            for hh in range(2):
                base = (idx * 2 + hh) * C
                for q in range(4):
                    e.dma_start(out=wqkv[:, base + q * 512:base + (q + 1) * 512],
                                in_=wsrc[hh, q * 512:(q + 1) * 512, :]
                                .rearrange("(a p) d -> p a d", p=128)
                                ).then_inc(g_ld.h, 16)
                    g_ld.v += 16
        wsrc0, r0, nr, c0, wd, _, _, _ = WT2[0]
        e.dma_start(out=wqa[0:nr, 0:wd], in_=wsrc0[r0:r0 + nr, c0:c0 + wd]
                    ).then_inc(g_wl.h, 16)
        g_wl.v += 16
    blk.gpsimd(body)
    g_c.v = 7 * 16
    GC_ALL = g_c.v
    GLD_QKV = g_ld.v

    VLN0 = v_ms.v
    ALN0 = a_ms.v
    TLN0 = t_ms.v
    GXN_B = (gxn[0].v, gxn[1].v)

    def body(e):
        for i in range(NT):
            if i >= 2:
                e.wait_ge(v_ms.h, VLN0 + 6 * (i - 2) + 4)
            e.dma_start(out=xt[i % 2][:], in_=x_my[i * 128:(i + 1) * 128, :]
                        ).then_inc(sx[i % 2].h, 16)
            sx[i % 2].v += 16
    blk.sync(body)

    def body(e):
        for i in range(NT):
            e.wait_ge(sx[i % 2].h, 16 * (i // 2 + 1))
            if i >= 1:
                e.wait_ge(v_ms.h, VLN0 + 6 * (i - 1) + 4)
            e.activation(xsq[:], xt[i % 2][:], AF.Square).then_inc(a_ms.h, 1)
            e.wait_ge(v_ms.h, VLN0 + 6 * i + 3)
            e.activation(stat[:, 56:57], stat[:, 16 + i:17 + i], AF.Ln).then_inc(a_ms.h, 1)
            e.wait_ge(a_ms.h, ALN0 + 5 * i + 2)
            e.activation(stat[:, 52 + i % 2:53 + i % 2], stat[:, 56:57], AF.Exp,
                         scale=-0.5).then_inc(a_ms.h, 1)
            di = i // 4
            psX = psA if i % 2 == 0 else psB
            for q in (0, 1):
                e.wait_ge(t_ms.h, TLN0 + 16 * i + 4 * (q + 1))
                if di >= 2 and i % 4 == 0 and q == 0:
                    e.wait_ge(gxn[di % 2].h, GXN_B[di % 2] + 64 * (di // 2))
                e.activation(xn_st[di % 2][:, (i % 4) * 2048 + q * 512:
                                           (i % 4) * 2048 + (q + 1) * 512],
                             psX[:, q * 512:(q + 1) * 512], AF.Identity,
                             bias=stat[:, 60:61]).then_inc(a_ms.h, 1)
    blk.scalar(body)
    a_ms.v += 5 * NT

    def body(e):
        sfb = v_sf.v
        e.memset(stat[:, 60:61], 0.0).then_inc(v_sf.h, 1)
        sfb += 1
        for i in range(NT):
            e.wait_ge(sx[i % 2].h, 16 * (i // 2 + 1))
            e.reduce_sum(stat[:, i:i + 1], xt[i % 2][:], axis=AX.X).then_inc(v_ms.h, 1)
            e.wait_ge(a_ms.h, ALN0 + 5 * i + 1)
            e.reduce_sum(stat[:, 16 + i:17 + i], xsq[:], axis=AX.X).then_inc(v_ms.h, 1)
            e.wait_ge(v_ms.h, VLN0 + 6 * i + 2)
            e.tensor_scalar(stat[:, i:i + 1], stat[:, i:i + 1], -1.0 / C, None,
                            ALU.mult).then_inc(v_sf.h, 1)
            e.wait_ge(v_sf.h, sfb + 4 * i + 1)
            e.tensor_tensor(stat[:, 54:55], stat[:, i:i + 1], stat[:, i:i + 1],
                            ALU.mult).then_inc(v_sf.h, 1)
            e.tensor_scalar(stat[:, 16 + i:17 + i], stat[:, 16 + i:17 + i], 1.0 / C, EPS_LN,
                            ALU.mult, ALU.add).then_inc(v_sf.h, 1)
            e.wait_ge(v_sf.h, sfb + 4 * i + 3)
            e.tensor_tensor(stat[:, 16 + i:17 + i], stat[:, 16 + i:17 + i], stat[:, 54:55],
                            ALU.subtract).then_inc(v_ms.h, 1)
            e.wait_ge(a_ms.h, ALN0 + 5 * i + 3)
            e.tensor_scalar(xsq[:], xt[i % 2][:], stat[:, i:i + 1], None,
                            ALU.add).then_inc(v_sf.h, 1)
            e.wait_ge(v_sf.h, sfb + 4 * i + 4)
            if i >= 2:
                e.wait_ge(t_ms.h, TLN0 + 16 * (i - 1))
            e.tensor_scalar(xnf[i % 2][:], xsq[:], stat[:, 52 + i % 2:53 + i % 2], None,
                            ALU.mult).then_inc(v_ms.h, 1)
            di = i // 4
            psX = psA if i % 2 == 0 else psB
            for q in (2, 3):
                e.wait_ge(t_ms.h, TLN0 + 16 * i + 4 * (q + 1))
                if di >= 2 and i % 4 == 0 and q == 2:
                    e.wait_ge(gxn[di % 2].h, GXN_B[di % 2] + 64 * (di // 2))
                e.tensor_copy(xn_st[di % 2][:, (i % 4) * 2048 + q * 512:
                                            (i % 4) * 2048 + (q + 1) * 512],
                              psX[:, q * 512:(q + 1) * 512]).then_inc(v_ms.h, 1)
    blk.vector(body)
    v_ms.v += 6 * NT
    v_sf.v += 4 * NT + 1

    def body(e):
        e.wait_ge(g_c.h, GC_ALL)
        for i in range(NT):
            e.wait_ge(v_ms.h, VLN0 + 6 * i + 4)
            psX = psA if i % 2 == 0 else psB
            for q in range(4):
                if i >= 2:
                    if q < 2:
                        e.wait_ge(a_ms.h, ALN0 + 5 * (i - 2) + 4 + q)
                    else:
                        e.wait_ge(v_ms.h, VLN0 + 6 * (i - 2) + 3 + q)
                for j in range(4):
                    cc = 4 * q + j
                    e.transpose(psX[:, q * 512 + j * 128:q * 512 + (j + 1) * 128],
                                xnf[i % 2][:, cc * 128:(cc + 1) * 128],
                                ident_f[:]).then_inc(t_ms.h, 1)
    blk.tensor(body)
    t_ms.v += 16 * NT

    def body(e):
        for di in range(NT // 4):
            xo = xn_cq[di].ap().rearrange("(a p) t -> p a t", p=128)
            for f in range(4):
                e.wait_ge(a_ms.h, ALN0 + 5 * (4 * di + f + 1))
                e.wait_ge(v_ms.h, VLN0 + 6 * (4 * di + f + 1))
                e.dma_start(out=xo[:, :, f * 128:(f + 1) * 128],
                            in_=xn_st[di % 2][:, f * 2048:(f + 1) * 2048]
                            .rearrange("p (a t) -> p a t", a=16)
                            ).then_inc(gxn[di % 2].h, 16)
                gxn[di % 2].v += 16
            e.wait_ge(gxn[di % 2].h, gxn[di % 2].v)
            e.collective_compute("AllGather", ALU.bypass, replica_groups=rg,
                                 ins=[xn_cq[di].ap().opt()],
                                 outs=[xn_aq[di].ap().opt()]).then_inc(ccq[di].h)
            ccq[di].v = 1
    blk.gpsimd(body)

    barrier()
    pLN.close()

    p1 = contextlib.ExitStack()
    def sb1(name, shape, dt):
        return p1.enter_context(nc.sbuf_tensor(name, shape, dt))
    xnT = [sb1("xnTa", [128, 16 * 512], BF16), sb1("xnTb", [128, 16 * 512], BF16)]
    qT2 = sb1("qT2", [128, 2 * T], BF16)
    kT2 = sb1("kT2", [128, 2 * T], BF16)
    v2 = sb1("v2", [128, 2 * T], BF16)
    vTs = sb1("vTs", [128, 512], F32)
    pT = [sb1(f"pT{i}", [128, 512], BF16) for i in range(6)]
    acc = sb1("acc", [128, 512], F32)
    rden_sb = sb1("rden_sb", [1, 512], F32)
    dtmp = sb1("dtmp", [1, 1024], F32)
    bc_sb = sb1("bc_sb", [128, 512], F32)
    yn_bf = [sb1(f"yn{i}", [128, 512], BF16) for i in range(2)]

    VT_B0 = v_ms.v
    def body(e):
        for k, (wsrc, r0, nr, c0, wd, dst, dr, wid) in enumerate(WT2):
            if k == 0:
                continue
            e.wait_ge(v_ms.h, VT_B0 + 9 * k)
            e.dma_start(out=wqa[0:nr, 0:wd], in_=wsrc[r0:r0 + nr, c0:c0 + wd]
                        ).then_inc(g_wl.h, 16)
            g_wl.v += 16
    blk.gpsimd(body)

    def body(e):
        e.dma_start(out=wo_bf[:, :], in_=wo_in[:, :]).then_inc(g_st.h, 16)
    blk.gpsimd(body)
    g_st.v += 16
    WO_CAST_G = g_st.v
    GLDQ0 = g_wl.v
    GQ_BASE = (gq[0].v, gq[1].v)
    V_SF_FOLD = None
    V_FOLD = None
    QSH = ({qb: qb - 6 for qb in range(6, 16)}, {qb: 10 + qb for qb in range(6)})
    RSH = ({qb: qb - 5 for qb in range(5, 16)}, {qb: 11 + qb for qb in range(5)})

    for b in range(B):
        PWIN = 512
        NPW = T // PWIN
        projT0 = t_ms.v
        projA0 = a_ms.v
        vtV0 = v_ms.v
        STP0 = s_tp.v

        STW_BASE = (stw[0].v, stw[1].v)
        worder = ([r * 4 + q for q in range(4) for r in range(4)] if b == 0
                  else list(range(NPW)))
        def body(e, b=b, projT0=projT0, worder=worder):
            for p in range(NPW):
                w = worder[p]
                if p <= 1:
                    if b == 1:
                        e.wait_ge(t_ms.h, projT0)
                else:
                    e.wait_ge(t_ms.h, projT0 + (p - 2) * 14 + 6)
                if b == 0 and p % 4 == 0:
                    e.wait_ge(ccq[p // 4].h, 1)
                gtok = b * T + w * PWIN
                r = gtok // MYT
                cq = (gtok % MYT) // PWIN
                for cchunk in range(16):
                    src = xn_aq[cq][r * C + cchunk * 128: r * C + (cchunk + 1) * 128, :]
                    e.dma_start(out=xnT[p % 2][:, cchunk * PWIN:(cchunk + 1) * PWIN], in_=src
                                ).then_inc(stw[p % 2].h, 16)
                stw[p % 2].v += 256
        blk.sync(body)

        def body(e, b=b, projT0=projT0, projA0=projA0, vtV0=vtV0, STW_BASE=STW_BASE,
                 worder=worder):
            if b == 0:
                e.wait_ge(g_ld.h, GLD_QKV)
            e.wait_ge(a_ms.h, projA0)
            e.wait_ge(v_ms.h, vtV0)
            for p in range(NPW):
                w = worder[p]
                e.wait_ge(stw[p % 2].h, STW_BASE[p % 2] + 256 * (p // 2 + 1))
                for rnd in range(6):
                    gr = p * 6 + rnd
                    qtr = rnd % 4
                    ps = psA[:, qtr * 512:(qtr + 1) * 512]
                    if gr >= 4:
                        e.wait_ge(a_ms.h, projA0 + gr - 3)
                    kind, hh = divmod(rnd, 2)
                    last = None
                    for cchunk in range(16):
                        lhsT = wqkv[:, (kind * 2 + hh) * C + cchunk * 128:
                                    (kind * 2 + hh) * C + (cchunk + 1) * 128]
                        last = e.matmul(ps, lhsT,
                                 xnT[p % 2][:, cchunk * PWIN:(cchunk + 1) * PWIN],
                                 start=(cchunk == 0), stop=(cchunk == 15))
                    last.then_inc(t_ms.h, 1)
                for hh in range(2):
                    e.wait_ge(a_ms.h, projA0 + p * 6 + 4 + hh + 1)
                    for j in range(4):
                        tki = p * 8 + hh * 4 + j
                        if tki >= 2:
                            tp = tki - 2
                            idx = 9 * (tp // 8) + tp % 8 + 1 if b == 0 else tp + 1
                            e.wait_ge(v_ms.h, vtV0 + idx)
                        psT = psB[:, 0:128] if tki % 2 == 0 else psB[:, 512:640]
                        e.transpose(psT, vTs[:, j * 128:(j + 1) * 128],
                                    ident_f[:]).then_inc(t_ms.h, 1)
        blk.tensor(body)
        t_ms.v += NPW * 14

        def body(e, b=b, projT0=projT0, projA0=projA0, worder=worder):
            for p in range(NPW):
                w = worder[p]
                for rnd in range(6):
                    kind, hh = divmod(rnd, 2)
                    e.wait_ge(t_ms.h, projT0 + p * 14 + rnd + 1)
                    if kind == 2 and hh == 1:
                        e.wait_ge(t_ms.h, projT0 + p * 14 + 6 + 4)
                    bcol = biasT[:, kind * 2 + hh:kind * 2 + hh + 1]
                    if kind == 0:
                        dst = qT2[:, hh * T + w * PWIN: hh * T + (w + 1) * PWIN]
                    elif kind == 1:
                        dst = kT2[:, hh * T + w * PWIN: hh * T + (w + 1) * PWIN]
                    else:
                        dst = vTs[:, 0:PWIN]
                    qtr = rnd % 4
                    ps = psA[:, qtr * 512:(qtr + 1) * 512]
                    e.activation(dst, ps, AF.Identity, bias=bcol).then_inc(a_ms.h, 1)
        blk.scalar(body)
        a_ms.v += NPW * 6

        def body(e, b=b, projT0=projT0, worder=worder):
            for p in range(NPW):
                w = worder[p]
                for hh in range(2):
                    for j in range(4):
                        tki = p * 8 + hh * 4 + j
                        psT = psB[:, 0:128] if tki % 2 == 0 else psB[:, 512:640]
                        e.wait_ge(t_ms.h, projT0 + p * 14 + 6 + hh * 4 + j + 1)
                        kb = w * 4 + j
                        e.tensor_copy(v2[:, hh * T + kb * 128:hh * T + (kb + 1) * 128],
                                      psT).then_inc(v_ms.h, 1)
                if b == 0:
                    k = p
                    wsrc, r0, nr, c0, wd, dst, dr, wid = WT2[k]
                    e.wait_ge(g_wl.h, (k + 1) * 16)
                    e.reduce_sum(stat[0:nr, 32 + k:33 + k], wqa[0:nr, 0:wd], axis=AX.X,
                                 apply_absolute_value=True).then_inc(v_ms.h, 1)
                    if nr < 128:
                        e.memset(stat[64:128, 32 + k:33 + k], 0.0)
            if b == 0:
                e.wait_ge(v_ms.h, vtV0 + NPW * 9)
                e.reduce_sum(stat[:, 48:49], stat[:, 32:38], axis=AX.X).then_inc(v_sf.h, 1)
                e.reduce_sum(stat[:, 49:50], stat[:, 38:44], axis=AX.X).then_inc(v_sf.h, 1)
                e.memset(stat[:, 51:52], 0.0).then_inc(v_sf.h, 1)
                e.reduce_sum(stat[:, 50:51], stat[:, 44:48], axis=AX.X).then_inc(v_ms.h, 1)
        blk.vector(body)
        if b == 0:
            assert vtV0 == VT_B0
            v_ms.v += NPW * 9 + 1
            v_sf.v += 3
            V_SF_FOLD = v_sf.v
            V_FOLD = v_ms.v
        else:
            v_ms.v += NPW * 8
        if b == 0:
            VTE_INS = v_ms.v
            def body(e, VTE_INS=VTE_INS):
                e.wait_ge(v_ms.h, V_FOLD)
                e.wait_ge(v_sf.h, V_SF_FOLD)
                e.wait_ge(v_ms.h, VTE_INS)
                e.matmul(psB[0:1, 0:4], ones_f[:, 0:1], stat[:, 48:52], start=True,
                         stop=True).then_inc(t_ms.h, 1)
            blk.tensor(body)
            t_ms.v += 1
            T_GAM = t_ms.v
            def body(e, T_GAM=T_GAM):
                e.wait_ge(t_ms.h, T_GAM)
                e.tensor_copy(grow[0:1, 0:4], psB[0:1, 0:4]).then_inc(v_ms.h, 1)
            blk.vector(body)
            v_ms.v += 1
            V_GROW = v_ms.v
            def body(e, V_GROW=V_GROW):
                e.wait_ge(v_ms.h, V_GROW)
                e.dma_start(out=gam_c[:, :], in_=grow[0:1, 0:4]).then_inc(g_st.h, 16)
                e.wait_ge(g_st.h, g_st.v + 16)
                e.collective_compute("AllReduce", ALU.add, replica_groups=rg,
                                     ins=[gam_c.ap().opt()],
                                     outs=[gam_s.ap().opt()]).then_inc(cc0.h)
                e.wait_ge(cc0.h, 1)
                e.dma_start(out=grow[0:1, 8:12], in_=gam_s[:, :]).then_inc(g_ld.h, 16)
            blk.gpsimd(body)
            g_st.v += 16
            g_ld.v += 16
            GLD_GAMS = g_ld.v
            cc0.v = 1
            def body(e, GLD_GAMS=GLD_GAMS):
                e.wait_ge(g_ld.h, GLD_GAMS)
                e.tensor_scalar(grow[0:1, 12:16], grow[0:1, 8:12], 1.0 / WCNT, EPS_Q,
                                ALU.mult, ALU.add).then_inc(v_sf.h, 1)
                e.wait_ge(v_sf.h, v_sf.v + 1)
                e.reciprocal(grow[0:1, 4:8], grow[0:1, 12:16]).then_inc(v_ms.h, 1)
            blk.vector(body)
            v_ms.v += 1
            v_sf.v += 1
            V_RECIP = v_ms.v
            def body(e, V_RECIP=V_RECIP):
                e.wait_ge(v_ms.h, V_RECIP)
                e.matmul(psB[:, 8:12], ones_f[0:1, :], grow[0:1, 4:8], start=True,
                         stop=True).then_inc(t_ms.h, 1)
            blk.tensor(body)
            t_ms.v += 1
            T_RQ = t_ms.v
            def body(e, T_RQ=T_RQ):
                e.wait_ge(t_ms.h, T_RQ)
                e.tensor_copy(rq_sb[:], psB[:, 8:12]).then_inc(v_ms.h, 1)
            blk.vector(body)
            v_ms.v += 1
            V_RQ = v_ms.v

        PROJ_A_END = a_ms.v
        VT_V_END = v_ms.v

        for hh in range(2):
            pair = b * 2 + hh
            TKD = t_ms.v
            def body(e, hh=hh, TKD=TKD):
                e.wait_ge(a_ms.h, PROJ_A_END)
                e.wait_ge(t_ms.h, TKD)
                e.tensor_tensor(xnT[0][:, 0:T], qT2[:, hh * T:(hh + 1) * T],
                                qT2[:, hh * T:(hh + 1) * T], ALU.mult).then_inc(v_ms.h, 1)
            blk.vector(body)
            v_ms.v += 1
            QSQ_V = v_ms.v

            BQ_T0 = t_ms.v
            def body(e, QSQ_V=QSQ_V):
                e.wait_ge(v_ms.h, QSQ_V)
                for ch in range(16):
                    if ch >= 2:
                        e.wait_ge(v_ms.h, QSQ_V + ch - 1)
                    ps = psB[0:1, 1536:2048] if ch % 2 == 0 else psB[0:1, 1024:1536]
                    e.matmul(ps, ones_bf[:, 0:1], xnT[0][:, ch * 512:(ch + 1) * 512],
                             start=True, stop=True).then_inc(t_ms.h, 1)
            blk.tensor(body)
            t_ms.v += 16

            def body(e, BQ_T0=BQ_T0, pair=pair, QSQ_V=QSQ_V):
                for ch in range(16):
                    ps = psB[0:1, 1536:2048] if ch % 2 == 0 else psB[0:1, 1024:1536]
                    e.wait_ge(t_ms.h, BQ_T0 + ch + 1)
                    e.reduce_max(bmax[0:1, ch:ch + 1], ps, axis=AX.X).then_inc(v_ms.h, 1)
                e.wait_ge(v_ms.h, QSQ_V + 16)
                e.reduce_max(bmax[0:1, 32 + pair:33 + pair], bmax[0:1, 0:16],
                             axis=AX.X).then_inc(v_ms.h, 1)
            blk.vector(body)
            v_ms.v += 17

            QD = t_ms.v
            def body(e, hh=hh, QD=QD):
                e.wait_ge(t_ms.h, QD)
                e.tensor_tensor(xnT[0][:, 0:T], kT2[:, hh * T:(hh + 1) * T],
                                kT2[:, hh * T:(hh + 1) * T], ALU.mult).then_inc(v_ms.h, 1)
            blk.vector(body)
            v_ms.v += 1
            KSQ_V = v_ms.v

            BK_T0 = t_ms.v
            def body(e, KSQ_V=KSQ_V):
                e.wait_ge(v_ms.h, KSQ_V)
                for ch in range(16):
                    if ch >= 2:
                        e.wait_ge(v_ms.h, KSQ_V + ch - 1)
                    ps = psB[0:1, 1536:2048] if ch % 2 == 0 else psB[0:1, 1024:1536]
                    e.matmul(ps, ones_bf[:, 0:1], xnT[0][:, ch * 512:(ch + 1) * 512],
                             start=True, stop=True).then_inc(t_ms.h, 1)
            blk.tensor(body)
            t_ms.v += 16

            def body(e, BK_T0=BK_T0, pair=pair, KSQ_V=KSQ_V):
                for ch in range(16):
                    ps = psB[0:1, 1536:2048] if ch % 2 == 0 else psB[0:1, 1024:1536]
                    e.wait_ge(t_ms.h, BK_T0 + ch + 1)
                    e.reduce_max(bmax[0:1, ch:ch + 1], ps, axis=AX.X).then_inc(v_ms.h, 1)
                e.wait_ge(v_ms.h, KSQ_V + 16)
                e.reduce_max(bmax[0:1, 40 + pair:41 + pair], bmax[0:1, 0:16],
                             axis=AX.X).then_inc(v_sf.h, 1)
                e.wait_ge(v_sf.h, v_sf.v + 1)
                e.tensor_tensor(bmax[0:1, 48 + pair:49 + pair], bmax[0:1, 32 + pair:33 + pair],
                                bmax[0:1, 40 + pair:41 + pair], ALU.mult).then_inc(v_ms.h, 1)
            blk.vector(body)
            v_ms.v += 17
            v_sf.v += 1
            PRD_V = v_ms.v

            def body(e, pair=pair, PRD_V=PRD_V):
                e.wait_ge(v_ms.h, PRD_V)
                e.activation(bmax[0:1, 56 + pair:57 + pair], bmax[0:1, 48 + pair:49 + pair],
                             AF.Ln).then_inc(a_sf.h, 1)
                e.wait_ge(a_sf.h, a_sf.v + 1)
                e.activation(bmax[0:1, 48 + pair:49 + pair], bmax[0:1, 56 + pair:57 + pair],
                             AF.Exp, scale=0.5).then_inc(a_ms.h, 1)
            blk.scalar(body)
            a_ms.v += 1
            a_sf.v += 1
            SQ_A = a_ms.v

            def body(e, pair=pair, SQ_A=SQ_A):
                e.wait_ge(a_ms.h, SQ_A)
                e.tensor_scalar(bmax[0:1, 56 + pair:57 + pair], bmax[0:1, 48 + pair:49 + pair],
                                -SM_SCALE, None, ALU.mult).then_inc(v_ms.h, 1)
            blk.vector(body)
            v_ms.v += 1
            NB_V = v_ms.v

            def body(e, NB_V=NB_V, pair=pair):
                e.wait_ge(v_ms.h, NB_V)
                e.matmul(psB[:, 1024:1025], ones_f[0:1, :], bmax[0:1, 56 + pair:57 + pair],
                         start=True, stop=True).then_inc(t_ms.h, 1)
            blk.tensor(body)
            t_ms.v += 1
            NB_T = t_ms.v

            def body(e, NB_T=NB_T, pair=pair):
                e.wait_ge(t_ms.h, NB_T)
                e.tensor_copy(negb[:, pair:pair + 1], psB[:, 1024:1025]).then_inc(v_ms.h, 1)
            blk.vector(body)
            v_ms.v += 1
            NEGB_V = v_ms.v

            qk_t0 = t_ms.v
            ae_a0 = a_ms.v
            acc_v0 = v_ms.v
            NKBS = [4 * (q + 1) for q in range(16)]
            klist = [(qb, j) for qb in range(16) for j in range(NKBS[qb])]
            NKI = len(klist)
            DLAG = 3
            NSL = 5
            NPT = 6
            def vb(qb):
                return acc_v0 + 6 * qb
            def mask_inc(ki):
                qb, j = klist[ki]
                return vb(qb) + (j - (NKBS[qb] - 4)) + 1
            def is_diag(ki):
                qb, j = klist[ki]
                return j >= NKBS[qb] - 4
            sched = []
            for ki in range(NKI + DLAG):
                if ki < NKI:
                    sched.append(("qk", ki))
                kj = ki - DLAG
                if kj >= 0:
                    sched.append(("pvd", kj))
                    qb, j = klist[kj]
                    if j == NKBS[qb] - 1:
                        sched.append(("bc", qb))
            qkI, pvdI, bcI = {}, {}, {}
            for idx, (kind, a) in enumerate(sched):
                {"qk": qkI, "pvd": pvdI, "bc": bcI}[kind][a] = qk_t0 + idx + 1
            def st_slot(ki):
                s = ki % NSL
                return psA[:, s * 512:(s + 1) * 512] if s < 4 else psB[:, 1024:1536]

            def body(e, hh=hh, pair=pair, NEGB_V=NEGB_V, VT_V_END=VT_V_END,
                     vb=vb, mask_inc=mask_inc, st_slot=st_slot, ae_a0=ae_a0,
                     sched=sched):
                e.wait_ge(v_ms.h, NEGB_V)
                e.wait_ge(v_ms.h, VT_V_END)
                for kind, a in sched:
                    if kind == "qk":
                        ki = a
                        qb, j = klist[ki]
                        if ki >= NSL:
                            e.wait_ge(a_ms.h, ae_a0 + ki - NSL + 1)
                        e.matmul(st_slot(ki),
                                 kT2[:, hh * T + j * 128:hh * T + (j + 1) * 128],
                                 qT2[:, hh * T + qb * 512:hh * T + (qb + 1) * 512],
                                 start=True, stop=True).then_inc(t_ms.h, 1)
                    elif kind == "pvd":
                        ki = a
                        qb, j = klist[ki]
                        nkb = NKBS[qb]
                        e.wait_ge(a_ms.h, ae_a0 + ki + 1)
                        if j >= nkb - 4:
                            e.wait_ge(v_ms.h, mask_inc(ki))
                        if j == 0 and qb >= 1:
                            e.wait_ge(v_ms.h, vb(qb))
                        buf = pT[ki % NPT]
                        e.matmul(psB[:, 0:512],
                                 v2[:, hh * T + j * 128:hh * T + (j + 1) * 128],
                                 buf[:], start=(j == 0), stop=(j == nkb - 1))
                        e.matmul(psB[0:1, 512:1024], ones_bf[:, 0:1], buf[:],
                                 start=(j == 0), stop=(j == nkb - 1)).then_inc(t_ms.h, 1)
                    else:
                        qb = a
                        e.wait_ge(v_ms.h, vb(qb) + 5)
                        e.matmul(psB[:, 1536:2048], ones_f[0:1, :], rden_sb[0:1, :],
                                 start=True, stop=True).then_inc(t_ms.h, 1)
            blk.tensor(body)
            t_ms.v = qk_t0 + len(sched)

            def body(e, pair=pair, qkI=qkI, pvdI=pvdI, mask_inc=mask_inc,
                     is_diag=is_diag, st_slot=st_slot, ae_a0=ae_a0):
                for ki in range(NKI):
                    e.wait_ge(t_ms.h, qkI[ki])
                    if ki >= NPT:
                        e.wait_ge(t_ms.h, pvdI[ki - NPT])
                        if is_diag(ki - NPT):
                            e.wait_ge(v_ms.h, mask_inc(ki - NPT))
                    e.activation(pT[ki % NPT][:], st_slot(ki), AF.Exp,
                                 bias=negb[:, pair:pair + 1],
                                 scale=SM_SCALE).then_inc(a_ms.h, 1)
            blk.scalar(body)
            a_ms.v = ae_a0 + NKI

            gy_base = (gy[0].v, gy[1].v)
            VSF_ATT = v_sf.v
            def body(e, b=b, hh=hh, pair=pair, ae_a0=ae_a0, vb=vb, pvdI=pvdI, bcI=bcI,
                     gy_base=gy_base, VSF_ATT=VSF_ATT):
                ki = 0
                for qb in range(16):
                    nkb = NKBS[qb]
                    for j in range(nkb):
                        if j >= nkb - 4:
                            e.wait_ge(a_ms.h, ae_a0 + ki + 1)
                            dj = j - (nkb - 4)
                            e.tensor_tensor(pT[ki % NPT][:], pT[ki % NPT][:],
                                            mask_bf[:, dj * 512:(dj + 1) * 512],
                                            ALU.mult).then_inc(v_ms.h, 1)
                        ki += 1
                    e.wait_ge(t_ms.h, pvdI[ki - 1])
                    e.tensor_copy(dtmp[0:1, 0:512], psB[0:1, 512:1024]).then_inc(v_sf.h, 1)
                    e.wait_ge(v_sf.h, VSF_ATT + 2 * qb + 1)
                    e.reciprocal(rden_sb[0:1, :], dtmp[0:1, 0:512]).then_inc(v_ms.h, 1)
                    e.wait_ge(t_ms.h, bcI[qb])
                    e.tensor_copy(bc_sb[:], psB[:, 1536:2048]).then_inc(v_sf.h, 1)
                    e.wait_ge(gy[qb % 2].h, gy_base[qb % 2] + 16 * (qb // 2))
                    e.wait_ge(v_sf.h, VSF_ATT + 2 * qb + 2)
                    e.tensor_tensor(yn_bf[qb % 2][:], psB[:, 0:512], bc_sb[:],
                                    ALU.mult).then_inc(v_ms.h, 1)
                    k = QSH[hh].get(qb) if b == 0 else None
                    if k is not None:
                        wsrc, r0, nr, c0, wd, dst, dr, wid = WT2[k]
                        e.wait_ge(g_wl.h, GLDQ0 + (k + 1) * 16)
                        if k == 0:
                            e.wait_ge(v_ms.h, V_RQ)
                        col = rq_sb[0:nr, wid:wid + 1]
                        e.tensor_scalar(wqa[0:nr, 0:wd], wqa[0:nr, 0:wd], col, RND_C,
                                        ALU.mult, ALU.add).then_inc(vq.h, 1)
                        e.wait_ge(vq.h, 3 * k + 1)
                        e.tensor_scalar(wqa[0:nr, 0:wd], wqa[0:nr, 0:wd], -RND_C, 1.0,
                                        ALU.add, ALU.min).then_inc(vq.h, 1)
                        e.wait_ge(vq.h, 3 * k + 2)
                        if k >= 2:
                            e.wait_ge(gq[k % 2].h, GQ_BASE[k % 2] + 16 * (k // 2))
                        e.tensor_scalar(qbf[k % 2][0:nr, 0:wd], wqa[0:nr, 0:wd], -1.0, None,
                                        ALU.max).then_inc(vq.h, 1)
            blk.vector(body)
            v_ms.v = vb(16)
            v_sf.v += 32
            if b == 0:
                vq.v += 3 * len(QSH[hh])

            def body(e, b=b, hh=hh, vb=vb):
                for qb in range(16):
                    e.wait_ge(v_ms.h, vb(qb) + 6)
                    jdst = 4 * b + qb // 4
                    row = 256 * jdst + 128 * hh
                    col = (qb % 4) * 512
                    e.dma_start(out=yt_c[row:row + 128, col:col + 512],
                                in_=yn_bf[qb % 2][:]).then_inc(gy[qb % 2].h, 16)
                    gy[qb % 2].v += 16
                    rk = RSH[hh].get(qb) if b == 0 else None
                    if rk is not None:
                        wsrc, r0, nr, c0, wd, dst, dr, wid = WT2[rk]
                        if rk == 0:
                            e.wait_ge(v_ms.h, V_FOLD)
                        else:
                            e.wait_ge(vq.h, 3 * (rk - 1) + 3)
                        e.dma_start(out=wqa[0:nr, 0:wd], in_=wsrc[r0:r0 + nr, c0:c0 + wd]
                                    ).then_inc(g_wl.h, 16)
                        g_wl.v += 16
                        if rk >= 1:
                            ws2, r02, nr2, c02, wd2, dst2, dr2, wid2 = WT2[rk - 1]
                            e.dma_start(out=dst2[dr2:dr2 + nr2, c02:c02 + wd2],
                                        in_=qbf[(rk - 1) % 2][0:nr2, 0:wd2]
                                        ).then_inc(gq[(rk - 1) % 2].h, 16)
                            gq[(rk - 1) % 2].v += 16
            blk.gpsimd(body)

        if b == 0 and hh == 1:
            def body(e):
                k = NWT2 - 1
                (ws2, r02, nr2, c02, wd2, dst2, dr2, wid2) = WT2[k]
                e.wait_ge(vq.h, 3 * k + 3)
                e.dma_start(out=dst2[dr2:dr2 + nr2, c02:c02 + wd2],
                            in_=qbf[k % 2][0:nr2, 0:wd2]).then_inc(gq[k % 2].h, 16)
                gq[k % 2].v += 16
                e.wait_ge(gq[0].h, gq[0].v)
                e.wait_ge(gq[1].h, gq[1].v)
                e.collective_compute("AllGather", ALU.bypass, replica_groups=rg,
                                     ins=[w1qc.ap().opt()], outs=[w1q_all.ap().opt()]).then_inc(cc3a.h)
                e.collective_compute("AllGather", ALU.bypass, replica_groups=rg,
                                     ins=[w2qc.ap().opt()], outs=[w2q_all.ap().opt()]).then_inc(cc3b.h)
                e.collective_compute("AllGather", ALU.bypass, replica_groups=rg,
                                     ins=[w3qc.ap().opt()], outs=[w3q_all.ap().opt()]).then_inc(cc3c.h)
            blk.gpsimd(body)
            cc3a.v = cc3b.v = cc3c.v = 1

    def body(e):
        e.wait_ge(gy[0].h, gy[0].v)
        e.wait_ge(gy[1].h, gy[1].v)
        with allow_ici_all_to_all():
            e.collective_compute("AllToAll", ALU.bypass, replica_groups=rg,
                                 ins=[yt_c.ap().opt()], outs=[yt_my.ap().opt()]).then_inc(cc2.h)
    blk.gpsimd(body)
    cc2.v = 1

    barrier()
    p1.close()
    pW.close()

    p2 = contextlib.ExitStack()
    def sb2(name, shape, dt):
        return p2.enter_context(nc.sbuf_tensor(name, shape, dt))
    NHT = HID // 128
    NHQ = NHT // 4
    hidT = sb2("hidT", [128, NHT * WIN], BF16)
    xnT2 = sb2("xnT2", [128, 16 * WIN], BF16)
    wp1 = [sb2("wp1a", [128, 16 * 512], BF16), sb2("wp1b", [128, 16 * 512], BF16)]
    wp2 = [sb2("wp2a", [128, 16 * 512], BF16), sb2("wp2b", [128, 16 * 512], BF16)]
    w3t = [sb2(f"w3t{i}", [128, 512], BF16) for i in range(4)]
    sgm = sb2("sgm", [128, WIN], F32)
    gbf = [sb2(f"gbf{i}", [128, 512], BF16) for i in range(2)]

    W3_T_PREV = [0]
    X1_PREV = [0]
    for half in range(2):
        STP_H = s_tp.v
        X1P_H = X1_PREV[0]
        X1T0 = t_ms.v
        HIDV0 = v_ms.v
        SGA0 = a_ms.v

        SWP_BASE = (swp[0].v, swp[1].v)
        SWQ_BASE = (swq[0].v, swq[1].v)
        def body(e, half=half, X1T0=X1T0, X1P=X1P_H):
            e.wait_ge(cc3a.h, 1)
            e.wait_ge(cc3b.h, 1)
            for hq in range(NHQ):
                if hq >= 2:
                    e.wait_ge(t_ms.h, X1T0 + (hq - 1) * 4)
                for cchunk in range(16):
                    e.dma_start(out=wp1[hq % 2][:, cchunk * 512:(cchunk + 1) * 512],
                                in_=w1q_all[hq * 512:(hq + 1) * 512,
                                            cchunk * 128:(cchunk + 1) * 128],
                                transpose=True).then_inc(swp[hq % 2].h, 16)
                    e.dma_start(out=wp2[hq % 2][:, cchunk * 512:(cchunk + 1) * 512],
                                in_=w2q_all[hq * 512:(hq + 1) * 512,
                                            cchunk * 128:(cchunk + 1) * 128],
                                transpose=True).then_inc(swp[hq % 2].h, 16)
                swp[hq % 2].v += 512
        blk.sync(body)

        def body(e, half=half, X1P=X1P_H):
            if half == 1:
                e.wait_ge(t_ms.h, X1P)
            for cchunk in range(16):
                for sub in range(2):
                    src = xn_cq[2 * half + sub][cchunk * 128:(cchunk + 1) * 128, :]
                    e.dma_start(out=xnT2[:, cchunk * WIN + sub * 512:
                                         cchunk * WIN + (sub + 1) * 512], in_=src
                                ).then_inc(swq[0].h, 16)
        blk.scalar(body)
        swq[0].v += 512

        def body(e, half=half, STP_H=STP_H, SWP_BASE=SWP_BASE, HIDV0=HIDV0,
                 W3P=W3_T_PREV[0]):
            e.wait_ge(swq[0].h, SWQ_BASE[0] + 512)
            if half == 1:
                e.wait_ge(v_ms.h, HIDV0)
            for hq in range(NHQ):
                e.wait_ge(swp[hq % 2].h, SWP_BASE[hq % 2] + 512 * (hq // 2 + 1))
                for ht in range(4):
                    hti = hq * 4 + ht
                    pspar = psA if hti % 2 == 0 else psB
                    x1ps = pspar[:, 0:1024]
                    x2ps = pspar[:, 1024:2048]
                    if hti >= 2:
                        e.wait_ge(v_ms.h, HIDV0 + hti - 1)
                    last = None
                    for cchunk in range(16):
                        lhs1 = wp1[hq % 2][:, cchunk * 512 + ht * 128: cchunk * 512 + ht * 128 + 128]
                        lhs2 = wp2[hq % 2][:, cchunk * 512 + ht * 128: cchunk * 512 + ht * 128 + 128]
                        for nn in range(2):
                            e.matmul(x1ps[:, nn * 512:(nn + 1) * 512], lhs1,
                                     xnT2[:, cchunk * WIN + nn * 512:cchunk * WIN + (nn + 1) * 512],
                                     start=(cchunk == 0), stop=(cchunk == 15))
                        for nn in range(2):
                            last = e.matmul(x2ps[:, nn * 512:(nn + 1) * 512], lhs2,
                                     xnT2[:, cchunk * WIN + nn * 512:cchunk * WIN + (nn + 1) * 512],
                                     start=(cchunk == 0), stop=(cchunk == 15))
                    last.then_inc(t_ms.h, 1)
        blk.tensor(body)
        t_ms.v += NHT
        X1_PREV[0] = t_ms.v

        def wp2load(e, hq):
            for cchunk in range(16):
                e.dma_start(out=wp2[hq % 2][:, cchunk * 512:(cchunk + 1) * 512],
                            in_=w2q_all[hq * 512:(hq + 1) * 512,
                                        cchunk * 128:(cchunk + 1) * 128],
                            transpose=True).then_inc(swq[hq % 2].h, 16)
            swq[hq % 2].v += 256

        def body(e, half=half, X1T0=X1T0, HIDV0=HIDV0, wp2load=wp2load,
                 X1P=X1P_H):
            for hti in range(NHT):
                pspar = psA if hti % 2 == 0 else psB
                e.wait_ge(t_ms.h, X1T0 + hti + 1)
                if hti >= 1:
                    e.wait_ge(v_ms.h, HIDV0 + hti)
                e.activation(sgm[:], pspar[:, 0:1024], AF.Sigmoid).then_inc(a_ms.h, 1)
        blk.scalar(body)
        a_ms.v += NHT

        def body(e, SGA0=SGA0, HIDV0=HIDV0):
            for hti in range(NHT):
                pspar = psA if hti % 2 == 0 else psB
                e.wait_ge(a_ms.h, SGA0 + hti + 1)
                e.tensor_tensor(hidT[:, hti * WIN:(hti + 1) * WIN], sgm[:],
                                pspar[:, 1024:2048], ALU.mult).then_inc(v_ms.h, 1)
        blk.vector(body)
        v_ms.v += NHT
        HID_END_V = v_ms.v

        W3T0 = t_ms.v
        W3A0 = a_ms.v
        W3V0 = v_ms.v

        SW3_BASE = tuple(s.v for s in sw3)
        def body(e, W3T0=W3T0, HID_END_V=HID_END_V):
            e.wait_ge(v_ms.h, HID_END_V)
            e.wait_ge(cc3c.h, 1)
            for cq in range(4):
                for hc in range(NHT):
                    idx = cq * NHT + hc
                    if idx >= 4:
                        e.wait_ge(t_ms.h, W3T0 + idx - 3)
                    slot = idx % 4
                    e.dma_start(out=w3t[slot][:],
                                in_=w3q_all[cq * 512:(cq + 1) * 512,
                                            hc * 128:(hc + 1) * 128],
                                transpose=True).then_inc(sw3[slot].h, 16)
                    sw3[slot].v += 16
        blk.sync(body)

        def body(e, W3T0=W3T0, W3V0=W3V0, SW3_BASE=SW3_BASE):
            for cq in range(4):
                for hc in range(NHT):
                    idx = cq * NHT + hc
                    slot = idx % 4
                    e.wait_ge(sw3[slot].h, SW3_BASE[slot] + 16 * (idx // 4 + 1))
                    last = None
                    for tt in range(8):
                        if hc == 0 and cq > 0:
                            e.wait_ge(v_ms.h, W3V0 + (cq - 1) * 8 + tt + 1)
                        ps = (psA if tt < 4 else psB)[:, (tt % 4) * 512:(tt % 4 + 1) * 512]
                        last = e.matmul(ps, hidT[:, hc * WIN + tt * 128: hc * WIN + (tt + 1) * 128],
                                 w3t[slot][:, :], start=(hc == 0), stop=(hc == NHT - 1))
                    last.then_inc(t_ms.h, 1)
        blk.tensor(body)
        t_ms.v += 4 * NHT
        W3_T_PREV[0] = t_ms.v

        def body(e, W3T0=W3T0, W3V0=W3V0):
            for cq in range(4):
                e.wait_ge(t_ms.h, W3T0 + (cq + 1) * NHT)
                for tt in range(8):
                    idx = cq * 8 + tt
                    if idx >= 1:
                        e.wait_ge(v_ms.h, W3V0 + idx)
                    ps = (psA if tt < 4 else psB)[:, (tt % 4) * 512:(tt % 4 + 1) * 512]
                    e.activation(sgm[:, 0:512], ps, AF.Sigmoid).then_inc(a_ms.h, 1)
        blk.scalar(body)
        a_ms.v += 32

        GG_BASE = (gg[0].v, gg[1].v)
        def body(e, W3A0=W3A0, W3V0=W3V0, GG_BASE=GG_BASE):
            for cq in range(4):
                for tt in range(8):
                    idx = cq * 8 + tt
                    ps = (psA if tt < 4 else psB)[:, (tt % 4) * 512:(tt % 4 + 1) * 512]
                    e.wait_ge(a_ms.h, W3A0 + idx + 1)
                    e.wait_ge(gg[idx % 2].h, GG_BASE[idx % 2] + 16 * (idx // 2))
                    e.tensor_tensor(gbf[idx % 2][:], sgm[:, 0:512], ps,
                                    ALU.mult).then_inc(v_ms.h, 1)
        blk.vector(body)
        v_ms.v += 32

        def body(e, half=half, W3V0=W3V0):
            for cq in range(4):
                for tt in range(8):
                    idx = cq * 8 + tt
                    e.wait_ge(v_ms.h, W3V0 + idx + 1)
                    e.dma_start(out=g_sc[half * WIN + tt * 128: half * WIN + (tt + 1) * 128,
                                         cq * 512:(cq + 1) * 512],
                                in_=gbf[idx % 2][:]).then_inc(gg[idx % 2].h, 16)
                    gg[idx % 2].v += 16
        blk.gpsimd(body)

    barrier()
    p2.close()

    p3 = contextlib.ExitStack()
    def sb3(name, shape, dt):
        return p3.enter_context(nc.sbuf_tensor(name, shape, dt))
    yT_sb = sb3("yT_sb", [128, 16 * 2048], BF16)
    wo_sb = sb3("wo_sb", [128, 16 * 2048], BF16)
    bo_row = sb3("bo_row", [1, 2048], F32)
    bobc = sb3("bobc", [128, 2048], F32)
    xt3 = [sb3("xt3a", [128, 2048], F32), sb3("xt3b", [128, 2048], F32)]
    gt3 = [sb3("gt3a", [128, 2048], BF16), sb3("gt3b", [128, 2048], BF16)]
    s_bf = [sb3("sbfa", [128, 2048], BF16), sb3("sbfb", [128, 2048], BF16)]
    sg_f = sb3("sg_f", [128, 2048], F32)

    P3G0 = g_ld.v
    P3S0 = s_ld.v
    P3T0 = t_ms.v
    P3V0 = v_ms.v
    P3GS0 = g_st.v

    def body(e):
        e.wait_ge(g_st.h, WO_CAST_G)
        for q in range(2):
            e.dma_start(out=wo_sb[:, q * 8192:(q + 1) * 8192],
                        in_=wo_bf[q * 512:(q + 1) * 512, :]
                        .rearrange("(a p) n -> p a n", p=128)).then_inc(g_yw.h, 16)
    blk.gpsimd(body)
    g_yw.v += 2 * 16
    YWO_G = g_yw.v

    SWQ3_BASE = swq[1].v
    def body(e):
        e.wait_ge(g_st.h, WO_CAST_G)
        for q in range(2, 4):
            e.dma_start(out=wo_sb[:, q * 8192:(q + 1) * 8192],
                        in_=wo_bf[q * 512:(q + 1) * 512, :]
                        .rearrange("(a p) n -> p a n", p=128)).then_inc(swq[1].h, 16)
        for tt in range(16):
            if tt >= 2:
                e.wait_ge(v_ms.h, P3V0 + 1 + 3 * (tt - 2) + 3)
            e.dma_start(out=gt3[tt % 2][:], in_=g_sc[tt * 128:(tt + 1) * 128, :]
                        ).then_inc(gt3s[tt % 2].h, 16)
            gt3s[tt % 2].v += 16
    blk.scalar(body)
    swq[1].v += 2 * 16

    def body(e):
        e.dma_start(out=bo_row[:], in_=bo_in[:, :]).then_inc(s_ld.h, 16)
        s_ld.v += 16
        e.wait_ge(cc2.h, 1)
        for q in range(4):
            e.dma_start(out=yT_sb[:, q * 8192:(q + 1) * 8192],
                        in_=yt_my[q * 512:(q + 1) * 512, :]
                        .rearrange("(a p) t -> p a t", p=128)).then_inc(s_ld.h, 16)
        s_ld.v += 4 * 16
        for tt in range(16):
            if tt >= 2:
                e.wait_ge(go[tt % 2].h, 16 * (tt // 2))
            e.dma_start(out=xt3[tt % 2][:], in_=x_my[tt * 128:(tt + 1) * 128, :]
                        ).then_inc(sx3[tt % 2].h, 16)
            sx3[tt % 2].v += 16
    blk.sync(body)

    def body(e):
        e.wait_ge(s_ld.h, P3S0 + 16)
        last = None
        for nn in range(4):
            last = e.matmul(psB[:, nn * 512:(nn + 1) * 512], ones_f[0:1, :],
                     bo_row[0:1, nn * 512:(nn + 1) * 512], start=True, stop=True)
        last.then_inc(t_ms.h, 1)
        e.wait_ge(g_yw.h, YWO_G)
        e.wait_ge(swq[1].h, SWQ3_BASE + 2 * 16)
        e.wait_ge(s_ld.h, P3S0 + 5 * 16)
        for tt in range(16):
            pspar = psA if tt % 2 == 0 else psB
            if tt == 1:
                e.wait_ge(v_ms.h, P3V0 + 1)
            if tt >= 2:
                e.wait_ge(v_ms.h, P3V0 + 1 + 3 * (tt - 2) + 1)
            last = None
            for cchunk in range(16):
                lhsT = yT_sb[:, cchunk * 2048 + tt * 128: cchunk * 2048 + tt * 128 + 128]
                for nn in range(4):
                    last = e.matmul(pspar[:, nn * 512:(nn + 1) * 512], lhsT,
                             wo_sb[:, cchunk * 2048 + nn * 512: cchunk * 2048 + (nn + 1) * 512],
                             start=(cchunk == 0), stop=(cchunk == 15))
            last.then_inc(t_ms.h, 1)
    blk.tensor(body)
    t_ms.v += 17

    def body(e):
        e.wait_ge(t_ms.h, P3T0 + 1)
        e.tensor_copy(bobc[:], psB[:, 0:2048]).then_inc(v_ms.h, 1)
        e.wait_ge(v_ms.h, P3V0 + 1)
        sfb3 = v_sf.v
        for tt in range(16):
            pspar = psA if tt % 2 == 0 else psB
            e.wait_ge(t_ms.h, P3T0 + tt + 2)
            e.tensor_tensor(s_bf[tt % 2][:], pspar[:, 0:2048], bobc[:],
                            ALU.add).then_inc(v_ms.h, 1)
            e.wait_ge(gt3s[tt % 2].h, 16 * (tt // 2 + 1))
            e.wait_ge(v_ms.h, P3V0 + 1 + 3 * tt + 1)
            e.tensor_tensor(sg_f[:], s_bf[tt % 2][:], gt3[tt % 2][:],
                            ALU.mult).then_inc(v_sf.h, 1)
            e.wait_ge(sx3[tt % 2].h, 16 * (tt // 2 + 1))
            e.wait_ge(v_sf.h, sfb3 + tt + 1)
            e.tensor_tensor(xt3[tt % 2][:], sg_f[:], xt3[tt % 2][:],
                            ALU.add).then_inc(v_ms.h, 2)
    blk.vector(body)
    v_ms.v += 1 + 3 * 16
    v_sf.v += 16

    def body(e):
        for tt in range(16):
            e.wait_ge(v_ms.h, P3V0 + 1 + 3 * tt + 3)
            e.dma_start(out=out_my[tt * 128:(tt + 1) * 128, :], in_=xt3[tt % 2][:]
                        ).then_inc(go[tt % 2].h, 16)
            go[tt % 2].v += 16
        e.wait_ge(go[0].h, go[0].v)
        e.wait_ge(go[1].h, go[1].v)
    blk.gpsimd(body)

    if debug_taps:
        tap_list = [("t_xn", xn_cq[0], [C, 512], BF16), ("t_gam", gam_s, [1, 4], F32),
                    ("t_w1q", w1qc, [RH, C], BF16), ("t_w3q", w3qc, [256, HID], BF16),
                    ("t_ytc", yt_c, [2048, 2048], BF16), ("t_ytm", yt_my, [2048, 2048], BF16),
                    ("t_g", g_sc, [MYT, C], BF16), ("t_xna", xn_aq[0], [NCORES * C, 512], BF16)]
        tap_out = {nm: nc.dram_tensor(nm, shp, dtp, kind="ExternalOutput")
                   for nm, _, shp, dtp in tap_list}
        def body(e):
            e.wait_ge(go[0].h, go[0].v)
            e.wait_ge(go[1].h, go[1].v)
            for nm, t, shp, dtp in tap_list:
                nrow = shp[0]
                step = max(1, 8192 * 2048 // max(1, int(np.prod(shp)))) if False else 512
                r = 0
                while r < nrow:
                    rr = min(step, nrow - r)
                    e.dma_start(out=tap_out[nm][r:r + rr, :], in_=t[r:r + rr, :]
                                ).then_inc(g_st.h, 16)
                    g_st.v += 16
                    r += rr
            e.wait_ge(g_st.h, g_st.v)
        blk.gpsimd(body)

    p3.close()
    ctx.close()
    return nc


def _make_mask():
    s = np.arange(128)[:, None]
    t = np.arange(512)[None, :]
    m = np.zeros((128, 2048), dtype=np.float32)
    for jj in range(4):
        m[:, jj * 512:(jj + 1) * 512] = (jj * 128 + s <= t).astype(np.float32)
    return m


def kernel(L0_attn_x, L0_attn_Wq, L0_attn_bq, L0_attn_Wk, L0_attn_bk, L0_attn_Wv, L0_attn_bv,
           L0_attn_Wo, L0_attn_bo, L0_ffn_w1, L0_ffn_w2, L0_ffn_w3):
    x = np.ascontiguousarray(np.asarray(L0_attn_x, dtype=np.float32))
    Wq = np.asarray(L0_attn_Wq, dtype=np.float32)
    Wk = np.asarray(L0_attn_Wk, dtype=np.float32)
    Wv = np.asarray(L0_attn_Wv, dtype=np.float32)
    bq = np.asarray(L0_attn_bq, dtype=np.float32)
    bk = np.asarray(L0_attn_bk, dtype=np.float32)
    bv = np.asarray(L0_attn_bv, dtype=np.float32)
    Wo = np.ascontiguousarray(np.asarray(L0_attn_Wo, dtype=np.float32))
    bo = np.asarray(L0_attn_bo, dtype=np.float32).reshape(1, C)
    w1 = np.asarray(L0_ffn_w1, dtype=np.float32)
    w2 = np.asarray(L0_ffn_w2, dtype=np.float32)
    w3 = np.asarray(L0_ffn_w3, dtype=np.float32)

    ident = np.eye(128, dtype=np.float32)
    ones = np.ones((128, 128), dtype=np.float32)
    mask = _make_mask()

    nc = build_nc()
    in_maps = []
    for c in range(NCORES):
        b = c // 4
        t0 = (c % 4) * MYT
        in_maps.append(dict(
            x_my=np.ascontiguousarray(x[b, t0:t0 + MYT, :]),
            wq_my=np.ascontiguousarray(Wq[2 * c:2 * c + 2]),
            wk_my=np.ascontiguousarray(Wk[2 * c:2 * c + 2]),
            wv_my=np.ascontiguousarray(Wv[2 * c:2 * c + 2]),
            bq_my=np.ascontiguousarray(bq[2 * c:2 * c + 2].T),
            bk_my=np.ascontiguousarray(bk[2 * c:2 * c + 2].T),
            bv_my=np.ascontiguousarray(bv[2 * c:2 * c + 2].T),
            wo_in=Wo,
            bo_in=bo,
            w1_my=np.ascontiguousarray(w1[RH * c:RH * (c + 1)]),
            w2_my=np.ascontiguousarray(w2[RH * c:RH * (c + 1)]),
            w3_my=np.ascontiguousarray(w3[256 * c:256 * (c + 1)]),
            ident_in=ident,
            ones_in=ones,
            mask_in=mask,
        ))
    res = run_bass_kernel_spmd(nc, in_maps, core_ids=list(range(NCORES)))
    out = np.empty((B, T, C), dtype=np.float32)
    for c in range(NCORES):
        b = c // 4
        t0 = (c % 4) * MYT
        out[b, t0:t0 + MYT, :] = np.asarray(res.results[c]["out_my"], dtype=np.float32)
    return out


# version 19
# speedup vs baseline: 1.0752x; 1.0752x over previous
import contextlib
import numpy as np
import ml_dtypes
import concourse.bass as bass
import concourse.mybir as mybir
from concourse.bass_utils import run_bass_kernel_spmd
from concourse.aot_env import allow_ici_all_to_all

F32 = mybir.dt.float32
BF16 = mybir.dt.bfloat16
AF = mybir.ActivationFunctionType
ALU = mybir.AluOpType
AX = mybir.AxisListType

NCORES = 8
B, T, C = 2, 8192, 2048
H, DH = 16, 128
HID, RH, PH = 5632, 704, 768
HPAD = NCORES * PH
MYT = 2048
NT = MYT // 128
SM_SCALE = float(DH) ** -0.5
RND_C = 1.5 * 2.0 ** 23
EPS_LN = 1e-5
EPS_Q = 1e-8
WCNT = float(HID * C)
WIN = 1024
NW = T // WIN
NTILE_W12 = 6


class Ctr:
    __slots__ = ("h", "v")
    def __init__(self, h):
        self.h = h
        self.v = 0

    def inc(self, ins, k=1):
        ins.then_inc(self.h, k)
        self.v += k
        return self.v


def w12_rows(j):
    r0 = j * 128
    return r0, min(RH, r0 + 128) - r0


def build_nc(debug_taps=False):
    nc = bass.Bass("TRN2", target_bir_lowering=False, detect_race_conditions=True)

    x_my = nc.dram_tensor("x_my", [MYT, C], F32, kind="ExternalInput")
    wq_my = nc.dram_tensor("wq_my", [2, C, DH], F32, kind="ExternalInput")
    wk_my = nc.dram_tensor("wk_my", [2, C, DH], F32, kind="ExternalInput")
    wv_my = nc.dram_tensor("wv_my", [2, C, DH], F32, kind="ExternalInput")
    bq_my = nc.dram_tensor("bq_my", [DH, 2], F32, kind="ExternalInput")
    bk_my = nc.dram_tensor("bk_my", [DH, 2], F32, kind="ExternalInput")
    bv_my = nc.dram_tensor("bv_my", [DH, 2], F32, kind="ExternalInput")
    wo_in = nc.dram_tensor("wo_in", [C, C], F32, kind="ExternalInput")
    bo_in = nc.dram_tensor("bo_in", [1, C], F32, kind="ExternalInput")
    w1_my = nc.dram_tensor("w1_my", [RH, C], F32, kind="ExternalInput")
    w2_my = nc.dram_tensor("w2_my", [RH, C], F32, kind="ExternalInput")
    w3_my = nc.dram_tensor("w3_my", [256, HID], F32, kind="ExternalInput")
    ident_in = nc.dram_tensor("ident_in", [128, 128], F32, kind="ExternalInput")
    ones_in = nc.dram_tensor("ones_in", [128, 128], F32, kind="ExternalInput")
    mask_in = nc.dram_tensor("mask_in", [128, 2048], F32, kind="ExternalInput")
    out_my = nc.dram_tensor("out_my", [MYT, C], F32, kind="ExternalOutput")

    xn_cq = [nc.dram_tensor(f"xn_cq{q}", [C, 512], BF16, kind="Internal") for q in range(4)]
    xn_aq = [nc.dram_tensor(f"xn_aq{q}", [NCORES * C, 512], BF16, kind="Internal") for q in range(4)]
    gam_c = nc.dram_tensor("gam_c", [1, 4], F32, kind="Internal")
    gam_s = nc.dram_tensor("gam_s", [1, 4], F32, kind="Internal")
    w1qc = nc.dram_tensor("w1qc", [RH, C], BF16, kind="Internal")
    w2qc = nc.dram_tensor("w2qc", [RH, C], BF16, kind="Internal")
    w3qc = nc.dram_tensor("w3qc", [256, HID], BF16, kind="Internal")
    w1q_all = nc.dram_tensor("w1q_all", [NCORES * RH, C], BF16, kind="Internal")
    w2q_all = nc.dram_tensor("w2q_all", [NCORES * RH, C], BF16, kind="Internal")
    w3q_all = nc.dram_tensor("w3q_all", [NCORES * 256, HID], BF16, kind="Internal")
    wo_bf = nc.dram_tensor("wo_bf", [C, C], BF16, kind="Internal")
    yt_c = nc.dram_tensor("yt_c", [2048, 2048], BF16, kind="Internal")
    yt_my = nc.dram_tensor("yt_my", [2048, 2048], BF16, kind="Internal")
    g_sc = nc.dram_tensor("g_sc", [MYT, C], BF16, kind="Internal")

    rg = [list(range(NCORES))]
    ctx = contextlib.ExitStack()

    def sem(name):
        return Ctr(ctx.enter_context(nc.semaphore(name)))

    def sb(name, shape, dt):
        return ctx.enter_context(nc.sbuf_tensor(name, shape, dt))

    ident_bf = sb("ident_bf", [128, 128], BF16)
    ident_f = sb("ident_f", [128, 128], F32)
    ones_f = sb("ones_f", [128, 128], F32)
    ones_bf = sb("ones_bf", [128, 1], BF16)
    stat = sb("stat", [128, 64], F32)
    grow = sb("grow", [1, 16], F32)
    rq_sb = sb("rq_sb", [128, 4], F32)
    negb = sb("negb", [128, 4], F32)
    bmax = sb("bmax", [1, 64], F32)
    biasT = sb("biasT", [128, 6], F32)

    psA = ctx.enter_context(nc.psum_tensor("psA", [128, 2048], F32))
    psB = ctx.enter_context(nc.psum_tensor("psB", [128, 2048], F32))

    s_ld = sem("s_ld")
    g_ld = sem("g_ld")
    s_tp = sem("s_tp")
    v_ms = sem("v_ms")
    a_ms = sem("a_ms")
    t_ms = sem("t_ms")
    g_st = sem("g_st")
    cc0 = sem("cc0"); cc2 = sem("cc2")
    ccq = [sem(f"ccq{q}") for q in range(4)]
    cc3a = sem("cc3a"); cc3b = sem("cc3b"); cc3c = sem("cc3c")
    g_c = sem("g_c")
    v_sf = sem("v_sf")
    a_sf = sem("a_sf")
    sx = [sem("sx0"), sem("sx1")]
    gxn = [sem("gxn0"), sem("gxn1")]
    gy = [sem("gy0"), sem("gy1")]
    gg = [sem("gg0"), sem("gg1")]
    sw3 = [sem(f"sw3{i}") for i in range(4)]
    g_yw = sem("g_yw")
    swp = [sem("swp0"), sem("swp1")]
    stw = [sem("stw0"), sem("stw1")]
    gq = [sem("gq0"), sem("gq1")]
    vq = sem("vq")
    vd = sem("vd")
    g_wl = sem("g_wl")
    swq = [sem("swq0"), sem("swq1")]
    gt3s = [sem("gt3s0"), sem("gt3s1")]
    sx3 = [sem("sx3a"), sem("sx3b")]
    go = [sem("go0"), sem("go1")]
    bar = sem("bar")

    blk = ctx.enter_context(nc.Block())
    barn = [0]

    def barrier():
        barn[0] += 1
        target = 5 * barn[0]
        caps = dict(sync=(s_ld, s_tp, sx[0], sx[1], sw3[0], sw3[1], sw3[2], sw3[3], sx3[0], sx3[1], swp[0], swp[1], stw[0], stw[1]),
                    gpsimd=(g_ld, g_st, g_c, gxn[0], gxn[1], gy[0], gy[1], gg[0], gg[1],
                            g_yw, gt3s[0], gt3s[1], go[0], go[1], gq[0], gq[1], g_wl),
                    tensor=(t_ms,), scalar=(a_ms, a_sf, swq[0], swq[1]), vector=(v_ms, v_sf, vq, vd))
        for name, dec in (("sync", blk.sync), ("gpsimd", blk.gpsimd), ("tensor", blk.tensor),
                          ("scalar", blk.scalar), ("vector", blk.vector)):
            def body(e, name=name):
                for s in caps[name]:
                    e.wait_ge(s.h, s.v)
                e.sem_inc(bar.h, 1)
                e.wait_ge(bar.h, target)
            dec(body)
        bar.v = target

    WQW = 2816
    WT2 = []
    for j in range(NTILE_W12):
        r0, nr = w12_rows(j)
        WT2.append((w1_my, r0, nr, 0, C, w1qc, r0, 0))
    for j in range(NTILE_W12):
        r0, nr = w12_rows(j)
        WT2.append((w2_my, r0, nr, 0, C, w2qc, r0, 1))
    for j in range(2):
        for hf in range(2):
            WT2.append((w3_my, j * 128, 128, hf * WQW, HID - hf * WQW if hf else WQW,
                        w3qc, j * 128, 2))
    NWT2 = len(WT2)

    pW = contextlib.ExitStack()
    def sbW(name, shape, dt):
        return pW.enter_context(nc.sbuf_tensor(name, shape, dt))
    wqa = sbW("wqa", [128, WQW], F32)
    qbf = [sbW("qbf0", [128, WQW], BF16), sbW("qbf1", [128, WQW], BF16)]
    wqkv = sbW("wqkv", [128, 6 * C], BF16)
    mask_bf = sbW("mask_bf", [128, 2048], BF16)

    pLN = contextlib.ExitStack()
    def sbL(name, shape, dt):
        return pLN.enter_context(nc.sbuf_tensor(name, shape, dt))
    xt = [sbL("xt0", [128, C], F32), sbL("xt1", [128, C], F32)]
    xsq = sbL("xsq", [128, C], F32)
    xnf = [sbL("xnf0", [128, C], F32), sbL("xnf1", [128, C], F32)]
    xn_st = [sbL("xnst0", [128, 16 * 512], BF16), sbL("xnst1", [128, 16 * 512], BF16)]

    def body(e):
        e.dma_start(out=ident_bf[:], in_=ident_in[:, :]).then_inc(g_c.h, 16)
        e.dma_start(out=ones_f[:], in_=ones_in[:, :]).then_inc(g_c.h, 16)
        e.dma_start(out=ones_bf[:], in_=ones_in[:, 0:1], allow_slow_non_contiguous=True).then_inc(g_c.h, 16)
        e.dma_start(out=biasT[:, 0:2], in_=bq_my[:, :]).then_inc(g_c.h, 16)
        e.dma_start(out=biasT[:, 2:4], in_=bk_my[:, :]).then_inc(g_c.h, 16)
        e.dma_start(out=biasT[:, 4:6], in_=bv_my[:, :]).then_inc(g_c.h, 16)
        e.dma_start(out=ident_f[:], in_=ident_in[:, :]).then_inc(g_c.h, 16)
        e.dma_start(out=mask_bf[:], in_=mask_in[:, :]).then_inc(g_ld.h, 16)
        g_ld.v += 16
        for idx, wsrc in enumerate((wq_my, wk_my, wv_my)):
            for hh in range(2):
                base = (idx * 2 + hh) * C
                for q in range(4):
                    e.dma_start(out=wqkv[:, base + q * 512:base + (q + 1) * 512],
                                in_=wsrc[hh, q * 512:(q + 1) * 512, :]
                                .rearrange("(a p) d -> p a d", p=128)
                                ).then_inc(g_ld.h, 16)
                    g_ld.v += 16
        wsrc0, r0, nr, c0, wd, _, _, _ = WT2[0]
        e.dma_start(out=wqa[0:nr, 0:wd], in_=wsrc0[r0:r0 + nr, c0:c0 + wd]
                    ).then_inc(g_wl.h, 16)
        g_wl.v += 16
    blk.gpsimd(body)
    g_c.v = 7 * 16
    GC_ALL = g_c.v
    GLD_QKV = g_ld.v

    VLN0 = v_ms.v
    ALN0 = a_ms.v
    TLN0 = t_ms.v
    GXN_B = (gxn[0].v, gxn[1].v)

    def body(e):
        for i in range(NT):
            if i >= 2:
                e.wait_ge(v_ms.h, VLN0 + 6 * (i - 2) + 4)
            e.dma_start(out=xt[i % 2][:], in_=x_my[i * 128:(i + 1) * 128, :]
                        ).then_inc(sx[i % 2].h, 16)
            sx[i % 2].v += 16
    blk.sync(body)

    def body(e):
        for i in range(NT):
            e.wait_ge(sx[i % 2].h, 16 * (i // 2 + 1))
            if i >= 1:
                e.wait_ge(v_ms.h, VLN0 + 6 * (i - 1) + 4)
            e.activation(xsq[:], xt[i % 2][:], AF.Square).then_inc(a_ms.h, 1)
            e.wait_ge(v_ms.h, VLN0 + 6 * i + 3)
            e.activation(stat[:, 56:57], stat[:, 16 + i:17 + i], AF.Ln).then_inc(a_ms.h, 1)
            e.wait_ge(a_ms.h, ALN0 + 5 * i + 2)
            e.activation(stat[:, 52 + i % 2:53 + i % 2], stat[:, 56:57], AF.Exp,
                         scale=-0.5).then_inc(a_ms.h, 1)
            di = i // 4
            psX = psA if i % 2 == 0 else psB
            for q in (0, 1):
                e.wait_ge(t_ms.h, TLN0 + 16 * i + 4 * (q + 1))
                if di >= 2 and i % 4 == 0 and q == 0:
                    e.wait_ge(gxn[di % 2].h, GXN_B[di % 2] + 64 * (di // 2))
                e.activation(xn_st[di % 2][:, (i % 4) * 2048 + q * 512:
                                           (i % 4) * 2048 + (q + 1) * 512],
                             psX[:, q * 512:(q + 1) * 512], AF.Identity,
                             bias=stat[:, 60:61]).then_inc(a_ms.h, 1)
    blk.scalar(body)
    a_ms.v += 5 * NT

    def body(e):
        sfb = v_sf.v
        e.memset(stat[:, 60:61], 0.0).then_inc(v_sf.h, 1)
        sfb += 1
        for i in range(NT):
            e.wait_ge(sx[i % 2].h, 16 * (i // 2 + 1))
            e.reduce_sum(stat[:, i:i + 1], xt[i % 2][:], axis=AX.X).then_inc(v_ms.h, 1)
            e.wait_ge(a_ms.h, ALN0 + 5 * i + 1)
            e.reduce_sum(stat[:, 16 + i:17 + i], xsq[:], axis=AX.X).then_inc(v_ms.h, 1)
            e.wait_ge(v_ms.h, VLN0 + 6 * i + 2)
            e.tensor_scalar(stat[:, i:i + 1], stat[:, i:i + 1], -1.0 / C, None,
                            ALU.mult).then_inc(v_sf.h, 1)
            e.wait_ge(v_sf.h, sfb + 4 * i + 1)
            e.tensor_tensor(stat[:, 54:55], stat[:, i:i + 1], stat[:, i:i + 1],
                            ALU.mult).then_inc(v_sf.h, 1)
            e.tensor_scalar(stat[:, 16 + i:17 + i], stat[:, 16 + i:17 + i], 1.0 / C, EPS_LN,
                            ALU.mult, ALU.add).then_inc(v_sf.h, 1)
            e.wait_ge(v_sf.h, sfb + 4 * i + 3)
            e.tensor_tensor(stat[:, 16 + i:17 + i], stat[:, 16 + i:17 + i], stat[:, 54:55],
                            ALU.subtract).then_inc(v_ms.h, 1)
            e.wait_ge(a_ms.h, ALN0 + 5 * i + 3)
            e.tensor_scalar(xsq[:], xt[i % 2][:], stat[:, i:i + 1], None,
                            ALU.add).then_inc(v_sf.h, 1)
            e.wait_ge(v_sf.h, sfb + 4 * i + 4)
            if i >= 2:
                e.wait_ge(t_ms.h, TLN0 + 16 * (i - 1))
            e.tensor_scalar(xnf[i % 2][:], xsq[:], stat[:, 52 + i % 2:53 + i % 2], None,
                            ALU.mult).then_inc(v_ms.h, 1)
            di = i // 4
            psX = psA if i % 2 == 0 else psB
            for q in (2, 3):
                e.wait_ge(t_ms.h, TLN0 + 16 * i + 4 * (q + 1))
                if di >= 2 and i % 4 == 0 and q == 2:
                    e.wait_ge(gxn[di % 2].h, GXN_B[di % 2] + 64 * (di // 2))
                e.tensor_copy(xn_st[di % 2][:, (i % 4) * 2048 + q * 512:
                                            (i % 4) * 2048 + (q + 1) * 512],
                              psX[:, q * 512:(q + 1) * 512]).then_inc(v_ms.h, 1)
    blk.vector(body)
    v_ms.v += 6 * NT
    v_sf.v += 4 * NT + 1

    def body(e):
        e.wait_ge(g_c.h, GC_ALL)
        for i in range(NT):
            e.wait_ge(v_ms.h, VLN0 + 6 * i + 4)
            psX = psA if i % 2 == 0 else psB
            for q in range(4):
                if i >= 2:
                    if q < 2:
                        e.wait_ge(a_ms.h, ALN0 + 5 * (i - 2) + 4 + q)
                    else:
                        e.wait_ge(v_ms.h, VLN0 + 6 * (i - 2) + 3 + q)
                for j in range(4):
                    cc = 4 * q + j
                    e.transpose(psX[:, q * 512 + j * 128:q * 512 + (j + 1) * 128],
                                xnf[i % 2][:, cc * 128:(cc + 1) * 128],
                                ident_f[:]).then_inc(t_ms.h, 1)
    blk.tensor(body)
    t_ms.v += 16 * NT

    def body(e):
        for di in range(NT // 4):
            xo = xn_cq[di].ap().rearrange("(a p) t -> p a t", p=128)
            for f in range(4):
                e.wait_ge(a_ms.h, ALN0 + 5 * (4 * di + f + 1))
                e.wait_ge(v_ms.h, VLN0 + 6 * (4 * di + f + 1))
                e.dma_start(out=xo[:, :, f * 128:(f + 1) * 128],
                            in_=xn_st[di % 2][:, f * 2048:(f + 1) * 2048]
                            .rearrange("p (a t) -> p a t", a=16)
                            ).then_inc(gxn[di % 2].h, 16)
                gxn[di % 2].v += 16
            e.wait_ge(gxn[di % 2].h, gxn[di % 2].v)
            e.collective_compute("AllGather", ALU.bypass, replica_groups=rg,
                                 ins=[xn_cq[di].ap().opt()],
                                 outs=[xn_aq[di].ap().opt()]).then_inc(ccq[di].h)
            ccq[di].v = 1
    blk.gpsimd(body)

    barrier()
    pLN.close()

    p1 = contextlib.ExitStack()
    def sb1(name, shape, dt):
        return p1.enter_context(nc.sbuf_tensor(name, shape, dt))
    xnT = [sb1("xnTa", [128, 16 * 512], BF16), sb1("xnTb", [128, 16 * 512], BF16)]
    qT2 = sb1("qT2", [128, 2 * T], BF16)
    kT2 = sb1("kT2", [128, 2 * T], BF16)
    v2 = sb1("v2", [128, 2 * T], BF16)
    vTs = sb1("vTs", [128, 512], F32)
    pT = [sb1(f"pT{i}", [128, 512], BF16) for i in range(6)]
    acc2 = [sb1("acc0", [128, 512], F32), sb1("acc1", [128, 512], F32)]
    accT_bf = sb1("accT", [128, 512], BF16)
    rden_sb = sb1("rden_sb", [1, 512], F32)
    dtmp = sb1("dtmp", [1, 1024], F32)
    bc_sb = sb1("bc_sb", [128, 512], F32)
    yn_bf = [sb1(f"yn{i}", [128, 512], BF16) for i in range(2)]

    VT_B0 = v_ms.v
    def body(e):
        for k, (wsrc, r0, nr, c0, wd, dst, dr, wid) in enumerate(WT2):
            if k == 0:
                continue
            e.wait_ge(v_ms.h, VT_B0 + 9 * k)
            e.dma_start(out=wqa[0:nr, 0:wd], in_=wsrc[r0:r0 + nr, c0:c0 + wd]
                        ).then_inc(g_wl.h, 16)
            g_wl.v += 16
    blk.gpsimd(body)

    def body(e):
        e.dma_start(out=wo_bf[:, :], in_=wo_in[:, :]).then_inc(g_st.h, 16)
    blk.gpsimd(body)
    g_st.v += 16
    WO_CAST_G = g_st.v
    GLDQ0 = g_wl.v
    GQ_BASE = (gq[0].v, gq[1].v)
    V_SF_FOLD = None
    V_FOLD = None
    QSH = ({qb: qb - 6 for qb in range(6, 16)}, {qb: 10 + qb for qb in range(6)})
    RSH = ({qb: qb - 5 for qb in range(5, 16)}, {qb: 11 + qb for qb in range(5)})

    for b in range(B):
        PWIN = 512
        NPW = T // PWIN
        projT0 = t_ms.v
        projA0 = a_ms.v
        vtV0 = v_ms.v
        STP0 = s_tp.v

        STW_BASE = (stw[0].v, stw[1].v)
        worder = ([r * 4 + q for q in range(4) for r in range(4)] if b == 0
                  else list(range(NPW)))
        def body(e, b=b, projT0=projT0, worder=worder):
            for p in range(NPW):
                w = worder[p]
                if p <= 1:
                    if b == 1:
                        e.wait_ge(t_ms.h, projT0)
                else:
                    e.wait_ge(t_ms.h, projT0 + (p - 2) * 14 + 6)
                if b == 0 and p % 4 == 0:
                    e.wait_ge(ccq[p // 4].h, 1)
                gtok = b * T + w * PWIN
                r = gtok // MYT
                cq = (gtok % MYT) // PWIN
                for cchunk in range(16):
                    src = xn_aq[cq][r * C + cchunk * 128: r * C + (cchunk + 1) * 128, :]
                    e.dma_start(out=xnT[p % 2][:, cchunk * PWIN:(cchunk + 1) * PWIN], in_=src
                                ).then_inc(stw[p % 2].h, 16)
                stw[p % 2].v += 256
        blk.sync(body)

        def body(e, b=b, projT0=projT0, projA0=projA0, vtV0=vtV0, STW_BASE=STW_BASE,
                 worder=worder):
            if b == 0:
                e.wait_ge(g_ld.h, GLD_QKV)
            e.wait_ge(a_ms.h, projA0)
            e.wait_ge(v_ms.h, vtV0)
            for p in range(NPW):
                w = worder[p]
                e.wait_ge(stw[p % 2].h, STW_BASE[p % 2] + 256 * (p // 2 + 1))
                for rnd in range(6):
                    gr = p * 6 + rnd
                    qtr = rnd % 4
                    ps = psA[:, qtr * 512:(qtr + 1) * 512]
                    if gr >= 4:
                        e.wait_ge(a_ms.h, projA0 + gr - 3)
                    kind, hh = divmod(rnd, 2)
                    last = None
                    for cchunk in range(16):
                        lhsT = wqkv[:, (kind * 2 + hh) * C + cchunk * 128:
                                    (kind * 2 + hh) * C + (cchunk + 1) * 128]
                        last = e.matmul(ps, lhsT,
                                 xnT[p % 2][:, cchunk * PWIN:(cchunk + 1) * PWIN],
                                 start=(cchunk == 0), stop=(cchunk == 15))
                    last.then_inc(t_ms.h, 1)
                for hh in range(2):
                    e.wait_ge(a_ms.h, projA0 + p * 6 + 4 + hh + 1)
                    for j in range(4):
                        tki = p * 8 + hh * 4 + j
                        if tki >= 2:
                            tp = tki - 2
                            idx = 9 * (tp // 8) + tp % 8 + 1 if b == 0 else tp + 1
                            e.wait_ge(v_ms.h, vtV0 + idx)
                        psT = psB[:, 0:128] if tki % 2 == 0 else psB[:, 512:640]
                        e.transpose(psT, vTs[:, j * 128:(j + 1) * 128],
                                    ident_f[:]).then_inc(t_ms.h, 1)
        blk.tensor(body)
        t_ms.v += NPW * 14

        def body(e, b=b, projT0=projT0, projA0=projA0, worder=worder):
            for p in range(NPW):
                w = worder[p]
                for rnd in range(6):
                    kind, hh = divmod(rnd, 2)
                    e.wait_ge(t_ms.h, projT0 + p * 14 + rnd + 1)
                    if kind == 2 and hh == 1:
                        e.wait_ge(t_ms.h, projT0 + p * 14 + 6 + 4)
                    bcol = biasT[:, kind * 2 + hh:kind * 2 + hh + 1]
                    if kind == 0:
                        dst = qT2[:, hh * T + w * PWIN: hh * T + (w + 1) * PWIN]
                    elif kind == 1:
                        dst = kT2[:, hh * T + w * PWIN: hh * T + (w + 1) * PWIN]
                    else:
                        dst = vTs[:, 0:PWIN]
                    qtr = rnd % 4
                    ps = psA[:, qtr * 512:(qtr + 1) * 512]
                    e.activation(dst, ps, AF.Identity, bias=bcol).then_inc(a_ms.h, 1)
        blk.scalar(body)
        a_ms.v += NPW * 6

        def body(e, b=b, projT0=projT0, worder=worder):
            for p in range(NPW):
                w = worder[p]
                for hh in range(2):
                    for j in range(4):
                        tki = p * 8 + hh * 4 + j
                        psT = psB[:, 0:128] if tki % 2 == 0 else psB[:, 512:640]
                        e.wait_ge(t_ms.h, projT0 + p * 14 + 6 + hh * 4 + j + 1)
                        kb = w * 4 + j
                        e.tensor_copy(v2[:, hh * T + kb * 128:hh * T + (kb + 1) * 128],
                                      psT).then_inc(v_ms.h, 1)
                if b == 0:
                    k = p
                    wsrc, r0, nr, c0, wd, dst, dr, wid = WT2[k]
                    e.wait_ge(g_wl.h, (k + 1) * 16)
                    e.reduce_sum(stat[0:nr, 32 + k:33 + k], wqa[0:nr, 0:wd], axis=AX.X,
                                 apply_absolute_value=True).then_inc(v_ms.h, 1)
                    if nr < 128:
                        e.memset(stat[64:128, 32 + k:33 + k], 0.0)
            if b == 0:
                e.wait_ge(v_ms.h, vtV0 + NPW * 9)
                e.reduce_sum(stat[:, 48:49], stat[:, 32:38], axis=AX.X).then_inc(v_sf.h, 1)
                e.reduce_sum(stat[:, 49:50], stat[:, 38:44], axis=AX.X).then_inc(v_sf.h, 1)
                e.memset(stat[:, 51:52], 0.0).then_inc(v_sf.h, 1)
                e.reduce_sum(stat[:, 50:51], stat[:, 44:48], axis=AX.X).then_inc(v_ms.h, 1)
        blk.vector(body)
        if b == 0:
            assert vtV0 == VT_B0
            v_ms.v += NPW * 9 + 1
            v_sf.v += 3
            V_SF_FOLD = v_sf.v
            V_FOLD = v_ms.v
        else:
            v_ms.v += NPW * 8
        if b == 0:
            VTE_INS = v_ms.v
            def body(e, VTE_INS=VTE_INS):
                e.wait_ge(v_ms.h, V_FOLD)
                e.wait_ge(v_sf.h, V_SF_FOLD)
                e.wait_ge(v_ms.h, VTE_INS)
                e.matmul(psB[0:1, 0:4], ones_f[:, 0:1], stat[:, 48:52], start=True,
                         stop=True).then_inc(t_ms.h, 1)
            blk.tensor(body)
            t_ms.v += 1
            T_GAM = t_ms.v
            def body(e, T_GAM=T_GAM):
                e.wait_ge(t_ms.h, T_GAM)
                e.tensor_copy(grow[0:1, 0:4], psB[0:1, 0:4]).then_inc(v_ms.h, 1)
            blk.vector(body)
            v_ms.v += 1
            V_GROW = v_ms.v
            def body(e, V_GROW=V_GROW):
                e.wait_ge(v_ms.h, V_GROW)
                e.dma_start(out=gam_c[:, :], in_=grow[0:1, 0:4]).then_inc(g_st.h, 16)
                e.wait_ge(g_st.h, g_st.v + 16)
                e.collective_compute("AllReduce", ALU.add, replica_groups=rg,
                                     ins=[gam_c.ap().opt()],
                                     outs=[gam_s.ap().opt()]).then_inc(cc0.h)
                e.wait_ge(cc0.h, 1)
                e.dma_start(out=grow[0:1, 8:12], in_=gam_s[:, :]).then_inc(g_ld.h, 16)
            blk.gpsimd(body)
            g_st.v += 16
            g_ld.v += 16
            GLD_GAMS = g_ld.v
            cc0.v = 1
            def body(e, GLD_GAMS=GLD_GAMS):
                e.wait_ge(g_ld.h, GLD_GAMS)
                e.tensor_scalar(grow[0:1, 12:16], grow[0:1, 8:12], 1.0 / WCNT, EPS_Q,
                                ALU.mult, ALU.add).then_inc(v_sf.h, 1)
                e.wait_ge(v_sf.h, v_sf.v + 1)
                e.reciprocal(grow[0:1, 4:8], grow[0:1, 12:16]).then_inc(v_ms.h, 1)
            blk.vector(body)
            v_ms.v += 1
            v_sf.v += 1
            V_RECIP = v_ms.v
            def body(e, V_RECIP=V_RECIP):
                e.wait_ge(v_ms.h, V_RECIP)
                e.matmul(psB[:, 8:12], ones_f[0:1, :], grow[0:1, 4:8], start=True,
                         stop=True).then_inc(t_ms.h, 1)
            blk.tensor(body)
            t_ms.v += 1
            T_RQ = t_ms.v
            def body(e, T_RQ=T_RQ):
                e.wait_ge(t_ms.h, T_RQ)
                e.tensor_copy(rq_sb[:], psB[:, 8:12]).then_inc(v_ms.h, 1)
            blk.vector(body)
            v_ms.v += 1
            V_RQ = v_ms.v

        PROJ_A_END = a_ms.v
        VT_V_END = v_ms.v

        for hh in range(2):
            pair = b * 2 + hh
            TKD = t_ms.v
            def body(e, hh=hh, TKD=TKD):
                e.wait_ge(a_ms.h, PROJ_A_END)
                e.wait_ge(t_ms.h, TKD)
                e.tensor_tensor(xnT[0][:, 0:T], qT2[:, hh * T:(hh + 1) * T],
                                qT2[:, hh * T:(hh + 1) * T], ALU.mult).then_inc(v_ms.h, 1)
            blk.vector(body)
            v_ms.v += 1
            QSQ_V = v_ms.v

            BQ_T0 = t_ms.v
            def body(e, QSQ_V=QSQ_V):
                e.wait_ge(v_ms.h, QSQ_V)
                for ch in range(16):
                    if ch >= 2:
                        e.wait_ge(v_ms.h, QSQ_V + ch - 1)
                    ps = psB[0:1, 1536:2048] if ch % 2 == 0 else psB[0:1, 1024:1536]
                    e.matmul(ps, ones_bf[:, 0:1], xnT[0][:, ch * 512:(ch + 1) * 512],
                             start=True, stop=True).then_inc(t_ms.h, 1)
            blk.tensor(body)
            t_ms.v += 16

            def body(e, BQ_T0=BQ_T0, pair=pair, QSQ_V=QSQ_V):
                for ch in range(16):
                    ps = psB[0:1, 1536:2048] if ch % 2 == 0 else psB[0:1, 1024:1536]
                    e.wait_ge(t_ms.h, BQ_T0 + ch + 1)
                    e.reduce_max(bmax[0:1, ch:ch + 1], ps, axis=AX.X).then_inc(v_ms.h, 1)
                e.wait_ge(v_ms.h, QSQ_V + 16)
                e.reduce_max(bmax[0:1, 32 + pair:33 + pair], bmax[0:1, 0:16],
                             axis=AX.X).then_inc(v_ms.h, 1)
            blk.vector(body)
            v_ms.v += 17

            QD = t_ms.v
            def body(e, hh=hh, QD=QD):
                e.wait_ge(t_ms.h, QD)
                e.tensor_tensor(xnT[0][:, 0:T], kT2[:, hh * T:(hh + 1) * T],
                                kT2[:, hh * T:(hh + 1) * T], ALU.mult).then_inc(v_ms.h, 1)
            blk.vector(body)
            v_ms.v += 1
            KSQ_V = v_ms.v

            BK_T0 = t_ms.v
            def body(e, KSQ_V=KSQ_V):
                e.wait_ge(v_ms.h, KSQ_V)
                for ch in range(16):
                    if ch >= 2:
                        e.wait_ge(v_ms.h, KSQ_V + ch - 1)
                    ps = psB[0:1, 1536:2048] if ch % 2 == 0 else psB[0:1, 1024:1536]
                    e.matmul(ps, ones_bf[:, 0:1], xnT[0][:, ch * 512:(ch + 1) * 512],
                             start=True, stop=True).then_inc(t_ms.h, 1)
            blk.tensor(body)
            t_ms.v += 16

            def body(e, BK_T0=BK_T0, pair=pair, KSQ_V=KSQ_V):
                for ch in range(16):
                    ps = psB[0:1, 1536:2048] if ch % 2 == 0 else psB[0:1, 1024:1536]
                    e.wait_ge(t_ms.h, BK_T0 + ch + 1)
                    e.reduce_max(bmax[0:1, ch:ch + 1], ps, axis=AX.X).then_inc(v_ms.h, 1)
                e.wait_ge(v_ms.h, KSQ_V + 16)
                e.reduce_max(bmax[0:1, 40 + pair:41 + pair], bmax[0:1, 0:16],
                             axis=AX.X).then_inc(v_sf.h, 1)
                e.wait_ge(v_sf.h, v_sf.v + 1)
                e.tensor_tensor(bmax[0:1, 48 + pair:49 + pair], bmax[0:1, 32 + pair:33 + pair],
                                bmax[0:1, 40 + pair:41 + pair], ALU.mult).then_inc(v_ms.h, 1)
            blk.vector(body)
            v_ms.v += 17
            v_sf.v += 1
            PRD_V = v_ms.v

            def body(e, pair=pair, PRD_V=PRD_V):
                e.wait_ge(v_ms.h, PRD_V)
                e.activation(bmax[0:1, 56 + pair:57 + pair], bmax[0:1, 48 + pair:49 + pair],
                             AF.Ln).then_inc(a_sf.h, 1)
                e.wait_ge(a_sf.h, a_sf.v + 1)
                e.activation(bmax[0:1, 48 + pair:49 + pair], bmax[0:1, 56 + pair:57 + pair],
                             AF.Exp, scale=0.5).then_inc(a_ms.h, 1)
            blk.scalar(body)
            a_ms.v += 1
            a_sf.v += 1
            SQ_A = a_ms.v

            def body(e, pair=pair, SQ_A=SQ_A):
                e.wait_ge(a_ms.h, SQ_A)
                e.tensor_scalar(bmax[0:1, 56 + pair:57 + pair], bmax[0:1, 48 + pair:49 + pair],
                                -SM_SCALE, None, ALU.mult).then_inc(v_ms.h, 1)
            blk.vector(body)
            v_ms.v += 1
            NB_V = v_ms.v

            def body(e, NB_V=NB_V, pair=pair):
                e.wait_ge(v_ms.h, NB_V)
                e.matmul(psB[:, 1024:1025], ones_f[0:1, :], bmax[0:1, 56 + pair:57 + pair],
                         start=True, stop=True).then_inc(t_ms.h, 1)
            blk.tensor(body)
            t_ms.v += 1
            NB_T = t_ms.v

            def body(e, NB_T=NB_T, pair=pair):
                e.wait_ge(t_ms.h, NB_T)
                e.tensor_copy(negb[:, pair:pair + 1], psB[:, 1024:1025]).then_inc(v_ms.h, 1)
            blk.vector(body)
            v_ms.v += 1
            NEGB_V = v_ms.v

            qk_t0 = t_ms.v
            ae_a0 = a_ms.v
            acc_v0 = v_ms.v
            vd0 = vd.v
            NKBS = [4 * (q + 1) for q in range(16)]
            klist = [(qb, j) for qb in range(16) for j in range(NKBS[qb])]
            CUM = []
            _c = 0
            for q in range(16):
                _c += NKBS[q]
                CUM.append(_c)
            NKI = len(klist)
            DLAG = 3
            NSL = 5
            NPT = 6
            def vb(qb):
                return acc_v0 + 6 * qb
            def mask_inc(ki):
                qb, j = klist[ki]
                return vb(qb) + (j - (NKBS[qb] - 4)) + 1
            def is_diag(ki):
                qb, j = klist[ki]
                return j >= NKBS[qb] - 4
            def acci(ki, vd0=vd0, klist=klist):
                return vd0 + ki + klist[ki][0] + 1
            def cmbi(qb, vd0=vd0, CUM=CUM):
                return vd0 + CUM[qb] + qb + 1
            sched = []
            for ki in range(NKI + DLAG):
                if ki < NKI:
                    sched.append(("qk", ki))
                kj = ki - DLAG
                if kj >= 0:
                    sched.append(("pv", kj))
                    qb, j = klist[kj]
                    if j == NKBS[qb] - 1:
                        sched.append(("dn", qb))
                        sched.append(("bc", qb))
            qkI, pvI, dnI, bcI = {}, {}, {}, {}
            for idx, (kind, a) in enumerate(sched):
                {"qk": qkI, "pv": pvI, "dn": dnI, "bc": bcI}[kind][a] = qk_t0 + idx + 1
            def st_slot(ki):
                s = ki % NSL
                return psA[:, s * 512:(s + 1) * 512] if s < 4 else psB[:, 1024:1536]

            def body(e, hh=hh, pair=pair, NEGB_V=NEGB_V, VT_V_END=VT_V_END,
                     vb=vb, mask_inc=mask_inc, st_slot=st_slot, ae_a0=ae_a0,
                     sched=sched, cmbi=cmbi):
                e.wait_ge(v_ms.h, NEGB_V)
                e.wait_ge(v_ms.h, VT_V_END)
                for kind, a in sched:
                    if kind == "qk":
                        ki = a
                        qb, j = klist[ki]
                        if ki >= NSL:
                            e.wait_ge(a_ms.h, ae_a0 + ki - NSL + 1)
                        e.matmul(st_slot(ki),
                                 kT2[:, hh * T + j * 128:hh * T + (j + 1) * 128],
                                 qT2[:, hh * T + qb * 512:hh * T + (qb + 1) * 512],
                                 start=True, stop=True).then_inc(t_ms.h, 1)
                    elif kind == "pv":
                        ki = a
                        qb, j = klist[ki]
                        nkb = NKBS[qb]
                        e.wait_ge(a_ms.h, ae_a0 + ki + 1)
                        if j >= nkb - 4:
                            e.wait_ge(v_ms.h, mask_inc(ki))
                        if j == 0 and qb >= 1:
                            e.wait_ge(v_ms.h, vb(qb))
                        buf = pT[ki % NPT]
                        e.matmul(psB[:, 0:512],
                                 v2[:, hh * T + j * 128:hh * T + (j + 1) * 128],
                                 buf[:], start=(j == 0), stop=(j == nkb - 1)
                                 ).then_inc(t_ms.h, 1)
                    elif kind == "dn":
                        qb = a
                        e.wait_ge(vd.h, cmbi(qb))
                        e.matmul(psB[0:1, 512:1024], ones_bf[:, 0:1], accT_bf[:],
                                 start=True, stop=True).then_inc(t_ms.h, 1)
                    else:
                        qb = a
                        e.wait_ge(v_ms.h, vb(qb) + 5)
                        e.matmul(psB[:, 1536:2048], ones_f[0:1, :], rden_sb[0:1, :],
                                 start=True, stop=True).then_inc(t_ms.h, 1)
            blk.tensor(body)
            t_ms.v = qk_t0 + len(sched)

            def body(e, pair=pair, qkI=qkI, pvI=pvI, mask_inc=mask_inc,
                     is_diag=is_diag, st_slot=st_slot, ae_a0=ae_a0, acci=acci):
                for ki in range(NKI):
                    e.wait_ge(t_ms.h, qkI[ki])
                    if ki >= NPT:
                        e.wait_ge(t_ms.h, pvI[ki - NPT])
                        e.wait_ge(vd.h, acci(ki - NPT))
                        if is_diag(ki - NPT):
                            e.wait_ge(v_ms.h, mask_inc(ki - NPT))
                    e.activation(pT[ki % NPT][:], st_slot(ki), AF.Exp,
                                 bias=negb[:, pair:pair + 1],
                                 scale=SM_SCALE).then_inc(a_ms.h, 1)
            blk.scalar(body)
            a_ms.v = ae_a0 + NKI

            gy_base = (gy[0].v, gy[1].v)
            VSF_ATT = v_sf.v
            def body(e, b=b, hh=hh, pair=pair, ae_a0=ae_a0, vb=vb, dnI=dnI, bcI=bcI,
                     gy_base=gy_base, VSF_ATT=VSF_ATT, acci=acci,
                     cmbi=cmbi):
                ki = 0
                for qb in range(16):
                    nkb = NKBS[qb]
                    for j in range(nkb):
                        e.wait_ge(a_ms.h, ae_a0 + ki + 1)
                        if j >= nkb - 4:
                            dj = j - (nkb - 4)
                            e.tensor_tensor(pT[ki % NPT][:], pT[ki % NPT][:],
                                            mask_bf[:, dj * 512:(dj + 1) * 512],
                                            ALU.mult).then_inc(v_ms.h, 1)
                        if j < 2:
                            if qb >= 1:
                                e.wait_ge(vd.h, cmbi(qb - 1))
                            e.tensor_copy(acc2[j][:], pT[ki % NPT][:]).then_inc(vd.h, 1)
                        else:
                            e.wait_ge(vd.h, acci(ki - 2))
                            e.tensor_tensor(acc2[j % 2][:], acc2[j % 2][:],
                                            pT[ki % NPT][:], ALU.add).then_inc(vd.h, 1)
                        ki += 1
                    e.wait_ge(vd.h, acci(ki - 1))
                    e.tensor_tensor(accT_bf[:], acc2[0][:], acc2[1][:],
                                    ALU.add).then_inc(vd.h, 1)
                    e.wait_ge(t_ms.h, dnI[qb])
                    e.tensor_copy(dtmp[0:1, 0:512], psB[0:1, 512:1024]).then_inc(v_sf.h, 1)
                    e.wait_ge(v_sf.h, VSF_ATT + 2 * qb + 1)
                    e.reciprocal(rden_sb[0:1, :], dtmp[0:1, 0:512]).then_inc(v_ms.h, 1)
                    e.wait_ge(t_ms.h, bcI[qb])
                    e.tensor_copy(bc_sb[:], psB[:, 1536:2048]).then_inc(v_sf.h, 1)
                    e.wait_ge(gy[qb % 2].h, gy_base[qb % 2] + 16 * (qb // 2))
                    e.wait_ge(v_sf.h, VSF_ATT + 2 * qb + 2)
                    e.tensor_tensor(yn_bf[qb % 2][:], psB[:, 0:512], bc_sb[:],
                                    ALU.mult).then_inc(v_ms.h, 1)
                    k = QSH[hh].get(qb) if b == 0 else None
                    if k is not None:
                        wsrc, r0, nr, c0, wd, dst, dr, wid = WT2[k]
                        e.wait_ge(g_wl.h, GLDQ0 + (k + 1) * 16)
                        if k == 0:
                            e.wait_ge(v_ms.h, V_RQ)
                        col = rq_sb[0:nr, wid:wid + 1]
                        e.tensor_scalar(wqa[0:nr, 0:wd], wqa[0:nr, 0:wd], col, RND_C,
                                        ALU.mult, ALU.add).then_inc(vq.h, 1)
                        e.wait_ge(vq.h, 3 * k + 1)
                        e.tensor_scalar(wqa[0:nr, 0:wd], wqa[0:nr, 0:wd], -RND_C, 1.0,
                                        ALU.add, ALU.min).then_inc(vq.h, 1)
                        e.wait_ge(vq.h, 3 * k + 2)
                        if k >= 2:
                            e.wait_ge(gq[k % 2].h, GQ_BASE[k % 2] + 16 * (k // 2))
                        e.tensor_scalar(qbf[k % 2][0:nr, 0:wd], wqa[0:nr, 0:wd], -1.0, None,
                                        ALU.max).then_inc(vq.h, 1)
            blk.vector(body)
            v_ms.v = vb(16)
            v_sf.v += 32
            vd.v = cmbi(15)
            if b == 0:
                vq.v += 3 * len(QSH[hh])

            def body(e, b=b, hh=hh, vb=vb):
                for qb in range(16):
                    e.wait_ge(v_ms.h, vb(qb) + 6)
                    jdst = 4 * b + qb // 4
                    row = 256 * jdst + 128 * hh
                    col = (qb % 4) * 512
                    e.dma_start(out=yt_c[row:row + 128, col:col + 512],
                                in_=yn_bf[qb % 2][:]).then_inc(gy[qb % 2].h, 16)
                    gy[qb % 2].v += 16
                    rk = RSH[hh].get(qb) if b == 0 else None
                    if rk is not None:
                        wsrc, r0, nr, c0, wd, dst, dr, wid = WT2[rk]
                        if rk == 0:
                            e.wait_ge(v_ms.h, V_FOLD)
                        else:
                            e.wait_ge(vq.h, 3 * (rk - 1) + 3)
                        e.dma_start(out=wqa[0:nr, 0:wd], in_=wsrc[r0:r0 + nr, c0:c0 + wd]
                                    ).then_inc(g_wl.h, 16)
                        g_wl.v += 16
                        if rk >= 1:
                            ws2, r02, nr2, c02, wd2, dst2, dr2, wid2 = WT2[rk - 1]
                            e.dma_start(out=dst2[dr2:dr2 + nr2, c02:c02 + wd2],
                                        in_=qbf[(rk - 1) % 2][0:nr2, 0:wd2]
                                        ).then_inc(gq[(rk - 1) % 2].h, 16)
                            gq[(rk - 1) % 2].v += 16
            blk.gpsimd(body)

        if b == 0 and hh == 1:
            def body(e):
                k = NWT2 - 1
                (ws2, r02, nr2, c02, wd2, dst2, dr2, wid2) = WT2[k]
                e.wait_ge(vq.h, 3 * k + 3)
                e.dma_start(out=dst2[dr2:dr2 + nr2, c02:c02 + wd2],
                            in_=qbf[k % 2][0:nr2, 0:wd2]).then_inc(gq[k % 2].h, 16)
                gq[k % 2].v += 16
                e.wait_ge(gq[0].h, gq[0].v)
                e.wait_ge(gq[1].h, gq[1].v)
                e.collective_compute("AllGather", ALU.bypass, replica_groups=rg,
                                     ins=[w1qc.ap().opt()], outs=[w1q_all.ap().opt()]).then_inc(cc3a.h)
                e.collective_compute("AllGather", ALU.bypass, replica_groups=rg,
                                     ins=[w2qc.ap().opt()], outs=[w2q_all.ap().opt()]).then_inc(cc3b.h)
                e.collective_compute("AllGather", ALU.bypass, replica_groups=rg,
                                     ins=[w3qc.ap().opt()], outs=[w3q_all.ap().opt()]).then_inc(cc3c.h)
            blk.gpsimd(body)
            cc3a.v = cc3b.v = cc3c.v = 1

    def body(e):
        e.wait_ge(gy[0].h, gy[0].v)
        e.wait_ge(gy[1].h, gy[1].v)
        with allow_ici_all_to_all():
            e.collective_compute("AllToAll", ALU.bypass, replica_groups=rg,
                                 ins=[yt_c.ap().opt()], outs=[yt_my.ap().opt()]).then_inc(cc2.h)
    blk.gpsimd(body)
    cc2.v = 1

    barrier()
    p1.close()
    pW.close()

    p2 = contextlib.ExitStack()
    def sb2(name, shape, dt):
        return p2.enter_context(nc.sbuf_tensor(name, shape, dt))
    NHT = HID // 128
    NHQ = NHT // 4
    hidT = sb2("hidT", [128, NHT * WIN], BF16)
    xnT2 = sb2("xnT2", [128, 16 * WIN], BF16)
    wp1 = [sb2("wp1a", [128, 16 * 512], BF16), sb2("wp1b", [128, 16 * 512], BF16)]
    wp2 = [sb2("wp2a", [128, 16 * 512], BF16), sb2("wp2b", [128, 16 * 512], BF16)]
    w3t = [sb2(f"w3t{i}", [128, 512], BF16) for i in range(4)]
    sgm = sb2("sgm", [128, WIN], F32)
    gbf = [sb2(f"gbf{i}", [128, 512], BF16) for i in range(2)]

    W3_T_PREV = [0]
    X1_PREV = [0]
    for half in range(2):
        STP_H = s_tp.v
        X1P_H = X1_PREV[0]
        X1T0 = t_ms.v
        HIDV0 = v_ms.v
        SGA0 = a_ms.v

        SWP_BASE = (swp[0].v, swp[1].v)
        SWQ_BASE = (swq[0].v, swq[1].v)
        def body(e, half=half, X1T0=X1T0, X1P=X1P_H):
            e.wait_ge(cc3a.h, 1)
            e.wait_ge(cc3b.h, 1)
            for hq in range(NHQ):
                if hq >= 2:
                    e.wait_ge(t_ms.h, X1T0 + (hq - 1) * 4)
                for cchunk in range(16):
                    e.dma_start(out=wp1[hq % 2][:, cchunk * 512:(cchunk + 1) * 512],
                                in_=w1q_all[hq * 512:(hq + 1) * 512,
                                            cchunk * 128:(cchunk + 1) * 128],
                                transpose=True).then_inc(swp[hq % 2].h, 16)
                    e.dma_start(out=wp2[hq % 2][:, cchunk * 512:(cchunk + 1) * 512],
                                in_=w2q_all[hq * 512:(hq + 1) * 512,
                                            cchunk * 128:(cchunk + 1) * 128],
                                transpose=True).then_inc(swp[hq % 2].h, 16)
                swp[hq % 2].v += 512
        blk.sync(body)

        def body(e, half=half, X1P=X1P_H):
            if half == 1:
                e.wait_ge(t_ms.h, X1P)
            for cchunk in range(16):
                for sub in range(2):
                    src = xn_cq[2 * half + sub][cchunk * 128:(cchunk + 1) * 128, :]
                    e.dma_start(out=xnT2[:, cchunk * WIN + sub * 512:
                                         cchunk * WIN + (sub + 1) * 512], in_=src
                                ).then_inc(swq[0].h, 16)
        blk.scalar(body)
        swq[0].v += 512

        def body(e, half=half, STP_H=STP_H, SWP_BASE=SWP_BASE, HIDV0=HIDV0,
                 W3P=W3_T_PREV[0]):
            e.wait_ge(swq[0].h, SWQ_BASE[0] + 512)
            if half == 1:
                e.wait_ge(v_ms.h, HIDV0)
            for hq in range(NHQ):
                e.wait_ge(swp[hq % 2].h, SWP_BASE[hq % 2] + 512 * (hq // 2 + 1))
                for ht in range(4):
                    hti = hq * 4 + ht
                    pspar = psA if hti % 2 == 0 else psB
                    x1ps = pspar[:, 0:1024]
                    x2ps = pspar[:, 1024:2048]
                    if hti >= 2:
                        e.wait_ge(v_ms.h, HIDV0 + hti - 1)
                    last = None
                    for cchunk in range(16):
                        lhs1 = wp1[hq % 2][:, cchunk * 512 + ht * 128: cchunk * 512 + ht * 128 + 128]
                        lhs2 = wp2[hq % 2][:, cchunk * 512 + ht * 128: cchunk * 512 + ht * 128 + 128]
                        for nn in range(2):
                            e.matmul(x1ps[:, nn * 512:(nn + 1) * 512], lhs1,
                                     xnT2[:, cchunk * WIN + nn * 512:cchunk * WIN + (nn + 1) * 512],
                                     start=(cchunk == 0), stop=(cchunk == 15))
                        for nn in range(2):
                            last = e.matmul(x2ps[:, nn * 512:(nn + 1) * 512], lhs2,
                                     xnT2[:, cchunk * WIN + nn * 512:cchunk * WIN + (nn + 1) * 512],
                                     start=(cchunk == 0), stop=(cchunk == 15))
                    last.then_inc(t_ms.h, 1)
        blk.tensor(body)
        t_ms.v += NHT
        X1_PREV[0] = t_ms.v

        def wp2load(e, hq):
            for cchunk in range(16):
                e.dma_start(out=wp2[hq % 2][:, cchunk * 512:(cchunk + 1) * 512],
                            in_=w2q_all[hq * 512:(hq + 1) * 512,
                                        cchunk * 128:(cchunk + 1) * 128],
                            transpose=True).then_inc(swq[hq % 2].h, 16)
            swq[hq % 2].v += 256

        def body(e, half=half, X1T0=X1T0, HIDV0=HIDV0, wp2load=wp2load,
                 X1P=X1P_H):
            for hti in range(NHT):
                pspar = psA if hti % 2 == 0 else psB
                e.wait_ge(t_ms.h, X1T0 + hti + 1)
                if hti >= 1:
                    e.wait_ge(v_ms.h, HIDV0 + hti)
                e.activation(sgm[:], pspar[:, 0:1024], AF.Sigmoid).then_inc(a_ms.h, 1)
        blk.scalar(body)
        a_ms.v += NHT

        def body(e, SGA0=SGA0, HIDV0=HIDV0):
            for hti in range(NHT):
                pspar = psA if hti % 2 == 0 else psB
                e.wait_ge(a_ms.h, SGA0 + hti + 1)
                e.tensor_tensor(hidT[:, hti * WIN:(hti + 1) * WIN], sgm[:],
                                pspar[:, 1024:2048], ALU.mult).then_inc(v_ms.h, 1)
        blk.vector(body)
        v_ms.v += NHT
        HID_END_V = v_ms.v

        W3T0 = t_ms.v
        W3A0 = a_ms.v
        W3V0 = v_ms.v

        SW3_BASE = tuple(s.v for s in sw3)
        def body(e, W3T0=W3T0, HID_END_V=HID_END_V):
            e.wait_ge(v_ms.h, HID_END_V)
            e.wait_ge(cc3c.h, 1)
            for cq in range(4):
                for hc in range(NHT):
                    idx = cq * NHT + hc
                    if idx >= 4:
                        e.wait_ge(t_ms.h, W3T0 + idx - 3)
                    slot = idx % 4
                    e.dma_start(out=w3t[slot][:],
                                in_=w3q_all[cq * 512:(cq + 1) * 512,
                                            hc * 128:(hc + 1) * 128],
                                transpose=True).then_inc(sw3[slot].h, 16)
                    sw3[slot].v += 16
        blk.sync(body)

        def body(e, W3T0=W3T0, W3V0=W3V0, SW3_BASE=SW3_BASE):
            for cq in range(4):
                for hc in range(NHT):
                    idx = cq * NHT + hc
                    slot = idx % 4
                    e.wait_ge(sw3[slot].h, SW3_BASE[slot] + 16 * (idx // 4 + 1))
                    last = None
                    for tt in range(8):
                        if hc == 0 and cq > 0:
                            e.wait_ge(v_ms.h, W3V0 + (cq - 1) * 8 + tt + 1)
                        ps = (psA if tt < 4 else psB)[:, (tt % 4) * 512:(tt % 4 + 1) * 512]
                        last = e.matmul(ps, hidT[:, hc * WIN + tt * 128: hc * WIN + (tt + 1) * 128],
                                 w3t[slot][:, :], start=(hc == 0), stop=(hc == NHT - 1))
                    last.then_inc(t_ms.h, 1)
        blk.tensor(body)
        t_ms.v += 4 * NHT
        W3_T_PREV[0] = t_ms.v

        def body(e, W3T0=W3T0, W3V0=W3V0):
            for cq in range(4):
                e.wait_ge(t_ms.h, W3T0 + (cq + 1) * NHT)
                for tt in range(8):
                    idx = cq * 8 + tt
                    if idx >= 1:
                        e.wait_ge(v_ms.h, W3V0 + idx)
                    ps = (psA if tt < 4 else psB)[:, (tt % 4) * 512:(tt % 4 + 1) * 512]
                    e.activation(sgm[:, 0:512], ps, AF.Sigmoid).then_inc(a_ms.h, 1)
        blk.scalar(body)
        a_ms.v += 32

        GG_BASE = (gg[0].v, gg[1].v)
        def body(e, W3A0=W3A0, W3V0=W3V0, GG_BASE=GG_BASE):
            for cq in range(4):
                for tt in range(8):
                    idx = cq * 8 + tt
                    ps = (psA if tt < 4 else psB)[:, (tt % 4) * 512:(tt % 4 + 1) * 512]
                    e.wait_ge(a_ms.h, W3A0 + idx + 1)
                    e.wait_ge(gg[idx % 2].h, GG_BASE[idx % 2] + 16 * (idx // 2))
                    e.tensor_tensor(gbf[idx % 2][:], sgm[:, 0:512], ps,
                                    ALU.mult).then_inc(v_ms.h, 1)
        blk.vector(body)
        v_ms.v += 32

        def body(e, half=half, W3V0=W3V0):
            for cq in range(4):
                for tt in range(8):
                    idx = cq * 8 + tt
                    e.wait_ge(v_ms.h, W3V0 + idx + 1)
                    e.dma_start(out=g_sc[half * WIN + tt * 128: half * WIN + (tt + 1) * 128,
                                         cq * 512:(cq + 1) * 512],
                                in_=gbf[idx % 2][:]).then_inc(gg[idx % 2].h, 16)
                    gg[idx % 2].v += 16
        blk.gpsimd(body)

    barrier()
    p2.close()

    p3 = contextlib.ExitStack()
    def sb3(name, shape, dt):
        return p3.enter_context(nc.sbuf_tensor(name, shape, dt))
    yT_sb = sb3("yT_sb", [128, 16 * 2048], BF16)
    wo_sb = sb3("wo_sb", [128, 16 * 2048], BF16)
    bo_row = sb3("bo_row", [1, 2048], F32)
    bobc = sb3("bobc", [128, 2048], F32)
    xt3 = [sb3("xt3a", [128, 2048], F32), sb3("xt3b", [128, 2048], F32)]
    gt3 = [sb3("gt3a", [128, 2048], BF16), sb3("gt3b", [128, 2048], BF16)]
    s_bf = [sb3("sbfa", [128, 2048], BF16), sb3("sbfb", [128, 2048], BF16)]
    sg_f = sb3("sg_f", [128, 2048], F32)

    P3G0 = g_ld.v
    P3S0 = s_ld.v
    P3T0 = t_ms.v
    P3V0 = v_ms.v
    P3GS0 = g_st.v

    def body(e):
        e.wait_ge(g_st.h, WO_CAST_G)
        for q in range(2):
            e.dma_start(out=wo_sb[:, q * 8192:(q + 1) * 8192],
                        in_=wo_bf[q * 512:(q + 1) * 512, :]
                        .rearrange("(a p) n -> p a n", p=128)).then_inc(g_yw.h, 16)
    blk.gpsimd(body)
    g_yw.v += 2 * 16
    YWO_G = g_yw.v

    SWQ3_BASE = swq[1].v
    def body(e):
        e.wait_ge(g_st.h, WO_CAST_G)
        for q in range(2, 4):
            e.dma_start(out=wo_sb[:, q * 8192:(q + 1) * 8192],
                        in_=wo_bf[q * 512:(q + 1) * 512, :]
                        .rearrange("(a p) n -> p a n", p=128)).then_inc(swq[1].h, 16)
        for tt in range(16):
            if tt >= 2:
                e.wait_ge(v_ms.h, P3V0 + 1 + 3 * (tt - 2) + 3)
            e.dma_start(out=gt3[tt % 2][:], in_=g_sc[tt * 128:(tt + 1) * 128, :]
                        ).then_inc(gt3s[tt % 2].h, 16)
            gt3s[tt % 2].v += 16
    blk.scalar(body)
    swq[1].v += 2 * 16

    def body(e):
        e.dma_start(out=bo_row[:], in_=bo_in[:, :]).then_inc(s_ld.h, 16)
        s_ld.v += 16
        e.wait_ge(cc2.h, 1)
        for q in range(4):
            e.dma_start(out=yT_sb[:, q * 8192:(q + 1) * 8192],
                        in_=yt_my[q * 512:(q + 1) * 512, :]
                        .rearrange("(a p) t -> p a t", p=128)).then_inc(s_ld.h, 16)
        s_ld.v += 4 * 16
        for tt in range(16):
            if tt >= 2:
                e.wait_ge(go[tt % 2].h, 16 * (tt // 2))
            e.dma_start(out=xt3[tt % 2][:], in_=x_my[tt * 128:(tt + 1) * 128, :]
                        ).then_inc(sx3[tt % 2].h, 16)
            sx3[tt % 2].v += 16
    blk.sync(body)

    def body(e):
        e.wait_ge(s_ld.h, P3S0 + 16)
        last = None
        for nn in range(4):
            last = e.matmul(psB[:, nn * 512:(nn + 1) * 512], ones_f[0:1, :],
                     bo_row[0:1, nn * 512:(nn + 1) * 512], start=True, stop=True)
        last.then_inc(t_ms.h, 1)
        e.wait_ge(g_yw.h, YWO_G)
        e.wait_ge(swq[1].h, SWQ3_BASE + 2 * 16)
        e.wait_ge(s_ld.h, P3S0 + 5 * 16)
        for tt in range(16):
            pspar = psA if tt % 2 == 0 else psB
            if tt == 1:
                e.wait_ge(v_ms.h, P3V0 + 1)
            if tt >= 2:
                e.wait_ge(v_ms.h, P3V0 + 1 + 3 * (tt - 2) + 1)
            last = None
            for cchunk in range(16):
                lhsT = yT_sb[:, cchunk * 2048 + tt * 128: cchunk * 2048 + tt * 128 + 128]
                for nn in range(4):
                    last = e.matmul(pspar[:, nn * 512:(nn + 1) * 512], lhsT,
                             wo_sb[:, cchunk * 2048 + nn * 512: cchunk * 2048 + (nn + 1) * 512],
                             start=(cchunk == 0), stop=(cchunk == 15))
            last.then_inc(t_ms.h, 1)
    blk.tensor(body)
    t_ms.v += 17

    def body(e):
        e.wait_ge(t_ms.h, P3T0 + 1)
        e.tensor_copy(bobc[:], psB[:, 0:2048]).then_inc(v_ms.h, 1)
        e.wait_ge(v_ms.h, P3V0 + 1)
        sfb3 = v_sf.v
        for tt in range(16):
            pspar = psA if tt % 2 == 0 else psB
            e.wait_ge(t_ms.h, P3T0 + tt + 2)
            e.tensor_tensor(s_bf[tt % 2][:], pspar[:, 0:2048], bobc[:],
                            ALU.add).then_inc(v_ms.h, 1)
            e.wait_ge(gt3s[tt % 2].h, 16 * (tt // 2 + 1))
            e.wait_ge(v_ms.h, P3V0 + 1 + 3 * tt + 1)
            e.tensor_tensor(sg_f[:], s_bf[tt % 2][:], gt3[tt % 2][:],
                            ALU.mult).then_inc(v_sf.h, 1)
            e.wait_ge(sx3[tt % 2].h, 16 * (tt // 2 + 1))
            e.wait_ge(v_sf.h, sfb3 + tt + 1)
            e.tensor_tensor(xt3[tt % 2][:], sg_f[:], xt3[tt % 2][:],
                            ALU.add).then_inc(v_ms.h, 2)
    blk.vector(body)
    v_ms.v += 1 + 3 * 16
    v_sf.v += 16

    def body(e):
        for tt in range(16):
            e.wait_ge(v_ms.h, P3V0 + 1 + 3 * tt + 3)
            e.dma_start(out=out_my[tt * 128:(tt + 1) * 128, :], in_=xt3[tt % 2][:]
                        ).then_inc(go[tt % 2].h, 16)
            go[tt % 2].v += 16
        e.wait_ge(go[0].h, go[0].v)
        e.wait_ge(go[1].h, go[1].v)
    blk.gpsimd(body)

    if debug_taps:
        tap_list = [("t_xn", xn_cq[0], [C, 512], BF16), ("t_gam", gam_s, [1, 4], F32),
                    ("t_w1q", w1qc, [RH, C], BF16), ("t_w3q", w3qc, [256, HID], BF16),
                    ("t_ytc", yt_c, [2048, 2048], BF16), ("t_ytm", yt_my, [2048, 2048], BF16),
                    ("t_g", g_sc, [MYT, C], BF16), ("t_xna", xn_aq[0], [NCORES * C, 512], BF16)]
        tap_out = {nm: nc.dram_tensor(nm, shp, dtp, kind="ExternalOutput")
                   for nm, _, shp, dtp in tap_list}
        def body(e):
            e.wait_ge(go[0].h, go[0].v)
            e.wait_ge(go[1].h, go[1].v)
            for nm, t, shp, dtp in tap_list:
                nrow = shp[0]
                step = max(1, 8192 * 2048 // max(1, int(np.prod(shp)))) if False else 512
                r = 0
                while r < nrow:
                    rr = min(step, nrow - r)
                    e.dma_start(out=tap_out[nm][r:r + rr, :], in_=t[r:r + rr, :]
                                ).then_inc(g_st.h, 16)
                    g_st.v += 16
                    r += rr
            e.wait_ge(g_st.h, g_st.v)
        blk.gpsimd(body)

    p3.close()
    ctx.close()
    return nc


def _make_mask():
    s = np.arange(128)[:, None]
    t = np.arange(512)[None, :]
    m = np.zeros((128, 2048), dtype=np.float32)
    for jj in range(4):
        m[:, jj * 512:(jj + 1) * 512] = (jj * 128 + s <= t).astype(np.float32)
    return m


def kernel(L0_attn_x, L0_attn_Wq, L0_attn_bq, L0_attn_Wk, L0_attn_bk, L0_attn_Wv, L0_attn_bv,
           L0_attn_Wo, L0_attn_bo, L0_ffn_w1, L0_ffn_w2, L0_ffn_w3):
    x = np.ascontiguousarray(np.asarray(L0_attn_x, dtype=np.float32))
    Wq = np.asarray(L0_attn_Wq, dtype=np.float32)
    Wk = np.asarray(L0_attn_Wk, dtype=np.float32)
    Wv = np.asarray(L0_attn_Wv, dtype=np.float32)
    bq = np.asarray(L0_attn_bq, dtype=np.float32)
    bk = np.asarray(L0_attn_bk, dtype=np.float32)
    bv = np.asarray(L0_attn_bv, dtype=np.float32)
    Wo = np.ascontiguousarray(np.asarray(L0_attn_Wo, dtype=np.float32))
    bo = np.asarray(L0_attn_bo, dtype=np.float32).reshape(1, C)
    w1 = np.asarray(L0_ffn_w1, dtype=np.float32)
    w2 = np.asarray(L0_ffn_w2, dtype=np.float32)
    w3 = np.asarray(L0_ffn_w3, dtype=np.float32)

    ident = np.eye(128, dtype=np.float32)
    ones = np.ones((128, 128), dtype=np.float32)
    mask = _make_mask()

    nc = build_nc()
    in_maps = []
    for c in range(NCORES):
        b = c // 4
        t0 = (c % 4) * MYT
        in_maps.append(dict(
            x_my=np.ascontiguousarray(x[b, t0:t0 + MYT, :]),
            wq_my=np.ascontiguousarray(Wq[2 * c:2 * c + 2]),
            wk_my=np.ascontiguousarray(Wk[2 * c:2 * c + 2]),
            wv_my=np.ascontiguousarray(Wv[2 * c:2 * c + 2]),
            bq_my=np.ascontiguousarray(bq[2 * c:2 * c + 2].T),
            bk_my=np.ascontiguousarray(bk[2 * c:2 * c + 2].T),
            bv_my=np.ascontiguousarray(bv[2 * c:2 * c + 2].T),
            wo_in=Wo,
            bo_in=bo,
            w1_my=np.ascontiguousarray(w1[RH * c:RH * (c + 1)]),
            w2_my=np.ascontiguousarray(w2[RH * c:RH * (c + 1)]),
            w3_my=np.ascontiguousarray(w3[256 * c:256 * (c + 1)]),
            ident_in=ident,
            ones_in=ones,
            mask_in=mask,
        ))
    res = run_bass_kernel_spmd(nc, in_maps, core_ids=list(range(NCORES)))
    out = np.empty((B, T, C), dtype=np.float32)
    for c in range(NCORES):
        b = c // 4
        t0 = (c % 4) * MYT
        out[b, t0:t0 + MYT, :] = np.asarray(res.results[c]["out_my"], dtype=np.float32)
    return out


# version 20
# speedup vs baseline: 1.0802x; 1.0047x over previous
import contextlib
import numpy as np
import ml_dtypes
import concourse.bass as bass
import concourse.mybir as mybir
from concourse.bass_utils import run_bass_kernel_spmd
from concourse.aot_env import allow_ici_all_to_all

F32 = mybir.dt.float32
BF16 = mybir.dt.bfloat16
AF = mybir.ActivationFunctionType
ALU = mybir.AluOpType
AX = mybir.AxisListType

NCORES = 8
B, T, C = 2, 8192, 2048
H, DH = 16, 128
HID, RH, PH = 5632, 704, 768
HPAD = NCORES * PH
MYT = 2048
NT = MYT // 128
SM_SCALE = float(DH) ** -0.5
RND_C = 1.5 * 2.0 ** 23
EPS_LN = 1e-5
EPS_Q = 1e-8
WCNT = float(HID * C)
WIN = 1024
NW = T // WIN
NTILE_W12 = 6


class Ctr:
    __slots__ = ("h", "v")
    def __init__(self, h):
        self.h = h
        self.v = 0

    def inc(self, ins, k=1):
        ins.then_inc(self.h, k)
        self.v += k
        return self.v


def w12_rows(j):
    r0 = j * 128
    return r0, min(RH, r0 + 128) - r0


def build_nc(debug_taps=False):
    nc = bass.Bass("TRN2", target_bir_lowering=False, detect_race_conditions=True)

    x_my = nc.dram_tensor("x_my", [MYT, C], F32, kind="ExternalInput")
    wq_my = nc.dram_tensor("wq_my", [2, C, DH], F32, kind="ExternalInput")
    wk_my = nc.dram_tensor("wk_my", [2, C, DH], F32, kind="ExternalInput")
    wv_my = nc.dram_tensor("wv_my", [2, C, DH], F32, kind="ExternalInput")
    bq_my = nc.dram_tensor("bq_my", [DH, 2], F32, kind="ExternalInput")
    bk_my = nc.dram_tensor("bk_my", [DH, 2], F32, kind="ExternalInput")
    bv_my = nc.dram_tensor("bv_my", [DH, 2], F32, kind="ExternalInput")
    wo_in = nc.dram_tensor("wo_in", [C, C], F32, kind="ExternalInput")
    bo_in = nc.dram_tensor("bo_in", [1, C], F32, kind="ExternalInput")
    w1_my = nc.dram_tensor("w1_my", [RH, C], F32, kind="ExternalInput")
    w2_my = nc.dram_tensor("w2_my", [RH, C], F32, kind="ExternalInput")
    w3_my = nc.dram_tensor("w3_my", [256, HID], F32, kind="ExternalInput")
    ident_in = nc.dram_tensor("ident_in", [128, 128], F32, kind="ExternalInput")
    ones_in = nc.dram_tensor("ones_in", [128, 128], F32, kind="ExternalInput")
    mask_in = nc.dram_tensor("mask_in", [128, 2048], F32, kind="ExternalInput")
    out_my = nc.dram_tensor("out_my", [MYT, C], F32, kind="ExternalOutput")

    xn_cq = [nc.dram_tensor(f"xn_cq{q}", [C, 512], BF16, kind="Internal") for q in range(4)]
    xn_aq = [nc.dram_tensor(f"xn_aq{q}", [NCORES * C, 512], BF16, kind="Internal") for q in range(4)]
    gam_c = nc.dram_tensor("gam_c", [1, 4], F32, kind="Internal")
    gam_s = nc.dram_tensor("gam_s", [1, 4], F32, kind="Internal")
    w1qc = nc.dram_tensor("w1qc", [RH, C], BF16, kind="Internal")
    w2qc = nc.dram_tensor("w2qc", [RH, C], BF16, kind="Internal")
    w3qc = nc.dram_tensor("w3qc", [256, HID], BF16, kind="Internal")
    w1q_all = nc.dram_tensor("w1q_all", [NCORES * RH, C], BF16, kind="Internal")
    w2q_all = nc.dram_tensor("w2q_all", [NCORES * RH, C], BF16, kind="Internal")
    w3q_all = nc.dram_tensor("w3q_all", [NCORES * 256, HID], BF16, kind="Internal")
    wo_bf = nc.dram_tensor("wo_bf", [C, C], BF16, kind="Internal")
    yt_c = nc.dram_tensor("yt_c", [2048, 2048], BF16, kind="Internal")
    yt_my = nc.dram_tensor("yt_my", [2048, 2048], BF16, kind="Internal")
    g_sc = nc.dram_tensor("g_sc", [MYT, C], BF16, kind="Internal")

    rg = [list(range(NCORES))]
    ctx = contextlib.ExitStack()

    def sem(name):
        return Ctr(ctx.enter_context(nc.semaphore(name)))

    def sb(name, shape, dt):
        return ctx.enter_context(nc.sbuf_tensor(name, shape, dt))

    ident_bf = sb("ident_bf", [128, 128], BF16)
    ident_f = sb("ident_f", [128, 128], F32)
    ones_f = sb("ones_f", [128, 128], F32)
    ones_bf = sb("ones_bf", [128, 1], BF16)
    stat = sb("stat", [128, 64], F32)
    grow = sb("grow", [1, 16], F32)
    rq_sb = sb("rq_sb", [128, 4], F32)
    negb = sb("negb", [128, 4], F32)
    bmax = sb("bmax", [1, 64], F32)
    biasT = sb("biasT", [128, 6], F32)

    psA = ctx.enter_context(nc.psum_tensor("psA", [128, 2048], F32))
    psB = ctx.enter_context(nc.psum_tensor("psB", [128, 2048], F32))

    s_ld = sem("s_ld")
    g_ld = sem("g_ld")
    s_tp = sem("s_tp")
    v_ms = sem("v_ms")
    a_ms = sem("a_ms")
    t_ms = sem("t_ms")
    g_st = sem("g_st")
    cc0 = sem("cc0"); cc2 = sem("cc2")
    ccq = [sem(f"ccq{q}") for q in range(4)]
    cc3a = sem("cc3a"); cc3b = sem("cc3b"); cc3c = sem("cc3c")
    g_c = sem("g_c")
    v_sf = sem("v_sf")
    a_sf = sem("a_sf")
    sx = [sem("sx0"), sem("sx1")]
    gxn = [sem("gxn0"), sem("gxn1")]
    gy = [sem("gy0"), sem("gy1")]
    gg = [sem("gg0"), sem("gg1")]
    sw3 = [sem(f"sw3{i}") for i in range(4)]
    g_yw = sem("g_yw")
    swp = [sem("swp0"), sem("swp1")]
    stw = [sem("stw0"), sem("stw1")]
    gq = [sem("gq0"), sem("gq1")]
    vq = sem("vq")
    vd = sem("vd")
    g_wl = sem("g_wl")
    swq = [sem("swq0"), sem("swq1")]
    gt3s = [sem("gt3s0"), sem("gt3s1")]
    sx3 = [sem("sx3a"), sem("sx3b")]
    go = [sem("go0"), sem("go1")]
    bar = sem("bar")

    blk = ctx.enter_context(nc.Block())
    barn = [0]

    def barrier():
        barn[0] += 1
        target = 5 * barn[0]
        caps = dict(sync=(s_ld, s_tp, sx[0], sx[1], sw3[0], sw3[1], sw3[2], sw3[3], sx3[0], sx3[1], swp[0], swp[1], stw[0], stw[1]),
                    gpsimd=(g_ld, g_st, g_c, gxn[0], gxn[1], gy[0], gy[1], gg[0], gg[1],
                            g_yw, gt3s[0], gt3s[1], go[0], go[1], gq[0], gq[1], g_wl),
                    tensor=(t_ms,), scalar=(a_ms, a_sf, swq[0], swq[1]), vector=(v_ms, v_sf, vq, vd))
        for name, dec in (("sync", blk.sync), ("gpsimd", blk.gpsimd), ("tensor", blk.tensor),
                          ("scalar", blk.scalar), ("vector", blk.vector)):
            def body(e, name=name):
                for s in caps[name]:
                    e.wait_ge(s.h, s.v)
                e.sem_inc(bar.h, 1)
                e.wait_ge(bar.h, target)
            dec(body)
        bar.v = target

    WQW = 2816
    WT2 = []
    for j in range(NTILE_W12):
        r0, nr = w12_rows(j)
        WT2.append((w1_my, r0, nr, 0, C, w1qc, r0, 0))
    for j in range(NTILE_W12):
        r0, nr = w12_rows(j)
        WT2.append((w2_my, r0, nr, 0, C, w2qc, r0, 1))
    for j in range(2):
        for hf in range(2):
            WT2.append((w3_my, j * 128, 128, hf * WQW, HID - hf * WQW if hf else WQW,
                        w3qc, j * 128, 2))
    NWT2 = len(WT2)

    pW = contextlib.ExitStack()
    def sbW(name, shape, dt):
        return pW.enter_context(nc.sbuf_tensor(name, shape, dt))
    wqa = sbW("wqa", [128, WQW], F32)
    qbf = [sbW("qbf0", [128, WQW], BF16), sbW("qbf1", [128, WQW], BF16)]
    wqkv = sbW("wqkv", [128, 6 * C], BF16)
    mask_bf = sbW("mask_bf", [128, 2048], BF16)

    pLN = contextlib.ExitStack()
    def sbL(name, shape, dt):
        return pLN.enter_context(nc.sbuf_tensor(name, shape, dt))
    xt = [sbL("xt0", [128, C], F32), sbL("xt1", [128, C], F32)]
    xsq = sbL("xsq", [128, C], F32)
    xnf = [sbL("xnf0", [128, C], F32), sbL("xnf1", [128, C], F32)]
    xn_st = [sbL("xnst0", [128, 16 * 512], BF16), sbL("xnst1", [128, 16 * 512], BF16)]

    def body(e):
        e.dma_start(out=ident_bf[:], in_=ident_in[:, :]).then_inc(g_c.h, 16)
        e.dma_start(out=ones_f[:], in_=ones_in[:, :]).then_inc(g_c.h, 16)
        e.dma_start(out=ones_bf[:], in_=ones_in[:, 0:1], allow_slow_non_contiguous=True).then_inc(g_c.h, 16)
        e.dma_start(out=biasT[:, 0:2], in_=bq_my[:, :]).then_inc(g_c.h, 16)
        e.dma_start(out=biasT[:, 2:4], in_=bk_my[:, :]).then_inc(g_c.h, 16)
        e.dma_start(out=biasT[:, 4:6], in_=bv_my[:, :]).then_inc(g_c.h, 16)
        e.dma_start(out=ident_f[:], in_=ident_in[:, :]).then_inc(g_c.h, 16)
        e.dma_start(out=mask_bf[:], in_=mask_in[:, :]).then_inc(g_ld.h, 16)
        g_ld.v += 16
        for idx, wsrc in enumerate((wq_my, wk_my, wv_my)):
            for hh in range(2):
                base = (idx * 2 + hh) * C
                for q in range(4):
                    e.dma_start(out=wqkv[:, base + q * 512:base + (q + 1) * 512],
                                in_=wsrc[hh, q * 512:(q + 1) * 512, :]
                                .rearrange("(a p) d -> p a d", p=128)
                                ).then_inc(g_ld.h, 16)
                    g_ld.v += 16
        wsrc0, r0, nr, c0, wd, _, _, _ = WT2[0]
        e.dma_start(out=wqa[0:nr, 0:wd], in_=wsrc0[r0:r0 + nr, c0:c0 + wd]
                    ).then_inc(g_wl.h, 16)
        g_wl.v += 16
    blk.gpsimd(body)
    g_c.v = 7 * 16
    GC_ALL = g_c.v
    GLD_QKV = g_ld.v

    VLN0 = v_ms.v
    ALN0 = a_ms.v
    TLN0 = t_ms.v
    GXN_B = (gxn[0].v, gxn[1].v)

    def body(e):
        for i in range(NT):
            if i >= 2:
                e.wait_ge(v_ms.h, VLN0 + 6 * (i - 2) + 4)
            e.dma_start(out=xt[i % 2][:], in_=x_my[i * 128:(i + 1) * 128, :]
                        ).then_inc(sx[i % 2].h, 16)
            sx[i % 2].v += 16
    blk.sync(body)

    def body(e):
        for i in range(NT):
            e.wait_ge(sx[i % 2].h, 16 * (i // 2 + 1))
            if i >= 1:
                e.wait_ge(v_ms.h, VLN0 + 6 * (i - 1) + 4)
            e.activation(xsq[:], xt[i % 2][:], AF.Square).then_inc(a_ms.h, 1)
            e.wait_ge(v_ms.h, VLN0 + 6 * i + 3)
            e.activation(stat[:, 56:57], stat[:, 16 + i:17 + i], AF.Ln).then_inc(a_ms.h, 1)
            e.wait_ge(a_ms.h, ALN0 + 5 * i + 2)
            e.activation(stat[:, 52 + i % 2:53 + i % 2], stat[:, 56:57], AF.Exp,
                         scale=-0.5).then_inc(a_ms.h, 1)
            di = i // 4
            psX = psA if i % 2 == 0 else psB
            for q in (0, 1):
                e.wait_ge(t_ms.h, TLN0 + 16 * i + 4 * (q + 1))
                if di >= 2 and i % 4 == 0 and q == 0:
                    e.wait_ge(gxn[di % 2].h, GXN_B[di % 2] + 64 * (di // 2))
                e.activation(xn_st[di % 2][:, (i % 4) * 2048 + q * 512:
                                           (i % 4) * 2048 + (q + 1) * 512],
                             psX[:, q * 512:(q + 1) * 512], AF.Identity,
                             bias=stat[:, 60:61]).then_inc(a_ms.h, 1)
    blk.scalar(body)
    a_ms.v += 5 * NT

    def body(e):
        sfb = v_sf.v
        e.memset(stat[:, 60:61], 0.0).then_inc(v_sf.h, 1)
        sfb += 1
        for i in range(NT):
            e.wait_ge(sx[i % 2].h, 16 * (i // 2 + 1))
            e.reduce_sum(stat[:, i:i + 1], xt[i % 2][:], axis=AX.X).then_inc(v_ms.h, 1)
            e.wait_ge(a_ms.h, ALN0 + 5 * i + 1)
            e.reduce_sum(stat[:, 16 + i:17 + i], xsq[:], axis=AX.X).then_inc(v_ms.h, 1)
            e.wait_ge(v_ms.h, VLN0 + 6 * i + 2)
            e.tensor_scalar(stat[:, i:i + 1], stat[:, i:i + 1], -1.0 / C, None,
                            ALU.mult).then_inc(v_sf.h, 1)
            e.wait_ge(v_sf.h, sfb + 4 * i + 1)
            e.tensor_tensor(stat[:, 54:55], stat[:, i:i + 1], stat[:, i:i + 1],
                            ALU.mult).then_inc(v_sf.h, 1)
            e.tensor_scalar(stat[:, 16 + i:17 + i], stat[:, 16 + i:17 + i], 1.0 / C, EPS_LN,
                            ALU.mult, ALU.add).then_inc(v_sf.h, 1)
            e.wait_ge(v_sf.h, sfb + 4 * i + 3)
            e.tensor_tensor(stat[:, 16 + i:17 + i], stat[:, 16 + i:17 + i], stat[:, 54:55],
                            ALU.subtract).then_inc(v_ms.h, 1)
            e.wait_ge(a_ms.h, ALN0 + 5 * i + 3)
            e.tensor_scalar(xsq[:], xt[i % 2][:], stat[:, i:i + 1], None,
                            ALU.add).then_inc(v_sf.h, 1)
            e.wait_ge(v_sf.h, sfb + 4 * i + 4)
            if i >= 2:
                e.wait_ge(t_ms.h, TLN0 + 16 * (i - 1))
            e.tensor_scalar(xnf[i % 2][:], xsq[:], stat[:, 52 + i % 2:53 + i % 2], None,
                            ALU.mult).then_inc(v_ms.h, 1)
            di = i // 4
            psX = psA if i % 2 == 0 else psB
            for q in (2, 3):
                e.wait_ge(t_ms.h, TLN0 + 16 * i + 4 * (q + 1))
                if di >= 2 and i % 4 == 0 and q == 2:
                    e.wait_ge(gxn[di % 2].h, GXN_B[di % 2] + 64 * (di // 2))
                e.tensor_copy(xn_st[di % 2][:, (i % 4) * 2048 + q * 512:
                                            (i % 4) * 2048 + (q + 1) * 512],
                              psX[:, q * 512:(q + 1) * 512]).then_inc(v_ms.h, 1)
    blk.vector(body)
    v_ms.v += 6 * NT
    v_sf.v += 4 * NT + 1

    def body(e):
        e.wait_ge(g_c.h, GC_ALL)
        for i in range(NT):
            e.wait_ge(v_ms.h, VLN0 + 6 * i + 4)
            psX = psA if i % 2 == 0 else psB
            for q in range(4):
                if i >= 2:
                    if q < 2:
                        e.wait_ge(a_ms.h, ALN0 + 5 * (i - 2) + 4 + q)
                    else:
                        e.wait_ge(v_ms.h, VLN0 + 6 * (i - 2) + 3 + q)
                for j in range(4):
                    cc = 4 * q + j
                    e.transpose(psX[:, q * 512 + j * 128:q * 512 + (j + 1) * 128],
                                xnf[i % 2][:, cc * 128:(cc + 1) * 128],
                                ident_f[:]).then_inc(t_ms.h, 1)
    blk.tensor(body)
    t_ms.v += 16 * NT

    def body(e):
        for di in range(NT // 4):
            xo = xn_cq[di].ap().rearrange("(a p) t -> p a t", p=128)
            for f in range(4):
                e.wait_ge(a_ms.h, ALN0 + 5 * (4 * di + f + 1))
                e.wait_ge(v_ms.h, VLN0 + 6 * (4 * di + f + 1))
                e.dma_start(out=xo[:, :, f * 128:(f + 1) * 128],
                            in_=xn_st[di % 2][:, f * 2048:(f + 1) * 2048]
                            .rearrange("p (a t) -> p a t", a=16)
                            ).then_inc(gxn[di % 2].h, 16)
                gxn[di % 2].v += 16
            e.wait_ge(gxn[di % 2].h, gxn[di % 2].v)
            e.collective_compute("AllGather", ALU.bypass, replica_groups=rg,
                                 ins=[xn_cq[di].ap().opt()],
                                 outs=[xn_aq[di].ap().opt()]).then_inc(ccq[di].h)
            ccq[di].v = 1
    blk.gpsimd(body)

    barrier()
    pLN.close()

    p1 = contextlib.ExitStack()
    def sb1(name, shape, dt):
        return p1.enter_context(nc.sbuf_tensor(name, shape, dt))
    xnT = [sb1("xnTa", [128, 16 * 512], BF16), sb1("xnTb", [128, 16 * 512], BF16)]
    qT2 = sb1("qT2", [128, 2 * T], BF16)
    kT2 = sb1("kT2", [128, 2 * T], BF16)
    v2 = sb1("v2", [128, 2 * T], BF16)
    vTs = sb1("vTs", [128, 512], F32)
    pT = [sb1(f"pT{i}", [128, 512], BF16) for i in range(6)]
    acc2 = [sb1("acc0", [128, 512], F32), sb1("acc1", [128, 512], F32)]
    accT_bf = sb1("accT", [128, 512], BF16)
    rden_sb = sb1("rden_sb", [1, 512], F32)
    dtmp = sb1("dtmp", [1, 1024], F32)
    bc_sb = sb1("bc_sb", [128, 512], F32)
    yn_bf = [sb1(f"yn{i}", [128, 512], BF16) for i in range(2)]

    VT_B0 = v_ms.v
    def body(e):
        for k, (wsrc, r0, nr, c0, wd, dst, dr, wid) in enumerate(WT2):
            if k == 0:
                continue
            e.wait_ge(v_ms.h, VT_B0 + 9 * k)
            e.dma_start(out=wqa[0:nr, 0:wd], in_=wsrc[r0:r0 + nr, c0:c0 + wd]
                        ).then_inc(g_wl.h, 16)
            g_wl.v += 16
    blk.gpsimd(body)

    def body(e):
        e.dma_start(out=wo_bf[:, :], in_=wo_in[:, :]).then_inc(g_st.h, 16)
    blk.gpsimd(body)
    g_st.v += 16
    WO_CAST_G = g_st.v
    GLDQ0 = g_wl.v
    GQ_BASE = (gq[0].v, gq[1].v)
    V_SF_FOLD = None
    V_FOLD = None
    QSH = ({qb: qb - 6 for qb in range(6, 16)}, {qb: 10 + qb for qb in range(6)})
    RSH = ({qb: qb - 5 for qb in range(5, 16)}, {qb: 11 + qb for qb in range(5)})

    for b in range(B):
        PWIN = 512
        NPW = T // PWIN
        projT0 = t_ms.v
        projA0 = a_ms.v
        vtV0 = v_ms.v
        STP0 = s_tp.v

        STW_BASE = (stw[0].v, stw[1].v)
        worder = ([r * 4 + q for q in range(4) for r in range(4)] if b == 0
                  else list(range(NPW)))
        def body(e, b=b, projT0=projT0, worder=worder):
            for p in range(NPW):
                w = worder[p]
                if p <= 1:
                    if b == 1:
                        e.wait_ge(t_ms.h, projT0)
                else:
                    e.wait_ge(t_ms.h, projT0 + (p - 2) * 14 + 6)
                if b == 0 and p % 4 == 0:
                    e.wait_ge(ccq[p // 4].h, 1)
                gtok = b * T + w * PWIN
                r = gtok // MYT
                cq = (gtok % MYT) // PWIN
                for cchunk in range(16):
                    src = xn_aq[cq][r * C + cchunk * 128: r * C + (cchunk + 1) * 128, :]
                    e.dma_start(out=xnT[p % 2][:, cchunk * PWIN:(cchunk + 1) * PWIN], in_=src
                                ).then_inc(stw[p % 2].h, 16)
                stw[p % 2].v += 256
        blk.sync(body)

        def body(e, b=b, projT0=projT0, projA0=projA0, vtV0=vtV0, STW_BASE=STW_BASE,
                 worder=worder):
            if b == 0:
                e.wait_ge(g_ld.h, GLD_QKV)
            e.wait_ge(a_ms.h, projA0)
            e.wait_ge(v_ms.h, vtV0)
            for p in range(NPW):
                w = worder[p]
                e.wait_ge(stw[p % 2].h, STW_BASE[p % 2] + 256 * (p // 2 + 1))
                for rnd in range(6):
                    gr = p * 6 + rnd
                    qtr = rnd % 4
                    ps = psA[:, qtr * 512:(qtr + 1) * 512]
                    if gr >= 4:
                        e.wait_ge(a_ms.h, projA0 + gr - 3)
                    kind, hh = divmod(rnd, 2)
                    last = None
                    for cchunk in range(16):
                        lhsT = wqkv[:, (kind * 2 + hh) * C + cchunk * 128:
                                    (kind * 2 + hh) * C + (cchunk + 1) * 128]
                        last = e.matmul(ps, lhsT,
                                 xnT[p % 2][:, cchunk * PWIN:(cchunk + 1) * PWIN],
                                 start=(cchunk == 0), stop=(cchunk == 15))
                    last.then_inc(t_ms.h, 1)
                for hh in range(2):
                    e.wait_ge(a_ms.h, projA0 + p * 6 + 4 + hh + 1)
                    for j in range(4):
                        tki = p * 8 + hh * 4 + j
                        if tki >= 2:
                            tp = tki - 2
                            idx = 9 * (tp // 8) + tp % 8 + 1 if b == 0 else tp + 1
                            e.wait_ge(v_ms.h, vtV0 + idx)
                        psT = psB[:, 0:128] if tki % 2 == 0 else psB[:, 512:640]
                        e.transpose(psT, vTs[:, j * 128:(j + 1) * 128],
                                    ident_f[:]).then_inc(t_ms.h, 1)
        blk.tensor(body)
        t_ms.v += NPW * 14

        def body(e, b=b, projT0=projT0, projA0=projA0, worder=worder):
            for p in range(NPW):
                w = worder[p]
                for rnd in range(6):
                    kind, hh = divmod(rnd, 2)
                    e.wait_ge(t_ms.h, projT0 + p * 14 + rnd + 1)
                    if kind == 2 and hh == 1:
                        e.wait_ge(t_ms.h, projT0 + p * 14 + 6 + 4)
                    bcol = biasT[:, kind * 2 + hh:kind * 2 + hh + 1]
                    if kind == 0:
                        dst = qT2[:, hh * T + w * PWIN: hh * T + (w + 1) * PWIN]
                    elif kind == 1:
                        dst = kT2[:, hh * T + w * PWIN: hh * T + (w + 1) * PWIN]
                    else:
                        dst = vTs[:, 0:PWIN]
                    qtr = rnd % 4
                    ps = psA[:, qtr * 512:(qtr + 1) * 512]
                    e.activation(dst, ps, AF.Identity, bias=bcol).then_inc(a_ms.h, 1)
        blk.scalar(body)
        a_ms.v += NPW * 6

        def body(e, b=b, projT0=projT0, worder=worder):
            for p in range(NPW):
                w = worder[p]
                for hh in range(2):
                    for j in range(4):
                        tki = p * 8 + hh * 4 + j
                        psT = psB[:, 0:128] if tki % 2 == 0 else psB[:, 512:640]
                        e.wait_ge(t_ms.h, projT0 + p * 14 + 6 + hh * 4 + j + 1)
                        kb = w * 4 + j
                        e.tensor_copy(v2[:, hh * T + kb * 128:hh * T + (kb + 1) * 128],
                                      psT).then_inc(v_ms.h, 1)
                if b == 0:
                    k = p
                    wsrc, r0, nr, c0, wd, dst, dr, wid = WT2[k]
                    e.wait_ge(g_wl.h, (k + 1) * 16)
                    e.reduce_sum(stat[0:nr, 32 + k:33 + k], wqa[0:nr, 0:wd], axis=AX.X,
                                 apply_absolute_value=True).then_inc(v_ms.h, 1)
                    if nr < 128:
                        e.memset(stat[64:128, 32 + k:33 + k], 0.0)
            if b == 0:
                e.wait_ge(v_ms.h, vtV0 + NPW * 9)
                e.reduce_sum(stat[:, 48:49], stat[:, 32:38], axis=AX.X).then_inc(v_sf.h, 1)
                e.reduce_sum(stat[:, 49:50], stat[:, 38:44], axis=AX.X).then_inc(v_sf.h, 1)
                e.memset(stat[:, 51:52], 0.0).then_inc(v_sf.h, 1)
                e.reduce_sum(stat[:, 50:51], stat[:, 44:48], axis=AX.X).then_inc(v_ms.h, 1)
        blk.vector(body)
        if b == 0:
            assert vtV0 == VT_B0
            v_ms.v += NPW * 9 + 1
            v_sf.v += 3
            V_SF_FOLD = v_sf.v
            V_FOLD = v_ms.v
        else:
            v_ms.v += NPW * 8
        if b == 0:
            VTE_INS = v_ms.v
            def body(e, VTE_INS=VTE_INS):
                e.wait_ge(v_ms.h, V_FOLD)
                e.wait_ge(v_sf.h, V_SF_FOLD)
                e.wait_ge(v_ms.h, VTE_INS)
                e.matmul(psB[0:1, 0:4], ones_f[:, 0:1], stat[:, 48:52], start=True,
                         stop=True).then_inc(t_ms.h, 1)
            blk.tensor(body)
            t_ms.v += 1
            T_GAM = t_ms.v
            def body(e, T_GAM=T_GAM):
                e.wait_ge(t_ms.h, T_GAM)
                e.tensor_copy(grow[0:1, 0:4], psB[0:1, 0:4]).then_inc(v_ms.h, 1)
            blk.vector(body)
            v_ms.v += 1
            V_GROW = v_ms.v
            def body(e, V_GROW=V_GROW):
                e.wait_ge(v_ms.h, V_GROW)
                e.dma_start(out=gam_c[:, :], in_=grow[0:1, 0:4]).then_inc(g_st.h, 16)
                e.wait_ge(g_st.h, g_st.v + 16)
                e.collective_compute("AllReduce", ALU.add, replica_groups=rg,
                                     ins=[gam_c.ap().opt()],
                                     outs=[gam_s.ap().opt()]).then_inc(cc0.h)
                e.wait_ge(cc0.h, 1)
                e.dma_start(out=grow[0:1, 8:12], in_=gam_s[:, :]).then_inc(g_ld.h, 16)
            blk.gpsimd(body)
            g_st.v += 16
            g_ld.v += 16
            GLD_GAMS = g_ld.v
            cc0.v = 1
            def body(e, GLD_GAMS=GLD_GAMS):
                e.wait_ge(g_ld.h, GLD_GAMS)
                e.tensor_scalar(grow[0:1, 12:16], grow[0:1, 8:12], 1.0 / WCNT, EPS_Q,
                                ALU.mult, ALU.add).then_inc(v_sf.h, 1)
                e.wait_ge(v_sf.h, v_sf.v + 1)
                e.reciprocal(grow[0:1, 4:8], grow[0:1, 12:16]).then_inc(v_ms.h, 1)
            blk.vector(body)
            v_ms.v += 1
            v_sf.v += 1
            V_RECIP = v_ms.v
            def body(e, V_RECIP=V_RECIP):
                e.wait_ge(v_ms.h, V_RECIP)
                e.matmul(psB[:, 8:12], ones_f[0:1, :], grow[0:1, 4:8], start=True,
                         stop=True).then_inc(t_ms.h, 1)
            blk.tensor(body)
            t_ms.v += 1
            T_RQ = t_ms.v
            def body(e, T_RQ=T_RQ):
                e.wait_ge(t_ms.h, T_RQ)
                e.tensor_copy(rq_sb[:], psB[:, 8:12]).then_inc(v_ms.h, 1)
            blk.vector(body)
            v_ms.v += 1
            V_RQ = v_ms.v

        PROJ_A_END = a_ms.v
        VT_V_END = v_ms.v

        for hh in range(2):
            pair = b * 2 + hh
            TKD = t_ms.v
            def body(e, hh=hh, TKD=TKD):
                e.wait_ge(a_ms.h, PROJ_A_END)
                e.wait_ge(t_ms.h, TKD)
                e.tensor_tensor(xnT[0][:, 0:T], qT2[:, hh * T:(hh + 1) * T],
                                qT2[:, hh * T:(hh + 1) * T], ALU.mult).then_inc(v_ms.h, 1)
            blk.vector(body)
            v_ms.v += 1
            QSQ_V = v_ms.v

            BQ_T0 = t_ms.v
            def body(e, QSQ_V=QSQ_V):
                e.wait_ge(v_ms.h, QSQ_V)
                for ch in range(16):
                    if ch >= 2:
                        e.wait_ge(v_ms.h, QSQ_V + ch - 1)
                    ps = psB[0:1, 1536:2048] if ch % 2 == 0 else psB[0:1, 1024:1536]
                    e.matmul(ps, ones_bf[:, 0:1], xnT[0][:, ch * 512:(ch + 1) * 512],
                             start=True, stop=True).then_inc(t_ms.h, 1)
            blk.tensor(body)
            t_ms.v += 16

            def body(e, BQ_T0=BQ_T0, pair=pair, QSQ_V=QSQ_V):
                for ch in range(16):
                    ps = psB[0:1, 1536:2048] if ch % 2 == 0 else psB[0:1, 1024:1536]
                    e.wait_ge(t_ms.h, BQ_T0 + ch + 1)
                    e.reduce_max(bmax[0:1, ch:ch + 1], ps, axis=AX.X).then_inc(v_ms.h, 1)
                e.wait_ge(v_ms.h, QSQ_V + 16)
                e.reduce_max(bmax[0:1, 32 + pair:33 + pair], bmax[0:1, 0:16],
                             axis=AX.X).then_inc(v_ms.h, 1)
            blk.vector(body)
            v_ms.v += 17

            QD = t_ms.v
            def body(e, hh=hh, QD=QD):
                e.wait_ge(t_ms.h, QD)
                e.tensor_tensor(xnT[0][:, 0:T], kT2[:, hh * T:(hh + 1) * T],
                                kT2[:, hh * T:(hh + 1) * T], ALU.mult).then_inc(v_ms.h, 1)
            blk.vector(body)
            v_ms.v += 1
            KSQ_V = v_ms.v

            BK_T0 = t_ms.v
            def body(e, KSQ_V=KSQ_V):
                e.wait_ge(v_ms.h, KSQ_V)
                for ch in range(16):
                    if ch >= 2:
                        e.wait_ge(v_ms.h, KSQ_V + ch - 1)
                    ps = psB[0:1, 1536:2048] if ch % 2 == 0 else psB[0:1, 1024:1536]
                    e.matmul(ps, ones_bf[:, 0:1], xnT[0][:, ch * 512:(ch + 1) * 512],
                             start=True, stop=True).then_inc(t_ms.h, 1)
            blk.tensor(body)
            t_ms.v += 16

            def body(e, BK_T0=BK_T0, pair=pair, KSQ_V=KSQ_V):
                for ch in range(16):
                    ps = psB[0:1, 1536:2048] if ch % 2 == 0 else psB[0:1, 1024:1536]
                    e.wait_ge(t_ms.h, BK_T0 + ch + 1)
                    e.reduce_max(bmax[0:1, ch:ch + 1], ps, axis=AX.X).then_inc(v_ms.h, 1)
                e.wait_ge(v_ms.h, KSQ_V + 16)
                e.reduce_max(bmax[0:1, 40 + pair:41 + pair], bmax[0:1, 0:16],
                             axis=AX.X).then_inc(v_sf.h, 1)
                e.wait_ge(v_sf.h, v_sf.v + 1)
                e.tensor_tensor(bmax[0:1, 48 + pair:49 + pair], bmax[0:1, 32 + pair:33 + pair],
                                bmax[0:1, 40 + pair:41 + pair], ALU.mult).then_inc(v_ms.h, 1)
            blk.vector(body)
            v_ms.v += 17
            v_sf.v += 1
            PRD_V = v_ms.v

            def body(e, pair=pair, PRD_V=PRD_V):
                e.wait_ge(v_ms.h, PRD_V)
                e.activation(bmax[0:1, 56 + pair:57 + pair], bmax[0:1, 48 + pair:49 + pair],
                             AF.Ln).then_inc(a_sf.h, 1)
                e.wait_ge(a_sf.h, a_sf.v + 1)
                e.activation(bmax[0:1, 48 + pair:49 + pair], bmax[0:1, 56 + pair:57 + pair],
                             AF.Exp, scale=0.5).then_inc(a_ms.h, 1)
            blk.scalar(body)
            a_ms.v += 1
            a_sf.v += 1
            SQ_A = a_ms.v

            def body(e, pair=pair, SQ_A=SQ_A):
                e.wait_ge(a_ms.h, SQ_A)
                e.tensor_scalar(bmax[0:1, 56 + pair:57 + pair], bmax[0:1, 48 + pair:49 + pair],
                                -SM_SCALE, None, ALU.mult).then_inc(v_ms.h, 1)
            blk.vector(body)
            v_ms.v += 1
            NB_V = v_ms.v

            def body(e, NB_V=NB_V, pair=pair):
                e.wait_ge(v_ms.h, NB_V)
                e.matmul(psB[:, 1024:1025], ones_f[0:1, :], bmax[0:1, 56 + pair:57 + pair],
                         start=True, stop=True).then_inc(t_ms.h, 1)
            blk.tensor(body)
            t_ms.v += 1
            NB_T = t_ms.v

            def body(e, NB_T=NB_T, pair=pair):
                e.wait_ge(t_ms.h, NB_T)
                e.tensor_copy(negb[:, pair:pair + 1], psB[:, 1024:1025]).then_inc(v_ms.h, 1)
            blk.vector(body)
            v_ms.v += 1
            NEGB_V = v_ms.v

            qk_t0 = t_ms.v
            ae_a0 = a_ms.v
            acc_v0 = v_ms.v
            vd0 = vd.v
            NKBS = [4 * (q + 1) for q in range(16)]
            klist = [(qb, j) for qb in range(16) for j in range(NKBS[qb])]
            CUM = []
            _c = 0
            for q in range(16):
                _c += NKBS[q]
                CUM.append(_c)
            NKI = len(klist)
            DLAG = 3
            NSL = 5
            NPT = 6
            def vb(qb):
                return acc_v0 + 6 * qb
            def mask_inc(ki):
                qb, j = klist[ki]
                return vb(qb) + (j - (NKBS[qb] - 4)) + 1
            def is_diag(ki):
                qb, j = klist[ki]
                return j >= NKBS[qb] - 4
            def acci(ki, vd0=vd0, klist=klist):
                return vd0 + ki + klist[ki][0] + 1
            def cmbi(qb, vd0=vd0, CUM=CUM):
                return vd0 + CUM[qb] + qb + 1
            sched = []
            for ki in range(NKI + DLAG):
                if ki < NKI:
                    sched.append(("qk", ki))
                kj = ki - DLAG
                if kj >= 0:
                    sched.append(("pv", kj))
                    qb, j = klist[kj]
                    if j == NKBS[qb] - 1:
                        sched.append(("dn", qb))
                        sched.append(("bc", qb))
            qkI, pvI, dnI, bcI = {}, {}, {}, {}
            for idx, (kind, a) in enumerate(sched):
                {"qk": qkI, "pv": pvI, "dn": dnI, "bc": bcI}[kind][a] = qk_t0 + idx + 1
            def st_slot(ki):
                s = ki % NSL
                return psA[:, s * 512:(s + 1) * 512] if s < 4 else psB[:, 1024:1536]

            def body(e, hh=hh, pair=pair, NEGB_V=NEGB_V, VT_V_END=VT_V_END,
                     vb=vb, mask_inc=mask_inc, st_slot=st_slot, ae_a0=ae_a0,
                     sched=sched, cmbi=cmbi):
                e.wait_ge(v_ms.h, NEGB_V)
                e.wait_ge(v_ms.h, VT_V_END)
                for kind, a in sched:
                    if kind == "qk":
                        ki = a
                        qb, j = klist[ki]
                        if ki >= NSL:
                            e.wait_ge(a_ms.h, ae_a0 + ki - NSL + 1)
                        e.matmul(st_slot(ki),
                                 kT2[:, hh * T + j * 128:hh * T + (j + 1) * 128],
                                 qT2[:, hh * T + qb * 512:hh * T + (qb + 1) * 512],
                                 start=True, stop=True).then_inc(t_ms.h, 1)
                    elif kind == "pv":
                        ki = a
                        qb, j = klist[ki]
                        nkb = NKBS[qb]
                        e.wait_ge(a_ms.h, ae_a0 + ki + 1)
                        if j >= nkb - 4:
                            e.wait_ge(v_ms.h, mask_inc(ki))
                        if j == 0 and qb >= 1:
                            e.wait_ge(v_ms.h, vb(qb))
                        buf = pT[ki % NPT]
                        e.matmul(psB[:, 0:512],
                                 v2[:, hh * T + j * 128:hh * T + (j + 1) * 128],
                                 buf[:], start=(j == 0), stop=(j == nkb - 1)
                                 ).then_inc(t_ms.h, 1)
                    elif kind == "dn":
                        qb = a
                        e.wait_ge(vd.h, cmbi(qb))
                        e.matmul(psB[0:1, 512:1024], ones_bf[:, 0:1], accT_bf[:],
                                 start=True, stop=True).then_inc(t_ms.h, 1)
                    else:
                        qb = a
                        e.wait_ge(v_ms.h, vb(qb) + 5)
                        e.matmul(psB[:, 1536:2048], ones_f[0:1, :], rden_sb[0:1, :],
                                 start=True, stop=True).then_inc(t_ms.h, 1)
            blk.tensor(body)
            t_ms.v = qk_t0 + len(sched)

            def body(e, pair=pair, qkI=qkI, pvI=pvI, mask_inc=mask_inc,
                     is_diag=is_diag, st_slot=st_slot, ae_a0=ae_a0, acci=acci):
                for ki in range(NKI):
                    e.wait_ge(t_ms.h, qkI[ki])
                    if ki >= NPT:
                        e.wait_ge(t_ms.h, pvI[ki - NPT])
                        e.wait_ge(vd.h, acci(ki - NPT))
                        if is_diag(ki - NPT):
                            e.wait_ge(v_ms.h, mask_inc(ki - NPT))
                    e.activation(pT[ki % NPT][:], st_slot(ki), AF.Exp,
                                 bias=negb[:, pair:pair + 1],
                                 scale=SM_SCALE).then_inc(a_ms.h, 1)
            blk.scalar(body)
            a_ms.v = ae_a0 + NKI

            gy_base = (gy[0].v, gy[1].v)
            VSF_ATT = v_sf.v
            def body(e, b=b, hh=hh, pair=pair, ae_a0=ae_a0, vb=vb, dnI=dnI, bcI=bcI,
                     gy_base=gy_base, VSF_ATT=VSF_ATT, acci=acci,
                     cmbi=cmbi):
                ki = 0
                for qb in range(16):
                    nkb = NKBS[qb]
                    for j in range(nkb):
                        e.wait_ge(a_ms.h, ae_a0 + ki + 1)
                        if j >= nkb - 4:
                            dj = j - (nkb - 4)
                            e.tensor_tensor(pT[ki % NPT][:], pT[ki % NPT][:],
                                            mask_bf[:, dj * 512:(dj + 1) * 512],
                                            ALU.mult).then_inc(v_ms.h, 1)
                        if j < 2:
                            if qb >= 1:
                                e.wait_ge(vd.h, cmbi(qb - 1))
                            e.tensor_copy(acc2[j][:], pT[ki % NPT][:]).then_inc(vd.h, 1)
                        else:
                            e.wait_ge(vd.h, acci(ki - 2))
                            e.tensor_tensor(acc2[j % 2][:], acc2[j % 2][:],
                                            pT[ki % NPT][:], ALU.add).then_inc(vd.h, 1)
                        ki += 1
                    e.wait_ge(vd.h, acci(ki - 1))
                    e.tensor_tensor(accT_bf[:], acc2[0][:], acc2[1][:],
                                    ALU.add).then_inc(vd.h, 1)
                    e.wait_ge(t_ms.h, dnI[qb])
                    e.tensor_copy(dtmp[0:1, 0:512], psB[0:1, 512:1024]).then_inc(v_sf.h, 1)
                    e.wait_ge(v_sf.h, VSF_ATT + 2 * qb + 1)
                    e.reciprocal(rden_sb[0:1, :], dtmp[0:1, 0:512]).then_inc(v_ms.h, 1)
                    e.wait_ge(t_ms.h, bcI[qb])
                    e.tensor_copy(bc_sb[:], psB[:, 1536:2048]).then_inc(v_sf.h, 1)
                    e.wait_ge(gy[qb % 2].h, gy_base[qb % 2] + 16 * (qb // 2))
                    e.wait_ge(v_sf.h, VSF_ATT + 2 * qb + 2)
                    e.tensor_tensor(yn_bf[qb % 2][:], psB[:, 0:512], bc_sb[:],
                                    ALU.mult).then_inc(v_ms.h, 1)
                    k = QSH[hh].get(qb) if b == 0 else None
                    if k is not None:
                        wsrc, r0, nr, c0, wd, dst, dr, wid = WT2[k]
                        e.wait_ge(g_wl.h, GLDQ0 + (k + 1) * 16)
                        if k == 0:
                            e.wait_ge(v_ms.h, V_RQ)
                        col = rq_sb[0:nr, wid:wid + 1]
                        e.tensor_scalar(wqa[0:nr, 0:wd], wqa[0:nr, 0:wd], col, RND_C,
                                        ALU.mult, ALU.add).then_inc(vq.h, 1)
                        e.wait_ge(vq.h, 3 * k + 1)
                        e.tensor_scalar(wqa[0:nr, 0:wd], wqa[0:nr, 0:wd], -RND_C, 1.0,
                                        ALU.add, ALU.min).then_inc(vq.h, 1)
                        e.wait_ge(vq.h, 3 * k + 2)
                        if k >= 2:
                            e.wait_ge(gq[k % 2].h, GQ_BASE[k % 2] + 16 * (k // 2))
                        e.tensor_scalar(qbf[k % 2][0:nr, 0:wd], wqa[0:nr, 0:wd], -1.0, None,
                                        ALU.max).then_inc(vq.h, 1)
            blk.vector(body)
            v_ms.v = vb(16)
            v_sf.v += 32
            vd.v = cmbi(15)
            if b == 0:
                vq.v += 3 * len(QSH[hh])

            def body(e, b=b, hh=hh, vb=vb):
                for qb in range(16):
                    e.wait_ge(v_ms.h, vb(qb) + 6)
                    jdst = 4 * b + qb // 4
                    row = 256 * jdst + 128 * hh
                    col = (qb % 4) * 512
                    e.dma_start(out=yt_c[row:row + 128, col:col + 512],
                                in_=yn_bf[qb % 2][:]).then_inc(gy[qb % 2].h, 16)
                    gy[qb % 2].v += 16
                    rk = RSH[hh].get(qb) if b == 0 else None
                    if rk is not None:
                        wsrc, r0, nr, c0, wd, dst, dr, wid = WT2[rk]
                        if rk == 0:
                            e.wait_ge(v_ms.h, V_FOLD)
                        else:
                            e.wait_ge(vq.h, 3 * (rk - 1) + 3)
                        e.dma_start(out=wqa[0:nr, 0:wd], in_=wsrc[r0:r0 + nr, c0:c0 + wd]
                                    ).then_inc(g_wl.h, 16)
                        g_wl.v += 16
                        if rk >= 1:
                            ws2, r02, nr2, c02, wd2, dst2, dr2, wid2 = WT2[rk - 1]
                            e.dma_start(out=dst2[dr2:dr2 + nr2, c02:c02 + wd2],
                                        in_=qbf[(rk - 1) % 2][0:nr2, 0:wd2]
                                        ).then_inc(gq[(rk - 1) % 2].h, 16)
                            gq[(rk - 1) % 2].v += 16
                        if rk == 6:
                            e.wait_ge(gq[0].h, GQ_BASE[0] + 48)
                            e.wait_ge(gq[1].h, GQ_BASE[1] + 48)
                            e.collective_compute("AllGather", ALU.bypass, replica_groups=rg,
                                                 ins=[w1qc.ap().opt()],
                                                 outs=[w1q_all.ap().opt()]).then_inc(cc3a.h)
                        if rk == 12:
                            e.wait_ge(gq[0].h, GQ_BASE[0] + 96)
                            e.wait_ge(gq[1].h, GQ_BASE[1] + 96)
                            e.collective_compute("AllGather", ALU.bypass, replica_groups=rg,
                                                 ins=[w2qc.ap().opt()],
                                                 outs=[w2q_all.ap().opt()]).then_inc(cc3b.h)
            blk.gpsimd(body)

        if b == 0 and hh == 1:
            def body(e):
                k = NWT2 - 1
                (ws2, r02, nr2, c02, wd2, dst2, dr2, wid2) = WT2[k]
                e.wait_ge(vq.h, 3 * k + 3)
                e.dma_start(out=dst2[dr2:dr2 + nr2, c02:c02 + wd2],
                            in_=qbf[k % 2][0:nr2, 0:wd2]).then_inc(gq[k % 2].h, 16)
                gq[k % 2].v += 16
                e.wait_ge(gq[0].h, gq[0].v)
                e.wait_ge(gq[1].h, gq[1].v)
                e.collective_compute("AllGather", ALU.bypass, replica_groups=rg,
                                     ins=[w3qc.ap().opt()], outs=[w3q_all.ap().opt()]).then_inc(cc3c.h)
            blk.gpsimd(body)
            cc3a.v = cc3b.v = cc3c.v = 1

    def body(e):
        e.wait_ge(gy[0].h, gy[0].v)
        e.wait_ge(gy[1].h, gy[1].v)
        with allow_ici_all_to_all():
            e.collective_compute("AllToAll", ALU.bypass, replica_groups=rg,
                                 ins=[yt_c.ap().opt()], outs=[yt_my.ap().opt()]).then_inc(cc2.h)
    blk.gpsimd(body)
    cc2.v = 1

    barrier()
    p1.close()
    pW.close()

    p2 = contextlib.ExitStack()
    def sb2(name, shape, dt):
        return p2.enter_context(nc.sbuf_tensor(name, shape, dt))
    NHT = HID // 128
    NHQ = NHT // 4
    hidT = sb2("hidT", [128, NHT * WIN], BF16)
    xnT2 = sb2("xnT2", [128, 16 * WIN], BF16)
    wp1 = [sb2("wp1a", [128, 16 * 512], BF16), sb2("wp1b", [128, 16 * 512], BF16)]
    wp2 = [sb2("wp2a", [128, 16 * 512], BF16), sb2("wp2b", [128, 16 * 512], BF16)]
    w3t = [sb2(f"w3t{i}", [128, 512], BF16) for i in range(4)]
    sgm = sb2("sgm", [128, WIN], F32)
    gbf = [sb2(f"gbf{i}", [128, 512], BF16) for i in range(2)]

    W3_T_PREV = [0]
    X1_PREV = [0]
    for half in range(2):
        STP_H = s_tp.v
        X1P_H = X1_PREV[0]
        X1T0 = t_ms.v
        HIDV0 = v_ms.v
        SGA0 = a_ms.v

        SWP_BASE = (swp[0].v, swp[1].v)
        SWQ_BASE = (swq[0].v, swq[1].v)
        def body(e, half=half, X1T0=X1T0, X1P=X1P_H):
            e.wait_ge(cc3a.h, 1)
            e.wait_ge(cc3b.h, 1)
            for hq in range(NHQ):
                if hq >= 2:
                    e.wait_ge(t_ms.h, X1T0 + (hq - 1) * 4)
                for cchunk in range(16):
                    e.dma_start(out=wp1[hq % 2][:, cchunk * 512:(cchunk + 1) * 512],
                                in_=w1q_all[hq * 512:(hq + 1) * 512,
                                            cchunk * 128:(cchunk + 1) * 128],
                                transpose=True).then_inc(swp[hq % 2].h, 16)
                    e.dma_start(out=wp2[hq % 2][:, cchunk * 512:(cchunk + 1) * 512],
                                in_=w2q_all[hq * 512:(hq + 1) * 512,
                                            cchunk * 128:(cchunk + 1) * 128],
                                transpose=True).then_inc(swp[hq % 2].h, 16)
                swp[hq % 2].v += 512
        blk.sync(body)

        def body(e, half=half, X1P=X1P_H):
            if half == 1:
                e.wait_ge(t_ms.h, X1P)
            for cchunk in range(16):
                for sub in range(2):
                    src = xn_cq[2 * half + sub][cchunk * 128:(cchunk + 1) * 128, :]
                    e.dma_start(out=xnT2[:, cchunk * WIN + sub * 512:
                                         cchunk * WIN + (sub + 1) * 512], in_=src
                                ).then_inc(swq[0].h, 16)
        blk.scalar(body)
        swq[0].v += 512

        def body(e, half=half, STP_H=STP_H, SWP_BASE=SWP_BASE, HIDV0=HIDV0,
                 W3P=W3_T_PREV[0]):
            e.wait_ge(swq[0].h, SWQ_BASE[0] + 512)
            if half == 1:
                e.wait_ge(v_ms.h, HIDV0)
            for hq in range(NHQ):
                e.wait_ge(swp[hq % 2].h, SWP_BASE[hq % 2] + 512 * (hq // 2 + 1))
                for ht in range(4):
                    hti = hq * 4 + ht
                    pspar = psA if hti % 2 == 0 else psB
                    x1ps = pspar[:, 0:1024]
                    x2ps = pspar[:, 1024:2048]
                    if hti >= 2:
                        e.wait_ge(v_ms.h, HIDV0 + hti - 1)
                    last = None
                    for cchunk in range(16):
                        lhs1 = wp1[hq % 2][:, cchunk * 512 + ht * 128: cchunk * 512 + ht * 128 + 128]
                        lhs2 = wp2[hq % 2][:, cchunk * 512 + ht * 128: cchunk * 512 + ht * 128 + 128]
                        for nn in range(2):
                            e.matmul(x1ps[:, nn * 512:(nn + 1) * 512], lhs1,
                                     xnT2[:, cchunk * WIN + nn * 512:cchunk * WIN + (nn + 1) * 512],
                                     start=(cchunk == 0), stop=(cchunk == 15))
                        for nn in range(2):
                            last = e.matmul(x2ps[:, nn * 512:(nn + 1) * 512], lhs2,
                                     xnT2[:, cchunk * WIN + nn * 512:cchunk * WIN + (nn + 1) * 512],
                                     start=(cchunk == 0), stop=(cchunk == 15))
                    last.then_inc(t_ms.h, 1)
        blk.tensor(body)
        t_ms.v += NHT
        X1_PREV[0] = t_ms.v

        def wp2load(e, hq):
            for cchunk in range(16):
                e.dma_start(out=wp2[hq % 2][:, cchunk * 512:(cchunk + 1) * 512],
                            in_=w2q_all[hq * 512:(hq + 1) * 512,
                                        cchunk * 128:(cchunk + 1) * 128],
                            transpose=True).then_inc(swq[hq % 2].h, 16)
            swq[hq % 2].v += 256

        def body(e, half=half, X1T0=X1T0, HIDV0=HIDV0, wp2load=wp2load,
                 X1P=X1P_H):
            for hti in range(NHT):
                pspar = psA if hti % 2 == 0 else psB
                e.wait_ge(t_ms.h, X1T0 + hti + 1)
                if hti >= 1:
                    e.wait_ge(v_ms.h, HIDV0 + hti)
                e.activation(sgm[:], pspar[:, 0:1024], AF.Sigmoid).then_inc(a_ms.h, 1)
        blk.scalar(body)
        a_ms.v += NHT

        def body(e, SGA0=SGA0, HIDV0=HIDV0):
            for hti in range(NHT):
                pspar = psA if hti % 2 == 0 else psB
                e.wait_ge(a_ms.h, SGA0 + hti + 1)
                e.tensor_tensor(hidT[:, hti * WIN:(hti + 1) * WIN], sgm[:],
                                pspar[:, 1024:2048], ALU.mult).then_inc(v_ms.h, 1)
        blk.vector(body)
        v_ms.v += NHT
        HID_END_V = v_ms.v

        W3T0 = t_ms.v
        W3A0 = a_ms.v
        W3V0 = v_ms.v

        SW3_BASE = tuple(s.v for s in sw3)
        def body(e, W3T0=W3T0, HID_END_V=HID_END_V):
            e.wait_ge(v_ms.h, HID_END_V)
            e.wait_ge(cc3c.h, 1)
            for cq in range(4):
                for hc in range(NHT):
                    idx = cq * NHT + hc
                    if idx >= 4:
                        e.wait_ge(t_ms.h, W3T0 + idx - 3)
                    slot = idx % 4
                    e.dma_start(out=w3t[slot][:],
                                in_=w3q_all[cq * 512:(cq + 1) * 512,
                                            hc * 128:(hc + 1) * 128],
                                transpose=True).then_inc(sw3[slot].h, 16)
                    sw3[slot].v += 16
        blk.sync(body)

        def body(e, W3T0=W3T0, W3V0=W3V0, SW3_BASE=SW3_BASE):
            for cq in range(4):
                for hc in range(NHT):
                    idx = cq * NHT + hc
                    slot = idx % 4
                    e.wait_ge(sw3[slot].h, SW3_BASE[slot] + 16 * (idx // 4 + 1))
                    last = None
                    for tt in range(8):
                        if hc == 0 and cq > 0:
                            e.wait_ge(v_ms.h, W3V0 + (cq - 1) * 8 + tt + 1)
                        ps = (psA if tt < 4 else psB)[:, (tt % 4) * 512:(tt % 4 + 1) * 512]
                        last = e.matmul(ps, hidT[:, hc * WIN + tt * 128: hc * WIN + (tt + 1) * 128],
                                 w3t[slot][:, :], start=(hc == 0), stop=(hc == NHT - 1))
                    last.then_inc(t_ms.h, 1)
        blk.tensor(body)
        t_ms.v += 4 * NHT
        W3_T_PREV[0] = t_ms.v

        def body(e, W3T0=W3T0, W3V0=W3V0):
            for cq in range(4):
                e.wait_ge(t_ms.h, W3T0 + (cq + 1) * NHT)
                for tt in range(8):
                    idx = cq * 8 + tt
                    if idx >= 1:
                        e.wait_ge(v_ms.h, W3V0 + idx)
                    ps = (psA if tt < 4 else psB)[:, (tt % 4) * 512:(tt % 4 + 1) * 512]
                    e.activation(sgm[:, 0:512], ps, AF.Sigmoid).then_inc(a_ms.h, 1)
        blk.scalar(body)
        a_ms.v += 32

        GG_BASE = (gg[0].v, gg[1].v)
        def body(e, W3A0=W3A0, W3V0=W3V0, GG_BASE=GG_BASE):
            for cq in range(4):
                for tt in range(8):
                    idx = cq * 8 + tt
                    ps = (psA if tt < 4 else psB)[:, (tt % 4) * 512:(tt % 4 + 1) * 512]
                    e.wait_ge(a_ms.h, W3A0 + idx + 1)
                    e.wait_ge(gg[idx % 2].h, GG_BASE[idx % 2] + 16 * (idx // 2))
                    e.tensor_tensor(gbf[idx % 2][:], sgm[:, 0:512], ps,
                                    ALU.mult).then_inc(v_ms.h, 1)
        blk.vector(body)
        v_ms.v += 32

        def body(e, half=half, W3V0=W3V0):
            for cq in range(4):
                for tt in range(8):
                    idx = cq * 8 + tt
                    e.wait_ge(v_ms.h, W3V0 + idx + 1)
                    e.dma_start(out=g_sc[half * WIN + tt * 128: half * WIN + (tt + 1) * 128,
                                         cq * 512:(cq + 1) * 512],
                                in_=gbf[idx % 2][:]).then_inc(gg[idx % 2].h, 16)
                    gg[idx % 2].v += 16
        blk.gpsimd(body)

    barrier()
    p2.close()

    p3 = contextlib.ExitStack()
    def sb3(name, shape, dt):
        return p3.enter_context(nc.sbuf_tensor(name, shape, dt))
    yT_sb = sb3("yT_sb", [128, 16 * 2048], BF16)
    wo_sb = sb3("wo_sb", [128, 16 * 2048], BF16)
    bo_row = sb3("bo_row", [1, 2048], F32)
    bobc = sb3("bobc", [128, 2048], F32)
    xt3 = [sb3("xt3a", [128, 2048], F32), sb3("xt3b", [128, 2048], F32)]
    gt3 = [sb3("gt3a", [128, 2048], BF16), sb3("gt3b", [128, 2048], BF16)]
    s_bf = [sb3("sbfa", [128, 2048], BF16), sb3("sbfb", [128, 2048], BF16)]
    sg_f = sb3("sg_f", [128, 2048], F32)

    P3G0 = g_ld.v
    P3S0 = s_ld.v
    P3T0 = t_ms.v
    P3V0 = v_ms.v
    P3GS0 = g_st.v

    def body(e):
        e.wait_ge(g_st.h, WO_CAST_G)
        for q in range(2):
            e.dma_start(out=wo_sb[:, q * 8192:(q + 1) * 8192],
                        in_=wo_bf[q * 512:(q + 1) * 512, :]
                        .rearrange("(a p) n -> p a n", p=128)).then_inc(g_yw.h, 16)
    blk.gpsimd(body)
    g_yw.v += 2 * 16
    YWO_G = g_yw.v

    SWQ3_BASE = swq[1].v
    def body(e):
        e.wait_ge(g_st.h, WO_CAST_G)
        for q in range(2, 4):
            e.dma_start(out=wo_sb[:, q * 8192:(q + 1) * 8192],
                        in_=wo_bf[q * 512:(q + 1) * 512, :]
                        .rearrange("(a p) n -> p a n", p=128)).then_inc(swq[1].h, 16)
        for tt in range(16):
            if tt >= 2:
                e.wait_ge(v_ms.h, P3V0 + 1 + 3 * (tt - 2) + 3)
            e.dma_start(out=gt3[tt % 2][:], in_=g_sc[tt * 128:(tt + 1) * 128, :]
                        ).then_inc(gt3s[tt % 2].h, 16)
            gt3s[tt % 2].v += 16
    blk.scalar(body)
    swq[1].v += 2 * 16

    def body(e):
        e.dma_start(out=bo_row[:], in_=bo_in[:, :]).then_inc(s_ld.h, 16)
        s_ld.v += 16
        e.wait_ge(cc2.h, 1)
        for q in range(4):
            e.dma_start(out=yT_sb[:, q * 8192:(q + 1) * 8192],
                        in_=yt_my[q * 512:(q + 1) * 512, :]
                        .rearrange("(a p) t -> p a t", p=128)).then_inc(s_ld.h, 16)
        s_ld.v += 4 * 16
        for tt in range(16):
            if tt >= 2:
                e.wait_ge(go[tt % 2].h, 16 * (tt // 2))
            e.dma_start(out=xt3[tt % 2][:], in_=x_my[tt * 128:(tt + 1) * 128, :]
                        ).then_inc(sx3[tt % 2].h, 16)
            sx3[tt % 2].v += 16
    blk.sync(body)

    def body(e):
        e.wait_ge(s_ld.h, P3S0 + 16)
        last = None
        for nn in range(4):
            last = e.matmul(psB[:, nn * 512:(nn + 1) * 512], ones_f[0:1, :],
                     bo_row[0:1, nn * 512:(nn + 1) * 512], start=True, stop=True)
        last.then_inc(t_ms.h, 1)
        e.wait_ge(g_yw.h, YWO_G)
        e.wait_ge(swq[1].h, SWQ3_BASE + 2 * 16)
        e.wait_ge(s_ld.h, P3S0 + 5 * 16)
        for tt in range(16):
            pspar = psA if tt % 2 == 0 else psB
            if tt == 1:
                e.wait_ge(v_ms.h, P3V0 + 1)
            if tt >= 2:
                e.wait_ge(v_ms.h, P3V0 + 1 + 3 * (tt - 2) + 1)
            last = None
            for cchunk in range(16):
                lhsT = yT_sb[:, cchunk * 2048 + tt * 128: cchunk * 2048 + tt * 128 + 128]
                for nn in range(4):
                    last = e.matmul(pspar[:, nn * 512:(nn + 1) * 512], lhsT,
                             wo_sb[:, cchunk * 2048 + nn * 512: cchunk * 2048 + (nn + 1) * 512],
                             start=(cchunk == 0), stop=(cchunk == 15))
            last.then_inc(t_ms.h, 1)
    blk.tensor(body)
    t_ms.v += 17

    def body(e):
        e.wait_ge(t_ms.h, P3T0 + 1)
        e.tensor_copy(bobc[:], psB[:, 0:2048]).then_inc(v_ms.h, 1)
        e.wait_ge(v_ms.h, P3V0 + 1)
        sfb3 = v_sf.v
        for tt in range(16):
            pspar = psA if tt % 2 == 0 else psB
            e.wait_ge(t_ms.h, P3T0 + tt + 2)
            e.tensor_tensor(s_bf[tt % 2][:], pspar[:, 0:2048], bobc[:],
                            ALU.add).then_inc(v_ms.h, 1)
            e.wait_ge(gt3s[tt % 2].h, 16 * (tt // 2 + 1))
            e.wait_ge(v_ms.h, P3V0 + 1 + 3 * tt + 1)
            e.tensor_tensor(sg_f[:], s_bf[tt % 2][:], gt3[tt % 2][:],
                            ALU.mult).then_inc(v_sf.h, 1)
            e.wait_ge(sx3[tt % 2].h, 16 * (tt // 2 + 1))
            e.wait_ge(v_sf.h, sfb3 + tt + 1)
            e.tensor_tensor(xt3[tt % 2][:], sg_f[:], xt3[tt % 2][:],
                            ALU.add).then_inc(v_ms.h, 2)
    blk.vector(body)
    v_ms.v += 1 + 3 * 16
    v_sf.v += 16

    def body(e):
        for tt in range(16):
            e.wait_ge(v_ms.h, P3V0 + 1 + 3 * tt + 3)
            e.dma_start(out=out_my[tt * 128:(tt + 1) * 128, :], in_=xt3[tt % 2][:]
                        ).then_inc(go[tt % 2].h, 16)
            go[tt % 2].v += 16
        e.wait_ge(go[0].h, go[0].v)
        e.wait_ge(go[1].h, go[1].v)
    blk.gpsimd(body)

    if debug_taps:
        tap_list = [("t_xn", xn_cq[0], [C, 512], BF16), ("t_gam", gam_s, [1, 4], F32),
                    ("t_w1q", w1qc, [RH, C], BF16), ("t_w3q", w3qc, [256, HID], BF16),
                    ("t_ytc", yt_c, [2048, 2048], BF16), ("t_ytm", yt_my, [2048, 2048], BF16),
                    ("t_g", g_sc, [MYT, C], BF16), ("t_xna", xn_aq[0], [NCORES * C, 512], BF16)]
        tap_out = {nm: nc.dram_tensor(nm, shp, dtp, kind="ExternalOutput")
                   for nm, _, shp, dtp in tap_list}
        def body(e):
            e.wait_ge(go[0].h, go[0].v)
            e.wait_ge(go[1].h, go[1].v)
            for nm, t, shp, dtp in tap_list:
                nrow = shp[0]
                step = max(1, 8192 * 2048 // max(1, int(np.prod(shp)))) if False else 512
                r = 0
                while r < nrow:
                    rr = min(step, nrow - r)
                    e.dma_start(out=tap_out[nm][r:r + rr, :], in_=t[r:r + rr, :]
                                ).then_inc(g_st.h, 16)
                    g_st.v += 16
                    r += rr
            e.wait_ge(g_st.h, g_st.v)
        blk.gpsimd(body)

    p3.close()
    ctx.close()
    return nc


def _make_mask():
    s = np.arange(128)[:, None]
    t = np.arange(512)[None, :]
    m = np.zeros((128, 2048), dtype=np.float32)
    for jj in range(4):
        m[:, jj * 512:(jj + 1) * 512] = (jj * 128 + s <= t).astype(np.float32)
    return m


def kernel(L0_attn_x, L0_attn_Wq, L0_attn_bq, L0_attn_Wk, L0_attn_bk, L0_attn_Wv, L0_attn_bv,
           L0_attn_Wo, L0_attn_bo, L0_ffn_w1, L0_ffn_w2, L0_ffn_w3):
    x = np.ascontiguousarray(np.asarray(L0_attn_x, dtype=np.float32))
    Wq = np.asarray(L0_attn_Wq, dtype=np.float32)
    Wk = np.asarray(L0_attn_Wk, dtype=np.float32)
    Wv = np.asarray(L0_attn_Wv, dtype=np.float32)
    bq = np.asarray(L0_attn_bq, dtype=np.float32)
    bk = np.asarray(L0_attn_bk, dtype=np.float32)
    bv = np.asarray(L0_attn_bv, dtype=np.float32)
    Wo = np.ascontiguousarray(np.asarray(L0_attn_Wo, dtype=np.float32))
    bo = np.asarray(L0_attn_bo, dtype=np.float32).reshape(1, C)
    w1 = np.asarray(L0_ffn_w1, dtype=np.float32)
    w2 = np.asarray(L0_ffn_w2, dtype=np.float32)
    w3 = np.asarray(L0_ffn_w3, dtype=np.float32)

    ident = np.eye(128, dtype=np.float32)
    ones = np.ones((128, 128), dtype=np.float32)
    mask = _make_mask()

    nc = build_nc()
    in_maps = []
    for c in range(NCORES):
        b = c // 4
        t0 = (c % 4) * MYT
        in_maps.append(dict(
            x_my=np.ascontiguousarray(x[b, t0:t0 + MYT, :]),
            wq_my=np.ascontiguousarray(Wq[2 * c:2 * c + 2]),
            wk_my=np.ascontiguousarray(Wk[2 * c:2 * c + 2]),
            wv_my=np.ascontiguousarray(Wv[2 * c:2 * c + 2]),
            bq_my=np.ascontiguousarray(bq[2 * c:2 * c + 2].T),
            bk_my=np.ascontiguousarray(bk[2 * c:2 * c + 2].T),
            bv_my=np.ascontiguousarray(bv[2 * c:2 * c + 2].T),
            wo_in=Wo,
            bo_in=bo,
            w1_my=np.ascontiguousarray(w1[RH * c:RH * (c + 1)]),
            w2_my=np.ascontiguousarray(w2[RH * c:RH * (c + 1)]),
            w3_my=np.ascontiguousarray(w3[256 * c:256 * (c + 1)]),
            ident_in=ident,
            ones_in=ones,
            mask_in=mask,
        ))
    res = run_bass_kernel_spmd(nc, in_maps, core_ids=list(range(NCORES)))
    out = np.empty((B, T, C), dtype=np.float32)
    for c in range(NCORES):
        b = c // 4
        t0 = (c % 4) * MYT
        out[b, t0:t0 + MYT, :] = np.asarray(res.results[c]["out_my"], dtype=np.float32)
    return out


# version 21 (final)
# speedup vs baseline: 1.0818x; 1.0015x over previous
import contextlib
import numpy as np
import ml_dtypes
import concourse.bass as bass
import concourse.mybir as mybir
from concourse.bass_utils import run_bass_kernel_spmd
from concourse.aot_env import allow_ici_all_to_all

F32 = mybir.dt.float32
BF16 = mybir.dt.bfloat16
AF = mybir.ActivationFunctionType
ALU = mybir.AluOpType
AX = mybir.AxisListType

NCORES = 8
B, T, C = 2, 8192, 2048
H, DH = 16, 128
HID, RH, PH = 5632, 704, 768
HPAD = NCORES * PH
MYT = 2048
NT = MYT // 128
SM_SCALE = float(DH) ** -0.5
RND_C = 1.5 * 2.0 ** 23
EPS_LN = 1e-5
EPS_Q = 1e-8
WCNT = float(HID * C)
WIN = 1024
NW = T // WIN
NTILE_W12 = 6


class Ctr:
    __slots__ = ("h", "v")
    def __init__(self, h):
        self.h = h
        self.v = 0

    def inc(self, ins, k=1):
        ins.then_inc(self.h, k)
        self.v += k
        return self.v


def w12_rows(j):
    r0 = j * 128
    return r0, min(RH, r0 + 128) - r0


def build_nc(debug_taps=False):
    nc = bass.Bass("TRN2", target_bir_lowering=False, detect_race_conditions=True)

    x_my = nc.dram_tensor("x_my", [MYT, C], F32, kind="ExternalInput")
    wq_my = nc.dram_tensor("wq_my", [2, C, DH], F32, kind="ExternalInput")
    wk_my = nc.dram_tensor("wk_my", [2, C, DH], F32, kind="ExternalInput")
    wv_my = nc.dram_tensor("wv_my", [2, C, DH], F32, kind="ExternalInput")
    bq_my = nc.dram_tensor("bq_my", [DH, 2], F32, kind="ExternalInput")
    bk_my = nc.dram_tensor("bk_my", [DH, 2], F32, kind="ExternalInput")
    bv_my = nc.dram_tensor("bv_my", [DH, 2], F32, kind="ExternalInput")
    wo_in = nc.dram_tensor("wo_in", [C, C], F32, kind="ExternalInput")
    bo_in = nc.dram_tensor("bo_in", [1, C], F32, kind="ExternalInput")
    w1_my = nc.dram_tensor("w1_my", [RH, C], F32, kind="ExternalInput")
    w2_my = nc.dram_tensor("w2_my", [RH, C], F32, kind="ExternalInput")
    w3_my = nc.dram_tensor("w3_my", [256, HID], F32, kind="ExternalInput")
    ident_in = nc.dram_tensor("ident_in", [128, 128], F32, kind="ExternalInput")
    ones_in = nc.dram_tensor("ones_in", [128, 128], F32, kind="ExternalInput")
    mask_in = nc.dram_tensor("mask_in", [128, 2048], F32, kind="ExternalInput")
    out_my = nc.dram_tensor("out_my", [MYT, C], F32, kind="ExternalOutput")

    xn_cq = [nc.dram_tensor(f"xn_cq{q}", [C, 512], BF16, kind="Internal") for q in range(4)]
    xn_aq = [nc.dram_tensor(f"xn_aq{q}", [NCORES * C, 512], BF16, kind="Internal") for q in range(4)]
    gam_c = nc.dram_tensor("gam_c", [1, 4], F32, kind="Internal")
    gam_s = nc.dram_tensor("gam_s", [1, 4], F32, kind="Internal")
    w1qc = nc.dram_tensor("w1qc", [RH, C], BF16, kind="Internal")
    w2qc = nc.dram_tensor("w2qc", [RH, C], BF16, kind="Internal")
    w3qc = nc.dram_tensor("w3qc", [256, HID], BF16, kind="Internal")
    w1q_all = nc.dram_tensor("w1q_all", [NCORES * RH, C], BF16, kind="Internal")
    w2q_all = nc.dram_tensor("w2q_all", [NCORES * RH, C], BF16, kind="Internal")
    w3q_all = nc.dram_tensor("w3q_all", [NCORES * 256, HID], BF16, kind="Internal")
    wo_bf = nc.dram_tensor("wo_bf", [C, C], BF16, kind="Internal")
    yt_c = nc.dram_tensor("yt_c", [2048, 2048], BF16, kind="Internal")
    yt_my = nc.dram_tensor("yt_my", [2048, 2048], BF16, kind="Internal")
    g_sc = nc.dram_tensor("g_sc", [MYT, C], BF16, kind="Internal")

    rg = [list(range(NCORES))]
    ctx = contextlib.ExitStack()

    def sem(name):
        return Ctr(ctx.enter_context(nc.semaphore(name)))

    def sb(name, shape, dt):
        return ctx.enter_context(nc.sbuf_tensor(name, shape, dt))

    ident_bf = sb("ident_bf", [128, 128], BF16)
    ident_f = sb("ident_f", [128, 128], F32)
    ones_f = sb("ones_f", [128, 128], F32)
    ones_bf = sb("ones_bf", [128, 1], BF16)
    stat = sb("stat", [128, 64], F32)
    grow = sb("grow", [1, 16], F32)
    rq_sb = sb("rq_sb", [128, 4], F32)
    negb = sb("negb", [128, 4], F32)
    bmax = sb("bmax", [1, 64], F32)
    biasT = sb("biasT", [128, 6], F32)

    psA = ctx.enter_context(nc.psum_tensor("psA", [128, 2048], F32))
    psB = ctx.enter_context(nc.psum_tensor("psB", [128, 2048], F32))

    s_ld = sem("s_ld")
    g_ld = sem("g_ld")
    s_tp = sem("s_tp")
    v_ms = sem("v_ms")
    a_ms = sem("a_ms")
    t_ms = sem("t_ms")
    g_st = sem("g_st")
    cc0 = sem("cc0"); cc2 = sem("cc2")
    ccq = [sem(f"ccq{q}") for q in range(4)]
    cc3a = sem("cc3a"); cc3b = sem("cc3b"); cc3c = sem("cc3c")
    g_c = sem("g_c")
    v_sf = sem("v_sf")
    a_sf = sem("a_sf")
    sx = [sem("sx0"), sem("sx1")]
    gxn = [sem("gxn0"), sem("gxn1")]
    gy = [sem("gy0"), sem("gy1")]
    gg = [sem("gg0"), sem("gg1")]
    sw3 = [sem(f"sw3{i}") for i in range(4)]
    g_yw = sem("g_yw")
    swp = [sem("swp0"), sem("swp1")]
    stw = [sem("stw0"), sem("stw1")]
    gq = [sem("gq0"), sem("gq1")]
    vq = sem("vq")
    vd = sem("vd")
    g_wl = sem("g_wl")
    swq = [sem("swq0"), sem("swq1")]
    gt3s = [sem("gt3s0"), sem("gt3s1")]
    sx3 = [sem("sx3a"), sem("sx3b")]
    go = [sem("go0"), sem("go1")]
    bar = sem("bar")

    blk = ctx.enter_context(nc.Block())
    barn = [0]

    def barrier():
        barn[0] += 1
        target = 5 * barn[0]
        caps = dict(sync=(s_ld, s_tp, sx[0], sx[1], sw3[0], sw3[1], sw3[2], sw3[3], sx3[0], sx3[1], swp[0], swp[1], stw[0], stw[1]),
                    gpsimd=(g_ld, g_st, g_c, gxn[0], gxn[1], gy[0], gy[1], gg[0], gg[1],
                            g_yw, gt3s[0], gt3s[1], go[0], go[1], gq[0], gq[1], g_wl),
                    tensor=(t_ms,), scalar=(a_ms, a_sf, swq[0], swq[1]), vector=(v_ms, v_sf, vq, vd))
        for name, dec in (("sync", blk.sync), ("gpsimd", blk.gpsimd), ("tensor", blk.tensor),
                          ("scalar", blk.scalar), ("vector", blk.vector)):
            def body(e, name=name):
                for s in caps[name]:
                    e.wait_ge(s.h, s.v)
                e.sem_inc(bar.h, 1)
                e.wait_ge(bar.h, target)
            dec(body)
        bar.v = target

    WQW = 2816
    WT2 = []
    for j in range(NTILE_W12):
        r0, nr = w12_rows(j)
        WT2.append((w1_my, r0, nr, 0, C, w1qc, r0, 0))
    for j in range(NTILE_W12):
        r0, nr = w12_rows(j)
        WT2.append((w2_my, r0, nr, 0, C, w2qc, r0, 1))
    for j in range(2):
        for hf in range(2):
            WT2.append((w3_my, j * 128, 128, hf * WQW, HID - hf * WQW if hf else WQW,
                        w3qc, j * 128, 2))
    NWT2 = len(WT2)

    pW = contextlib.ExitStack()
    def sbW(name, shape, dt):
        return pW.enter_context(nc.sbuf_tensor(name, shape, dt))
    wqa = sbW("wqa", [128, WQW], F32)
    qbf = [sbW("qbf0", [128, WQW], BF16), sbW("qbf1", [128, WQW], BF16)]
    wqkv = sbW("wqkv", [128, 6 * C], BF16)
    mask_bf = sbW("mask_bf", [128, 2048], BF16)

    pLN = contextlib.ExitStack()
    def sbL(name, shape, dt):
        return pLN.enter_context(nc.sbuf_tensor(name, shape, dt))
    xt = [sbL("xt0", [128, C], F32), sbL("xt1", [128, C], F32)]
    xsq = sbL("xsq", [128, C], F32)
    xnf = [sbL("xnf0", [128, C], F32), sbL("xnf1", [128, C], F32)]
    xn_st = [sbL("xnst0", [128, 16 * 512], BF16), sbL("xnst1", [128, 16 * 512], BF16)]

    def body(e):
        e.dma_start(out=ident_bf[:], in_=ident_in[:, :]).then_inc(g_c.h, 16)
        e.dma_start(out=ones_f[:], in_=ones_in[:, :]).then_inc(g_c.h, 16)
        e.dma_start(out=ones_bf[:], in_=ones_in[:, 0:1], allow_slow_non_contiguous=True).then_inc(g_c.h, 16)
        e.dma_start(out=biasT[:, 0:2], in_=bq_my[:, :]).then_inc(g_c.h, 16)
        e.dma_start(out=biasT[:, 2:4], in_=bk_my[:, :]).then_inc(g_c.h, 16)
        e.dma_start(out=biasT[:, 4:6], in_=bv_my[:, :]).then_inc(g_c.h, 16)
        e.dma_start(out=ident_f[:], in_=ident_in[:, :]).then_inc(g_c.h, 16)
        e.dma_start(out=mask_bf[:], in_=mask_in[:, :]).then_inc(g_ld.h, 16)
        g_ld.v += 16
        for idx, wsrc in enumerate((wq_my, wk_my, wv_my)):
            for hh in range(2):
                base = (idx * 2 + hh) * C
                for q in range(4):
                    e.dma_start(out=wqkv[:, base + q * 512:base + (q + 1) * 512],
                                in_=wsrc[hh, q * 512:(q + 1) * 512, :]
                                .rearrange("(a p) d -> p a d", p=128)
                                ).then_inc(g_ld.h, 16)
                    g_ld.v += 16
        wsrc0, r0, nr, c0, wd, _, _, _ = WT2[0]
        e.dma_start(out=wqa[0:nr, 0:wd], in_=wsrc0[r0:r0 + nr, c0:c0 + wd]
                    ).then_inc(g_wl.h, 16)
        g_wl.v += 16
    blk.gpsimd(body)
    g_c.v = 7 * 16
    GC_ALL = g_c.v
    GLD_QKV = g_ld.v

    VLN0 = v_ms.v
    ALN0 = a_ms.v
    TLN0 = t_ms.v
    GXN_B = (gxn[0].v, gxn[1].v)

    def body(e):
        for i in range(NT):
            if i >= 2:
                e.wait_ge(v_ms.h, VLN0 + 6 * (i - 2) + 4)
            e.dma_start(out=xt[i % 2][:], in_=x_my[i * 128:(i + 1) * 128, :]
                        ).then_inc(sx[i % 2].h, 16)
            sx[i % 2].v += 16
    blk.sync(body)

    def body(e):
        for i in range(NT):
            e.wait_ge(sx[i % 2].h, 16 * (i // 2 + 1))
            if i >= 1:
                e.wait_ge(v_ms.h, VLN0 + 6 * (i - 1) + 4)
            e.activation(xsq[:], xt[i % 2][:], AF.Square).then_inc(a_ms.h, 1)
            e.wait_ge(v_ms.h, VLN0 + 6 * i + 3)
            e.activation(stat[:, 56:57], stat[:, 16 + i:17 + i], AF.Ln).then_inc(a_ms.h, 1)
            e.wait_ge(a_ms.h, ALN0 + 5 * i + 2)
            e.activation(stat[:, 52 + i % 2:53 + i % 2], stat[:, 56:57], AF.Exp,
                         scale=-0.5).then_inc(a_ms.h, 1)
            di = i // 4
            psX = psA if i % 2 == 0 else psB
            for q in (0, 1):
                e.wait_ge(t_ms.h, TLN0 + 16 * i + 4 * (q + 1))
                if di >= 2 and i % 4 == 0 and q == 0:
                    e.wait_ge(gxn[di % 2].h, GXN_B[di % 2] + 64 * (di // 2))
                e.activation(xn_st[di % 2][:, (i % 4) * 2048 + q * 512:
                                           (i % 4) * 2048 + (q + 1) * 512],
                             psX[:, q * 512:(q + 1) * 512], AF.Identity,
                             bias=stat[:, 60:61]).then_inc(a_ms.h, 1)
    blk.scalar(body)
    a_ms.v += 5 * NT

    def body(e):
        sfb = v_sf.v
        e.memset(stat[:, 60:61], 0.0).then_inc(v_sf.h, 1)
        sfb += 1
        for i in range(NT):
            e.wait_ge(sx[i % 2].h, 16 * (i // 2 + 1))
            e.reduce_sum(stat[:, i:i + 1], xt[i % 2][:], axis=AX.X).then_inc(v_ms.h, 1)
            e.wait_ge(a_ms.h, ALN0 + 5 * i + 1)
            e.reduce_sum(stat[:, 16 + i:17 + i], xsq[:], axis=AX.X).then_inc(v_ms.h, 1)
            e.wait_ge(v_ms.h, VLN0 + 6 * i + 2)
            e.tensor_scalar(stat[:, i:i + 1], stat[:, i:i + 1], -1.0 / C, None,
                            ALU.mult).then_inc(v_sf.h, 1)
            e.wait_ge(v_sf.h, sfb + 4 * i + 1)
            e.tensor_tensor(stat[:, 54:55], stat[:, i:i + 1], stat[:, i:i + 1],
                            ALU.mult).then_inc(v_sf.h, 1)
            e.tensor_scalar(stat[:, 16 + i:17 + i], stat[:, 16 + i:17 + i], 1.0 / C, EPS_LN,
                            ALU.mult, ALU.add).then_inc(v_sf.h, 1)
            e.wait_ge(v_sf.h, sfb + 4 * i + 3)
            e.tensor_tensor(stat[:, 16 + i:17 + i], stat[:, 16 + i:17 + i], stat[:, 54:55],
                            ALU.subtract).then_inc(v_ms.h, 1)
            e.wait_ge(a_ms.h, ALN0 + 5 * i + 3)
            e.tensor_scalar(xsq[:], xt[i % 2][:], stat[:, i:i + 1], None,
                            ALU.add).then_inc(v_sf.h, 1)
            e.wait_ge(v_sf.h, sfb + 4 * i + 4)
            if i >= 2:
                e.wait_ge(t_ms.h, TLN0 + 16 * (i - 1))
            e.tensor_scalar(xnf[i % 2][:], xsq[:], stat[:, 52 + i % 2:53 + i % 2], None,
                            ALU.mult).then_inc(v_ms.h, 1)
            di = i // 4
            psX = psA if i % 2 == 0 else psB
            for q in (2, 3):
                e.wait_ge(t_ms.h, TLN0 + 16 * i + 4 * (q + 1))
                if di >= 2 and i % 4 == 0 and q == 2:
                    e.wait_ge(gxn[di % 2].h, GXN_B[di % 2] + 64 * (di // 2))
                e.tensor_copy(xn_st[di % 2][:, (i % 4) * 2048 + q * 512:
                                            (i % 4) * 2048 + (q + 1) * 512],
                              psX[:, q * 512:(q + 1) * 512]).then_inc(v_ms.h, 1)
    blk.vector(body)
    v_ms.v += 6 * NT
    v_sf.v += 4 * NT + 1

    def body(e):
        e.wait_ge(g_c.h, GC_ALL)
        for i in range(NT):
            e.wait_ge(v_ms.h, VLN0 + 6 * i + 4)
            psX = psA if i % 2 == 0 else psB
            for q in range(4):
                if i >= 2:
                    if q < 2:
                        e.wait_ge(a_ms.h, ALN0 + 5 * (i - 2) + 4 + q)
                    else:
                        e.wait_ge(v_ms.h, VLN0 + 6 * (i - 2) + 3 + q)
                for j in range(4):
                    cc = 4 * q + j
                    e.transpose(psX[:, q * 512 + j * 128:q * 512 + (j + 1) * 128],
                                xnf[i % 2][:, cc * 128:(cc + 1) * 128],
                                ident_f[:]).then_inc(t_ms.h, 1)
    blk.tensor(body)
    t_ms.v += 16 * NT

    def body(e):
        for di in range(NT // 4):
            xo = xn_cq[di].ap().rearrange("(a p) t -> p a t", p=128)
            for f in range(4):
                e.wait_ge(a_ms.h, ALN0 + 5 * (4 * di + f + 1))
                e.wait_ge(v_ms.h, VLN0 + 6 * (4 * di + f + 1))
                e.dma_start(out=xo[:, :, f * 128:(f + 1) * 128],
                            in_=xn_st[di % 2][:, f * 2048:(f + 1) * 2048]
                            .rearrange("p (a t) -> p a t", a=16)
                            ).then_inc(gxn[di % 2].h, 16)
                gxn[di % 2].v += 16
            e.wait_ge(gxn[di % 2].h, gxn[di % 2].v)
            e.collective_compute("AllGather", ALU.bypass, replica_groups=rg,
                                 ins=[xn_cq[di].ap().opt()],
                                 outs=[xn_aq[di].ap().opt()]).then_inc(ccq[di].h)
            ccq[di].v = 1
    blk.gpsimd(body)

    barrier()
    pLN.close()

    p1 = contextlib.ExitStack()
    def sb1(name, shape, dt):
        return p1.enter_context(nc.sbuf_tensor(name, shape, dt))
    xnT = [sb1("xnTa", [128, 16 * 512], BF16), sb1("xnTb", [128, 16 * 512], BF16)]
    qT2 = sb1("qT2", [128, 2 * T], BF16)
    kT2 = sb1("kT2", [128, 2 * T], BF16)
    v2 = sb1("v2", [128, 2 * T], BF16)
    vTs = sb1("vTs", [128, 512], F32)
    pT = [sb1(f"pT{i}", [128, 512], BF16) for i in range(6)]
    acc2 = [sb1("acc0", [128, 512], F32), sb1("acc1", [128, 512], F32)]
    accT_bf = sb1("accT", [128, 512], BF16)
    rden_sb = sb1("rden_sb", [1, 512], F32)
    dtmp = sb1("dtmp", [1, 1024], F32)
    bc_sb = sb1("bc_sb", [128, 512], F32)
    yn_bf = [sb1(f"yn{i}", [128, 512], BF16) for i in range(2)]

    VT_B0 = v_ms.v
    def body(e):
        for k, (wsrc, r0, nr, c0, wd, dst, dr, wid) in enumerate(WT2):
            if k == 0:
                continue
            e.wait_ge(v_ms.h, VT_B0 + 9 * k)
            e.dma_start(out=wqa[0:nr, 0:wd], in_=wsrc[r0:r0 + nr, c0:c0 + wd]
                        ).then_inc(g_wl.h, 16)
            g_wl.v += 16
    blk.gpsimd(body)

    def body(e):
        e.dma_start(out=wo_bf[:, :], in_=wo_in[:, :]).then_inc(g_st.h, 16)
    blk.gpsimd(body)
    g_st.v += 16
    WO_CAST_G = g_st.v
    GLDQ0 = g_wl.v
    GQ_BASE = (gq[0].v, gq[1].v)
    V_SF_FOLD = None
    V_FOLD = None
    QSH = ({qb: qb - 6 for qb in range(6, 16)}, {qb: 10 + qb for qb in range(6)})
    RSH = ({qb: qb - 5 for qb in range(5, 16)}, {qb: 11 + qb for qb in range(5)})

    for b in range(B):
        PWIN = 512
        NPW = T // PWIN
        projT0 = t_ms.v
        projA0 = a_ms.v
        vtV0 = v_ms.v
        STP0 = s_tp.v

        STW_BASE = (stw[0].v, stw[1].v)
        worder = ([r * 4 + q for q in range(4) for r in range(4)] if b == 0
                  else list(range(NPW)))
        def body(e, b=b, projT0=projT0, worder=worder):
            for p in range(NPW):
                w = worder[p]
                if p <= 1:
                    if b == 1:
                        e.wait_ge(t_ms.h, projT0)
                else:
                    e.wait_ge(t_ms.h, projT0 + (p - 2) * 14 + 6)
                if b == 0 and p % 4 == 0:
                    e.wait_ge(ccq[p // 4].h, 1)
                gtok = b * T + w * PWIN
                r = gtok // MYT
                cq = (gtok % MYT) // PWIN
                for cchunk in range(16):
                    src = xn_aq[cq][r * C + cchunk * 128: r * C + (cchunk + 1) * 128, :]
                    e.dma_start(out=xnT[p % 2][:, cchunk * PWIN:(cchunk + 1) * PWIN], in_=src
                                ).then_inc(stw[p % 2].h, 16)
                stw[p % 2].v += 256
        blk.sync(body)

        def body(e, b=b, projT0=projT0, projA0=projA0, vtV0=vtV0, STW_BASE=STW_BASE,
                 worder=worder):
            if b == 0:
                e.wait_ge(g_ld.h, GLD_QKV)
            e.wait_ge(a_ms.h, projA0)
            e.wait_ge(v_ms.h, vtV0)
            for p in range(NPW):
                w = worder[p]
                e.wait_ge(stw[p % 2].h, STW_BASE[p % 2] + 256 * (p // 2 + 1))
                for rnd in range(6):
                    gr = p * 6 + rnd
                    qtr = rnd % 4
                    ps = psA[:, qtr * 512:(qtr + 1) * 512]
                    if gr >= 4:
                        e.wait_ge(a_ms.h, projA0 + gr - 3)
                    kind, hh = divmod(rnd, 2)
                    last = None
                    for cchunk in range(16):
                        lhsT = wqkv[:, (kind * 2 + hh) * C + cchunk * 128:
                                    (kind * 2 + hh) * C + (cchunk + 1) * 128]
                        last = e.matmul(ps, lhsT,
                                 xnT[p % 2][:, cchunk * PWIN:(cchunk + 1) * PWIN],
                                 start=(cchunk == 0), stop=(cchunk == 15))
                    last.then_inc(t_ms.h, 1)
                for hh in range(2):
                    e.wait_ge(a_ms.h, projA0 + p * 6 + 4 + hh + 1)
                    for j in range(4):
                        tki = p * 8 + hh * 4 + j
                        if tki >= 2:
                            tp = tki - 2
                            idx = 9 * (tp // 8) + tp % 8 + 1 if b == 0 else tp + 1
                            e.wait_ge(v_ms.h, vtV0 + idx)
                        psT = psB[:, 0:128] if tki % 2 == 0 else psB[:, 512:640]
                        e.transpose(psT, vTs[:, j * 128:(j + 1) * 128],
                                    ident_f[:]).then_inc(t_ms.h, 1)
        blk.tensor(body)
        t_ms.v += NPW * 14

        def body(e, b=b, projT0=projT0, projA0=projA0, worder=worder):
            for p in range(NPW):
                w = worder[p]
                for rnd in range(6):
                    kind, hh = divmod(rnd, 2)
                    e.wait_ge(t_ms.h, projT0 + p * 14 + rnd + 1)
                    if kind == 2 and hh == 1:
                        e.wait_ge(t_ms.h, projT0 + p * 14 + 6 + 4)
                    bcol = biasT[:, kind * 2 + hh:kind * 2 + hh + 1]
                    if kind == 0:
                        dst = qT2[:, hh * T + w * PWIN: hh * T + (w + 1) * PWIN]
                    elif kind == 1:
                        dst = kT2[:, hh * T + w * PWIN: hh * T + (w + 1) * PWIN]
                    else:
                        dst = vTs[:, 0:PWIN]
                    qtr = rnd % 4
                    ps = psA[:, qtr * 512:(qtr + 1) * 512]
                    e.activation(dst, ps, AF.Identity, bias=bcol).then_inc(a_ms.h, 1)
        blk.scalar(body)
        a_ms.v += NPW * 6

        def body(e, b=b, projT0=projT0, worder=worder):
            for p in range(NPW):
                w = worder[p]
                for hh in range(2):
                    for j in range(4):
                        tki = p * 8 + hh * 4 + j
                        psT = psB[:, 0:128] if tki % 2 == 0 else psB[:, 512:640]
                        e.wait_ge(t_ms.h, projT0 + p * 14 + 6 + hh * 4 + j + 1)
                        kb = w * 4 + j
                        e.tensor_copy(v2[:, hh * T + kb * 128:hh * T + (kb + 1) * 128],
                                      psT).then_inc(v_ms.h, 1)
                if b == 0:
                    k = p
                    wsrc, r0, nr, c0, wd, dst, dr, wid = WT2[k]
                    e.wait_ge(g_wl.h, (k + 1) * 16)
                    e.reduce_sum(stat[0:nr, 32 + k:33 + k], wqa[0:nr, 0:wd], axis=AX.X,
                                 apply_absolute_value=True).then_inc(v_ms.h, 1)
                    if nr < 128:
                        e.memset(stat[64:128, 32 + k:33 + k], 0.0)
            if b == 0:
                e.wait_ge(v_ms.h, vtV0 + NPW * 9)
                e.reduce_sum(stat[:, 48:49], stat[:, 32:38], axis=AX.X).then_inc(v_sf.h, 1)
                e.reduce_sum(stat[:, 49:50], stat[:, 38:44], axis=AX.X).then_inc(v_sf.h, 1)
                e.memset(stat[:, 51:52], 0.0).then_inc(v_sf.h, 1)
                e.reduce_sum(stat[:, 50:51], stat[:, 44:48], axis=AX.X).then_inc(v_ms.h, 1)
        blk.vector(body)
        if b == 0:
            assert vtV0 == VT_B0
            v_ms.v += NPW * 9 + 1
            v_sf.v += 3
            V_SF_FOLD = v_sf.v
            V_FOLD = v_ms.v
        else:
            v_ms.v += NPW * 8
        if b == 0:
            VTE_INS = v_ms.v
            def body(e, VTE_INS=VTE_INS):
                e.wait_ge(v_ms.h, V_FOLD)
                e.wait_ge(v_sf.h, V_SF_FOLD)
                e.wait_ge(v_ms.h, VTE_INS)
                e.matmul(psB[0:1, 0:4], ones_f[:, 0:1], stat[:, 48:52], start=True,
                         stop=True).then_inc(t_ms.h, 1)
            blk.tensor(body)
            t_ms.v += 1
            T_GAM = t_ms.v
            def body(e, T_GAM=T_GAM):
                e.wait_ge(t_ms.h, T_GAM)
                e.tensor_copy(grow[0:1, 0:4], psB[0:1, 0:4]).then_inc(v_ms.h, 1)
            blk.vector(body)
            v_ms.v += 1
            V_GROW = v_ms.v
            def body(e, V_GROW=V_GROW):
                e.wait_ge(v_ms.h, V_GROW)
                e.dma_start(out=gam_c[:, :], in_=grow[0:1, 0:4]).then_inc(g_st.h, 16)
                e.wait_ge(g_st.h, g_st.v + 16)
                e.collective_compute("AllReduce", ALU.add, replica_groups=rg,
                                     ins=[gam_c.ap().opt()],
                                     outs=[gam_s.ap().opt()]).then_inc(cc0.h)
                e.wait_ge(cc0.h, 1)
                e.dma_start(out=grow[0:1, 8:12], in_=gam_s[:, :]).then_inc(g_ld.h, 16)
            blk.gpsimd(body)
            g_st.v += 16
            g_ld.v += 16
            GLD_GAMS = g_ld.v
            cc0.v = 1
            def body(e, GLD_GAMS=GLD_GAMS):
                e.wait_ge(g_ld.h, GLD_GAMS)
                e.tensor_scalar(grow[0:1, 12:16], grow[0:1, 8:12], 1.0 / WCNT, EPS_Q,
                                ALU.mult, ALU.add).then_inc(v_sf.h, 1)
                e.wait_ge(v_sf.h, v_sf.v + 1)
                e.reciprocal(grow[0:1, 4:8], grow[0:1, 12:16]).then_inc(v_ms.h, 1)
            blk.vector(body)
            v_ms.v += 1
            v_sf.v += 1
            V_RECIP = v_ms.v
            def body(e, V_RECIP=V_RECIP):
                e.wait_ge(v_ms.h, V_RECIP)
                e.matmul(psB[:, 8:12], ones_f[0:1, :], grow[0:1, 4:8], start=True,
                         stop=True).then_inc(t_ms.h, 1)
            blk.tensor(body)
            t_ms.v += 1
            T_RQ = t_ms.v
            def body(e, T_RQ=T_RQ):
                e.wait_ge(t_ms.h, T_RQ)
                e.tensor_copy(rq_sb[:], psB[:, 8:12]).then_inc(v_ms.h, 1)
            blk.vector(body)
            v_ms.v += 1
            V_RQ = v_ms.v

        PROJ_A_END = a_ms.v
        VT_V_END = v_ms.v

        for hh in range(2):
            pair = b * 2 + hh
            TKD = t_ms.v
            def body(e, hh=hh, TKD=TKD):
                e.wait_ge(a_ms.h, PROJ_A_END)
                e.wait_ge(t_ms.h, TKD)
                e.tensor_tensor(xnT[0][:, 0:T], qT2[:, hh * T:(hh + 1) * T],
                                qT2[:, hh * T:(hh + 1) * T], ALU.mult).then_inc(v_ms.h, 1)
            blk.vector(body)
            v_ms.v += 1
            QSQ_V = v_ms.v

            BQ_T0 = t_ms.v
            def body(e, QSQ_V=QSQ_V):
                e.wait_ge(v_ms.h, QSQ_V)
                for ch in range(16):
                    if ch >= 2:
                        e.wait_ge(v_ms.h, QSQ_V + ch - 1)
                    ps = psB[0:1, 1536:2048] if ch % 2 == 0 else psB[0:1, 1024:1536]
                    e.matmul(ps, ones_bf[:, 0:1], xnT[0][:, ch * 512:(ch + 1) * 512],
                             start=True, stop=True).then_inc(t_ms.h, 1)
            blk.tensor(body)
            t_ms.v += 16

            def body(e, BQ_T0=BQ_T0, pair=pair, QSQ_V=QSQ_V):
                for ch in range(16):
                    ps = psB[0:1, 1536:2048] if ch % 2 == 0 else psB[0:1, 1024:1536]
                    e.wait_ge(t_ms.h, BQ_T0 + ch + 1)
                    e.reduce_max(bmax[0:1, ch:ch + 1], ps, axis=AX.X).then_inc(v_ms.h, 1)
                e.wait_ge(v_ms.h, QSQ_V + 16)
                e.reduce_max(bmax[0:1, 32 + pair:33 + pair], bmax[0:1, 0:16],
                             axis=AX.X).then_inc(v_ms.h, 1)
            blk.vector(body)
            v_ms.v += 17

            QD = t_ms.v
            def body(e, hh=hh, QD=QD):
                e.wait_ge(t_ms.h, QD)
                e.tensor_tensor(xnT[0][:, 0:T], kT2[:, hh * T:(hh + 1) * T],
                                kT2[:, hh * T:(hh + 1) * T], ALU.mult).then_inc(v_ms.h, 1)
            blk.vector(body)
            v_ms.v += 1
            KSQ_V = v_ms.v

            BK_T0 = t_ms.v
            def body(e, KSQ_V=KSQ_V):
                e.wait_ge(v_ms.h, KSQ_V)
                for ch in range(16):
                    if ch >= 2:
                        e.wait_ge(v_ms.h, KSQ_V + ch - 1)
                    ps = psB[0:1, 1536:2048] if ch % 2 == 0 else psB[0:1, 1024:1536]
                    e.matmul(ps, ones_bf[:, 0:1], xnT[0][:, ch * 512:(ch + 1) * 512],
                             start=True, stop=True).then_inc(t_ms.h, 1)
            blk.tensor(body)
            t_ms.v += 16

            def body(e, BK_T0=BK_T0, pair=pair, KSQ_V=KSQ_V):
                for ch in range(16):
                    ps = psB[0:1, 1536:2048] if ch % 2 == 0 else psB[0:1, 1024:1536]
                    e.wait_ge(t_ms.h, BK_T0 + ch + 1)
                    e.reduce_max(bmax[0:1, ch:ch + 1], ps, axis=AX.X).then_inc(v_ms.h, 1)
                e.wait_ge(v_ms.h, KSQ_V + 16)
                e.reduce_max(bmax[0:1, 40 + pair:41 + pair], bmax[0:1, 0:16],
                             axis=AX.X).then_inc(v_sf.h, 1)
                e.wait_ge(v_sf.h, v_sf.v + 1)
                e.tensor_tensor(bmax[0:1, 48 + pair:49 + pair], bmax[0:1, 32 + pair:33 + pair],
                                bmax[0:1, 40 + pair:41 + pair], ALU.mult).then_inc(v_ms.h, 1)
            blk.vector(body)
            v_ms.v += 17
            v_sf.v += 1
            PRD_V = v_ms.v

            def body(e, pair=pair, PRD_V=PRD_V):
                e.wait_ge(v_ms.h, PRD_V)
                e.activation(bmax[0:1, 56 + pair:57 + pair], bmax[0:1, 48 + pair:49 + pair],
                             AF.Ln).then_inc(a_sf.h, 1)
                e.wait_ge(a_sf.h, a_sf.v + 1)
                e.activation(bmax[0:1, 48 + pair:49 + pair], bmax[0:1, 56 + pair:57 + pair],
                             AF.Exp, scale=0.5).then_inc(a_ms.h, 1)
            blk.scalar(body)
            a_ms.v += 1
            a_sf.v += 1
            SQ_A = a_ms.v

            def body(e, pair=pair, SQ_A=SQ_A):
                e.wait_ge(a_ms.h, SQ_A)
                e.tensor_scalar(bmax[0:1, 56 + pair:57 + pair], bmax[0:1, 48 + pair:49 + pair],
                                -SM_SCALE, None, ALU.mult).then_inc(v_ms.h, 1)
            blk.vector(body)
            v_ms.v += 1
            NB_V = v_ms.v

            def body(e, NB_V=NB_V, pair=pair):
                e.wait_ge(v_ms.h, NB_V)
                e.matmul(psB[:, 1024:1025], ones_f[0:1, :], bmax[0:1, 56 + pair:57 + pair],
                         start=True, stop=True).then_inc(t_ms.h, 1)
            blk.tensor(body)
            t_ms.v += 1
            NB_T = t_ms.v

            def body(e, NB_T=NB_T, pair=pair):
                e.wait_ge(t_ms.h, NB_T)
                e.tensor_copy(negb[:, pair:pair + 1], psB[:, 1024:1025]).then_inc(v_ms.h, 1)
            blk.vector(body)
            v_ms.v += 1
            NEGB_V = v_ms.v

            qk_t0 = t_ms.v
            ae_a0 = a_ms.v
            acc_v0 = v_ms.v
            vd0 = vd.v
            NKBS = [4 * (q + 1) for q in range(16)]
            klist = [(qb, j) for qb in range(16) for j in range(NKBS[qb])]
            CUM = []
            _c = 0
            for q in range(16):
                _c += NKBS[q]
                CUM.append(_c)
            NKI = len(klist)
            DLAG = 3
            NSL = 5
            NPT = 6
            def vb(qb):
                return acc_v0 + 6 * qb
            def mask_inc(ki):
                qb, j = klist[ki]
                return vb(qb) + (j - (NKBS[qb] - 4)) + 1
            def is_diag(ki):
                qb, j = klist[ki]
                return j >= NKBS[qb] - 4
            def acci(ki, vd0=vd0, klist=klist):
                return vd0 + ki + klist[ki][0] + 1
            def cmbi(qb, vd0=vd0, CUM=CUM):
                return vd0 + CUM[qb] + qb + 1
            sched = []
            for ki in range(NKI + DLAG):
                if ki < NKI:
                    sched.append(("qk", ki))
                kj = ki - DLAG
                if kj >= 0:
                    sched.append(("pv", kj))
                    qb, j = klist[kj]
                    if j == NKBS[qb] - 1:
                        sched.append(("dn", qb))
                        sched.append(("bc", qb))
            qkI, pvI, dnI, bcI = {}, {}, {}, {}
            for idx, (kind, a) in enumerate(sched):
                {"qk": qkI, "pv": pvI, "dn": dnI, "bc": bcI}[kind][a] = qk_t0 + idx + 1
            def st_slot(ki):
                s = ki % NSL
                return psA[:, s * 512:(s + 1) * 512] if s < 4 else psB[:, 1024:1536]

            def body(e, hh=hh, pair=pair, NEGB_V=NEGB_V, VT_V_END=VT_V_END,
                     vb=vb, mask_inc=mask_inc, st_slot=st_slot, ae_a0=ae_a0,
                     sched=sched, cmbi=cmbi):
                e.wait_ge(v_ms.h, NEGB_V)
                e.wait_ge(v_ms.h, VT_V_END)
                for kind, a in sched:
                    if kind == "qk":
                        ki = a
                        qb, j = klist[ki]
                        if ki >= NSL:
                            e.wait_ge(a_ms.h, ae_a0 + ki - NSL + 1)
                        e.matmul(st_slot(ki),
                                 kT2[:, hh * T + j * 128:hh * T + (j + 1) * 128],
                                 qT2[:, hh * T + qb * 512:hh * T + (qb + 1) * 512],
                                 start=True, stop=True).then_inc(t_ms.h, 1)
                    elif kind == "pv":
                        ki = a
                        qb, j = klist[ki]
                        nkb = NKBS[qb]
                        e.wait_ge(a_ms.h, ae_a0 + ki + 1)
                        if j >= nkb - 4:
                            e.wait_ge(v_ms.h, mask_inc(ki))
                        if j == 0 and qb >= 1:
                            e.wait_ge(v_ms.h, vb(qb))
                        buf = pT[ki % NPT]
                        e.matmul(psB[:, 0:512],
                                 v2[:, hh * T + j * 128:hh * T + (j + 1) * 128],
                                 buf[:], start=(j == 0), stop=(j == nkb - 1)
                                 ).then_inc(t_ms.h, 1)
                    elif kind == "dn":
                        qb = a
                        e.wait_ge(vd.h, cmbi(qb))
                        e.matmul(psB[0:1, 512:1024], ones_bf[:, 0:1], accT_bf[:],
                                 start=True, stop=True).then_inc(t_ms.h, 1)
                    else:
                        qb = a
                        e.wait_ge(v_ms.h, vb(qb) + 5)
                        e.matmul(psB[:, 1536:2048], ones_f[0:1, :], rden_sb[0:1, :],
                                 start=True, stop=True).then_inc(t_ms.h, 1)
            blk.tensor(body)
            t_ms.v = qk_t0 + len(sched)

            def body(e, pair=pair, qkI=qkI, pvI=pvI, mask_inc=mask_inc,
                     is_diag=is_diag, st_slot=st_slot, ae_a0=ae_a0, acci=acci):
                for ki in range(NKI):
                    e.wait_ge(t_ms.h, qkI[ki])
                    if ki >= NPT:
                        e.wait_ge(t_ms.h, pvI[ki - NPT])
                        e.wait_ge(vd.h, acci(ki - NPT))
                        if is_diag(ki - NPT):
                            e.wait_ge(v_ms.h, mask_inc(ki - NPT))
                    e.activation(pT[ki % NPT][:], st_slot(ki), AF.Exp,
                                 bias=negb[:, pair:pair + 1],
                                 scale=SM_SCALE).then_inc(a_ms.h, 1)
            blk.scalar(body)
            a_ms.v = ae_a0 + NKI

            gy_base = (gy[0].v, gy[1].v)
            VSF_ATT = v_sf.v
            def body(e, b=b, hh=hh, pair=pair, ae_a0=ae_a0, vb=vb, dnI=dnI, bcI=bcI,
                     gy_base=gy_base, VSF_ATT=VSF_ATT, acci=acci,
                     cmbi=cmbi):
                ki = 0
                for qb in range(16):
                    nkb = NKBS[qb]
                    for j in range(nkb):
                        e.wait_ge(a_ms.h, ae_a0 + ki + 1)
                        if j >= nkb - 4:
                            dj = j - (nkb - 4)
                            e.tensor_tensor(pT[ki % NPT][:], pT[ki % NPT][:],
                                            mask_bf[:, dj * 512:(dj + 1) * 512],
                                            ALU.mult).then_inc(v_ms.h, 1)
                        if j < 2:
                            if qb >= 1:
                                e.wait_ge(vd.h, cmbi(qb - 1))
                            e.tensor_copy(acc2[j][:], pT[ki % NPT][:]).then_inc(vd.h, 1)
                        else:
                            e.wait_ge(vd.h, acci(ki - 2))
                            e.tensor_tensor(acc2[j % 2][:], acc2[j % 2][:],
                                            pT[ki % NPT][:], ALU.add).then_inc(vd.h, 1)
                        ki += 1
                    e.wait_ge(vd.h, acci(ki - 1))
                    e.tensor_tensor(accT_bf[:], acc2[0][:], acc2[1][:],
                                    ALU.add).then_inc(vd.h, 1)
                    e.wait_ge(t_ms.h, dnI[qb])
                    e.tensor_copy(dtmp[0:1, 0:512], psB[0:1, 512:1024]).then_inc(v_sf.h, 1)
                    e.wait_ge(v_sf.h, VSF_ATT + 2 * qb + 1)
                    e.reciprocal(rden_sb[0:1, :], dtmp[0:1, 0:512]).then_inc(v_ms.h, 1)
                    e.wait_ge(t_ms.h, bcI[qb])
                    e.tensor_copy(bc_sb[:], psB[:, 1536:2048]).then_inc(v_sf.h, 1)
                    e.wait_ge(gy[qb % 2].h, gy_base[qb % 2] + 16 * (qb // 2))
                    e.wait_ge(v_sf.h, VSF_ATT + 2 * qb + 2)
                    e.tensor_tensor(yn_bf[qb % 2][:], psB[:, 0:512], bc_sb[:],
                                    ALU.mult).then_inc(v_ms.h, 1)
                    k = QSH[hh].get(qb) if b == 0 else None
                    if k is not None:
                        wsrc, r0, nr, c0, wd, dst, dr, wid = WT2[k]
                        e.wait_ge(g_wl.h, GLDQ0 + (k + 1) * 16)
                        if k == 0:
                            e.wait_ge(v_ms.h, V_RQ)
                        col = rq_sb[0:nr, wid:wid + 1]
                        e.tensor_scalar(wqa[0:nr, 0:wd], wqa[0:nr, 0:wd], col, RND_C,
                                        ALU.mult, ALU.add).then_inc(vq.h, 1)
                        e.wait_ge(vq.h, 3 * k + 1)
                        e.tensor_scalar(wqa[0:nr, 0:wd], wqa[0:nr, 0:wd], -RND_C, 1.0,
                                        ALU.add, ALU.min).then_inc(vq.h, 1)
                        e.wait_ge(vq.h, 3 * k + 2)
                        if k >= 2:
                            e.wait_ge(gq[k % 2].h, GQ_BASE[k % 2] + 16 * (k // 2))
                        e.tensor_scalar(qbf[k % 2][0:nr, 0:wd], wqa[0:nr, 0:wd], -1.0, None,
                                        ALU.max).then_inc(vq.h, 1)
            blk.vector(body)
            v_ms.v = vb(16)
            v_sf.v += 32
            vd.v = cmbi(15)
            if b == 0:
                vq.v += 3 * len(QSH[hh])

            def body(e, b=b, hh=hh, vb=vb):
                for qb in range(16):
                    e.wait_ge(v_ms.h, vb(qb) + 6)
                    jdst = 4 * b + qb // 4
                    row = 256 * jdst + 128 * hh
                    col = (qb % 4) * 512
                    e.dma_start(out=yt_c[row:row + 128, col:col + 512],
                                in_=yn_bf[qb % 2][:]).then_inc(gy[qb % 2].h, 16)
                    gy[qb % 2].v += 16
                    rk = RSH[hh].get(qb) if b == 0 else None
                    if rk is not None:
                        wsrc, r0, nr, c0, wd, dst, dr, wid = WT2[rk]
                        if rk == 0:
                            e.wait_ge(v_ms.h, V_FOLD)
                        else:
                            e.wait_ge(vq.h, 3 * (rk - 1) + 3)
                        e.dma_start(out=wqa[0:nr, 0:wd], in_=wsrc[r0:r0 + nr, c0:c0 + wd]
                                    ).then_inc(g_wl.h, 16)
                        g_wl.v += 16
                        if rk >= 1:
                            ws2, r02, nr2, c02, wd2, dst2, dr2, wid2 = WT2[rk - 1]
                            e.dma_start(out=dst2[dr2:dr2 + nr2, c02:c02 + wd2],
                                        in_=qbf[(rk - 1) % 2][0:nr2, 0:wd2]
                                        ).then_inc(gq[(rk - 1) % 2].h, 16)
                            gq[(rk - 1) % 2].v += 16
                        if rk == 6:
                            e.wait_ge(gq[0].h, GQ_BASE[0] + 48)
                            e.wait_ge(gq[1].h, GQ_BASE[1] + 48)
                            e.collective_compute("AllGather", ALU.bypass, replica_groups=rg,
                                                 ins=[w1qc.ap().opt()],
                                                 outs=[w1q_all.ap().opt()]).then_inc(cc3a.h)
                        if rk == 12:
                            e.wait_ge(gq[0].h, GQ_BASE[0] + 96)
                            e.wait_ge(gq[1].h, GQ_BASE[1] + 96)
                            e.collective_compute("AllGather", ALU.bypass, replica_groups=rg,
                                                 ins=[w2qc.ap().opt()],
                                                 outs=[w2q_all.ap().opt()]).then_inc(cc3b.h)
            blk.gpsimd(body)

        if b == 0 and hh == 1:
            def body(e):
                k = NWT2 - 1
                (ws2, r02, nr2, c02, wd2, dst2, dr2, wid2) = WT2[k]
                e.wait_ge(vq.h, 3 * k + 3)
                e.dma_start(out=dst2[dr2:dr2 + nr2, c02:c02 + wd2],
                            in_=qbf[k % 2][0:nr2, 0:wd2]).then_inc(gq[k % 2].h, 16)
                gq[k % 2].v += 16
                e.wait_ge(gq[0].h, gq[0].v)
                e.wait_ge(gq[1].h, gq[1].v)
                e.collective_compute("AllGather", ALU.bypass, replica_groups=rg,
                                     ins=[w3qc.ap().opt()], outs=[w3q_all.ap().opt()]).then_inc(cc3c.h)
            blk.gpsimd(body)
            cc3a.v = cc3b.v = cc3c.v = 1

    def body(e):
        e.wait_ge(gy[0].h, gy[0].v)
        e.wait_ge(gy[1].h, gy[1].v)
        with allow_ici_all_to_all():
            e.collective_compute("AllToAll", ALU.bypass, replica_groups=rg,
                                 ins=[yt_c.ap().opt()], outs=[yt_my.ap().opt()]).then_inc(cc2.h)
    blk.gpsimd(body)
    cc2.v = 1

    barrier()
    p1.close()
    pW.close()

    p2 = contextlib.ExitStack()
    def sb2(name, shape, dt):
        return p2.enter_context(nc.sbuf_tensor(name, shape, dt))
    NHT = HID // 128
    NHQ = NHT // 4
    hidT = sb2("hidT", [128, NHT * WIN], BF16)
    xnT2 = sb2("xnT2", [128, 16 * WIN], BF16)
    wp1 = [sb2("wp1a", [128, 16 * 512], BF16), sb2("wp1b", [128, 16 * 512], BF16)]
    wp2 = [sb2("wp2a", [128, 16 * 512], BF16), sb2("wp2b", [128, 16 * 512], BF16)]
    w3t = [sb2(f"w3t{i}", [128, 512], BF16) for i in range(4)]
    sgm = sb2("sgm", [128, WIN], F32)
    gbf = [sb2(f"gbf{i}", [128, 512], BF16) for i in range(2)]

    W3_T_PREV = [0]
    X1_PREV = [0]
    for half in range(2):
        STP_H = s_tp.v
        X1P_H = X1_PREV[0]
        X1T0 = t_ms.v
        HIDV0 = v_ms.v
        SGA0 = a_ms.v

        SWP_BASE = (swp[0].v, swp[1].v)
        SWQ_BASE = (swq[0].v, swq[1].v)
        def body(e, half=half, X1T0=X1T0, X1P=X1P_H):
            e.wait_ge(cc3a.h, 1)
            e.wait_ge(cc3b.h, 1)
            for hq in range(NHQ):
                if hq >= 2:
                    e.wait_ge(t_ms.h, X1T0 + (hq - 1) * 4)
                for cchunk in range(16):
                    e.dma_start(out=wp1[hq % 2][:, cchunk * 512:(cchunk + 1) * 512],
                                in_=w1q_all[hq * 512:(hq + 1) * 512,
                                            cchunk * 128:(cchunk + 1) * 128],
                                transpose=True).then_inc(swp[hq % 2].h, 16)
                    e.dma_start(out=wp2[hq % 2][:, cchunk * 512:(cchunk + 1) * 512],
                                in_=w2q_all[hq * 512:(hq + 1) * 512,
                                            cchunk * 128:(cchunk + 1) * 128],
                                transpose=True).then_inc(swp[hq % 2].h, 16)
                swp[hq % 2].v += 512
        blk.sync(body)

        def body(e, half=half, X1P=X1P_H):
            if half == 1:
                e.wait_ge(t_ms.h, X1P)
            for cchunk in range(16):
                for sub in range(2):
                    src = xn_cq[2 * half + sub][cchunk * 128:(cchunk + 1) * 128, :]
                    e.dma_start(out=xnT2[:, cchunk * WIN + sub * 512:
                                         cchunk * WIN + (sub + 1) * 512], in_=src
                                ).then_inc(swq[0].h, 16)
        blk.scalar(body)
        swq[0].v += 512

        def body(e, half=half, STP_H=STP_H, SWP_BASE=SWP_BASE, HIDV0=HIDV0,
                 W3P=W3_T_PREV[0]):
            e.wait_ge(swq[0].h, SWQ_BASE[0] + 512)
            if half == 1:
                e.wait_ge(v_ms.h, HIDV0)
            for hq in range(NHQ):
                e.wait_ge(swp[hq % 2].h, SWP_BASE[hq % 2] + 512 * (hq // 2 + 1))
                for ht in range(4):
                    hti = hq * 4 + ht
                    pspar = psA if hti % 2 == 0 else psB
                    x1ps = pspar[:, 0:1024]
                    x2ps = pspar[:, 1024:2048]
                    if hti >= 2:
                        e.wait_ge(v_ms.h, HIDV0 + hti - 1)
                    last = None
                    for cchunk in range(16):
                        lhs1 = wp1[hq % 2][:, cchunk * 512 + ht * 128: cchunk * 512 + ht * 128 + 128]
                        lhs2 = wp2[hq % 2][:, cchunk * 512 + ht * 128: cchunk * 512 + ht * 128 + 128]
                        for nn in range(2):
                            e.matmul(x1ps[:, nn * 512:(nn + 1) * 512], lhs1,
                                     xnT2[:, cchunk * WIN + nn * 512:cchunk * WIN + (nn + 1) * 512],
                                     start=(cchunk == 0), stop=(cchunk == 15))
                        for nn in range(2):
                            last = e.matmul(x2ps[:, nn * 512:(nn + 1) * 512], lhs2,
                                     xnT2[:, cchunk * WIN + nn * 512:cchunk * WIN + (nn + 1) * 512],
                                     start=(cchunk == 0), stop=(cchunk == 15))
                    last.then_inc(t_ms.h, 1)
        blk.tensor(body)
        t_ms.v += NHT
        X1_PREV[0] = t_ms.v

        def wp2load(e, hq):
            for cchunk in range(16):
                e.dma_start(out=wp2[hq % 2][:, cchunk * 512:(cchunk + 1) * 512],
                            in_=w2q_all[hq * 512:(hq + 1) * 512,
                                        cchunk * 128:(cchunk + 1) * 128],
                            transpose=True).then_inc(swq[hq % 2].h, 16)
            swq[hq % 2].v += 256

        def body(e, half=half, X1T0=X1T0, HIDV0=HIDV0, wp2load=wp2load,
                 X1P=X1P_H):
            for hti in range(NHT):
                pspar = psA if hti % 2 == 0 else psB
                e.wait_ge(t_ms.h, X1T0 + hti + 1)
                if hti >= 1:
                    e.wait_ge(v_ms.h, HIDV0 + hti)
                e.activation(sgm[:], pspar[:, 0:1024], AF.Sigmoid).then_inc(a_ms.h, 1)
        blk.scalar(body)
        a_ms.v += NHT

        def body(e, SGA0=SGA0, HIDV0=HIDV0):
            for hti in range(NHT):
                pspar = psA if hti % 2 == 0 else psB
                e.wait_ge(a_ms.h, SGA0 + hti + 1)
                e.tensor_tensor(hidT[:, hti * WIN:(hti + 1) * WIN], sgm[:],
                                pspar[:, 1024:2048], ALU.mult).then_inc(v_ms.h, 1)
        blk.vector(body)
        v_ms.v += NHT
        HID_END_V = v_ms.v

        W3T0 = t_ms.v
        W3A0 = a_ms.v
        W3V0 = v_ms.v

        SW3_BASE = tuple(s.v for s in sw3)
        def body(e, W3T0=W3T0, HID_END_V=HID_END_V):
            e.wait_ge(v_ms.h, HID_END_V)
            e.wait_ge(cc3c.h, 1)
            for cq in range(4):
                for hc in range(NHT):
                    idx = cq * NHT + hc
                    if idx >= 4:
                        e.wait_ge(t_ms.h, W3T0 + idx - 3)
                    slot = idx % 4
                    e.dma_start(out=w3t[slot][:],
                                in_=w3q_all[cq * 512:(cq + 1) * 512,
                                            hc * 128:(hc + 1) * 128],
                                transpose=True).then_inc(sw3[slot].h, 16)
                    sw3[slot].v += 16
        blk.sync(body)

        def body(e, W3T0=W3T0, W3V0=W3V0, SW3_BASE=SW3_BASE):
            for cq in range(4):
                for hc in range(NHT):
                    idx = cq * NHT + hc
                    slot = idx % 4
                    e.wait_ge(sw3[slot].h, SW3_BASE[slot] + 16 * (idx // 4 + 1))
                    last = None
                    for tt in range(8):
                        if hc == 0 and cq > 0:
                            e.wait_ge(v_ms.h, W3V0 + (cq - 1) * 8 + tt + 1)
                        ps = (psA if tt < 4 else psB)[:, (tt % 4) * 512:(tt % 4 + 1) * 512]
                        last = e.matmul(ps, hidT[:, hc * WIN + tt * 128: hc * WIN + (tt + 1) * 128],
                                 w3t[slot][:, :], start=(hc == 0), stop=(hc == NHT - 1))
                    last.then_inc(t_ms.h, 1)
        blk.tensor(body)
        t_ms.v += 4 * NHT
        W3_T_PREV[0] = t_ms.v

        def body(e, W3T0=W3T0, W3V0=W3V0):
            for cq in range(4):
                e.wait_ge(t_ms.h, W3T0 + (cq + 1) * NHT)
                for tt in range(8):
                    idx = cq * 8 + tt
                    if idx >= 1:
                        e.wait_ge(v_ms.h, W3V0 + idx)
                    ps = (psA if tt < 4 else psB)[:, (tt % 4) * 512:(tt % 4 + 1) * 512]
                    e.activation(sgm[:, 0:512], ps, AF.Sigmoid).then_inc(a_ms.h, 1)
        blk.scalar(body)
        a_ms.v += 32

        GG_BASE = (gg[0].v, gg[1].v)
        def body(e, W3A0=W3A0, W3V0=W3V0, GG_BASE=GG_BASE):
            for cq in range(4):
                for tt in range(8):
                    idx = cq * 8 + tt
                    ps = (psA if tt < 4 else psB)[:, (tt % 4) * 512:(tt % 4 + 1) * 512]
                    e.wait_ge(a_ms.h, W3A0 + idx + 1)
                    e.wait_ge(gg[idx % 2].h, GG_BASE[idx % 2] + 16 * (idx // 2))
                    e.tensor_tensor(gbf[idx % 2][:], sgm[:, 0:512], ps,
                                    ALU.mult).then_inc(v_ms.h, 1)
        blk.vector(body)
        v_ms.v += 32

        def body(e, half=half, W3V0=W3V0):
            for cq in range(4):
                for tt in range(8):
                    idx = cq * 8 + tt
                    e.wait_ge(v_ms.h, W3V0 + idx + 1)
                    e.dma_start(out=g_sc[half * WIN + tt * 128: half * WIN + (tt + 1) * 128,
                                         cq * 512:(cq + 1) * 512],
                                in_=gbf[idx % 2][:]).then_inc(gg[idx % 2].h, 16)
                    gg[idx % 2].v += 16
        blk.gpsimd(body)

    barrier()
    p2.close()

    p3 = contextlib.ExitStack()
    def sb3(name, shape, dt):
        return p3.enter_context(nc.sbuf_tensor(name, shape, dt))
    yT_sb = sb3("yT_sb", [128, 16 * 2048], BF16)
    wo_sb = sb3("wo_sb", [128, 16 * 2048], BF16)
    bo_row = sb3("bo_row", [1, 2048], F32)
    bobc = sb3("bobc", [128, 2048], F32)
    xt3 = [sb3("xt3a", [128, 2048], F32), sb3("xt3b", [128, 2048], F32)]
    gt3 = [sb3("gt3a", [128, 2048], BF16), sb3("gt3b", [128, 2048], BF16)]
    s_bf = [sb3("sbfa", [128, 2048], BF16), sb3("sbfb", [128, 2048], BF16)]
    sg_f = sb3("sg_f", [128, 2048], F32)

    P3G0 = g_ld.v
    P3S0 = s_ld.v
    P3T0 = t_ms.v
    P3V0 = v_ms.v
    P3GS0 = g_st.v

    def body(e):
        e.wait_ge(g_st.h, WO_CAST_G)
        e.dma_start(out=wo_sb[:, 0:8192],
                    in_=wo_bf[0:512, :]
                    .rearrange("(a p) n -> p a n", p=128)).then_inc(g_yw.h, 16)
        e.wait_ge(cc2.h, 1)
        for q in range(2, 4):
            e.dma_start(out=yT_sb[:, q * 8192:(q + 1) * 8192],
                        in_=yt_my[q * 512:(q + 1) * 512, :]
                        .rearrange("(a p) t -> p a t", p=128)).then_inc(g_yw.h, 16)
    blk.gpsimd(body)
    g_yw.v += 3 * 16
    YWO_G = g_yw.v

    SWQ3_BASE = swq[1].v
    def body(e):
        e.wait_ge(g_st.h, WO_CAST_G)
        for q in range(1, 4):
            e.dma_start(out=wo_sb[:, q * 8192:(q + 1) * 8192],
                        in_=wo_bf[q * 512:(q + 1) * 512, :]
                        .rearrange("(a p) n -> p a n", p=128)).then_inc(swq[1].h, 16)
        for tt in range(16):
            if tt >= 2:
                e.wait_ge(v_ms.h, P3V0 + 1 + 3 * (tt - 2) + 3)
            e.dma_start(out=gt3[tt % 2][:], in_=g_sc[tt * 128:(tt + 1) * 128, :]
                        ).then_inc(gt3s[tt % 2].h, 16)
            gt3s[tt % 2].v += 16
    blk.scalar(body)
    swq[1].v += 3 * 16

    def body(e):
        e.dma_start(out=bo_row[:], in_=bo_in[:, :]).then_inc(s_ld.h, 16)
        s_ld.v += 16
        e.wait_ge(cc2.h, 1)
        for q in range(2):
            e.dma_start(out=yT_sb[:, q * 8192:(q + 1) * 8192],
                        in_=yt_my[q * 512:(q + 1) * 512, :]
                        .rearrange("(a p) t -> p a t", p=128)).then_inc(s_ld.h, 16)
        s_ld.v += 2 * 16
        for tt in range(16):
            if tt >= 2:
                e.wait_ge(go[tt % 2].h, 16 * (tt // 2))
            e.dma_start(out=xt3[tt % 2][:], in_=x_my[tt * 128:(tt + 1) * 128, :]
                        ).then_inc(sx3[tt % 2].h, 16)
            sx3[tt % 2].v += 16
    blk.sync(body)

    def body(e):
        e.wait_ge(s_ld.h, P3S0 + 16)
        last = None
        for nn in range(4):
            last = e.matmul(psB[:, nn * 512:(nn + 1) * 512], ones_f[0:1, :],
                     bo_row[0:1, nn * 512:(nn + 1) * 512], start=True, stop=True)
        last.then_inc(t_ms.h, 1)
        e.wait_ge(g_yw.h, YWO_G)
        e.wait_ge(swq[1].h, SWQ3_BASE + 3 * 16)
        e.wait_ge(s_ld.h, P3S0 + 3 * 16)
        for tt in range(16):
            pspar = psA if tt % 2 == 0 else psB
            if tt == 1:
                e.wait_ge(v_ms.h, P3V0 + 1)
            if tt >= 2:
                e.wait_ge(v_ms.h, P3V0 + 1 + 3 * (tt - 2) + 1)
            last = None
            for cchunk in range(16):
                lhsT = yT_sb[:, cchunk * 2048 + tt * 128: cchunk * 2048 + tt * 128 + 128]
                for nn in range(4):
                    last = e.matmul(pspar[:, nn * 512:(nn + 1) * 512], lhsT,
                             wo_sb[:, cchunk * 2048 + nn * 512: cchunk * 2048 + (nn + 1) * 512],
                             start=(cchunk == 0), stop=(cchunk == 15))
            last.then_inc(t_ms.h, 1)
    blk.tensor(body)
    t_ms.v += 17

    def body(e):
        e.wait_ge(t_ms.h, P3T0 + 1)
        e.tensor_copy(bobc[:], psB[:, 0:2048]).then_inc(v_ms.h, 1)
        e.wait_ge(v_ms.h, P3V0 + 1)
        sfb3 = v_sf.v
        for tt in range(16):
            pspar = psA if tt % 2 == 0 else psB
            e.wait_ge(t_ms.h, P3T0 + tt + 2)
            e.tensor_tensor(s_bf[tt % 2][:], pspar[:, 0:2048], bobc[:],
                            ALU.add).then_inc(v_ms.h, 1)
            e.wait_ge(gt3s[tt % 2].h, 16 * (tt // 2 + 1))
            e.wait_ge(v_ms.h, P3V0 + 1 + 3 * tt + 1)
            e.tensor_tensor(sg_f[:], s_bf[tt % 2][:], gt3[tt % 2][:],
                            ALU.mult).then_inc(v_sf.h, 1)
            e.wait_ge(sx3[tt % 2].h, 16 * (tt // 2 + 1))
            e.wait_ge(v_sf.h, sfb3 + tt + 1)
            e.tensor_tensor(xt3[tt % 2][:], sg_f[:], xt3[tt % 2][:],
                            ALU.add).then_inc(v_ms.h, 2)
    blk.vector(body)
    v_ms.v += 1 + 3 * 16
    v_sf.v += 16

    def body(e):
        for tt in range(16):
            e.wait_ge(v_ms.h, P3V0 + 1 + 3 * tt + 3)
            e.dma_start(out=out_my[tt * 128:(tt + 1) * 128, :], in_=xt3[tt % 2][:]
                        ).then_inc(go[tt % 2].h, 16)
            go[tt % 2].v += 16
        e.wait_ge(go[0].h, go[0].v)
        e.wait_ge(go[1].h, go[1].v)
    blk.gpsimd(body)

    if debug_taps:
        tap_list = [("t_xn", xn_cq[0], [C, 512], BF16), ("t_gam", gam_s, [1, 4], F32),
                    ("t_w1q", w1qc, [RH, C], BF16), ("t_w3q", w3qc, [256, HID], BF16),
                    ("t_ytc", yt_c, [2048, 2048], BF16), ("t_ytm", yt_my, [2048, 2048], BF16),
                    ("t_g", g_sc, [MYT, C], BF16), ("t_xna", xn_aq[0], [NCORES * C, 512], BF16)]
        tap_out = {nm: nc.dram_tensor(nm, shp, dtp, kind="ExternalOutput")
                   for nm, _, shp, dtp in tap_list}
        def body(e):
            e.wait_ge(go[0].h, go[0].v)
            e.wait_ge(go[1].h, go[1].v)
            for nm, t, shp, dtp in tap_list:
                nrow = shp[0]
                step = max(1, 8192 * 2048 // max(1, int(np.prod(shp)))) if False else 512
                r = 0
                while r < nrow:
                    rr = min(step, nrow - r)
                    e.dma_start(out=tap_out[nm][r:r + rr, :], in_=t[r:r + rr, :]
                                ).then_inc(g_st.h, 16)
                    g_st.v += 16
                    r += rr
            e.wait_ge(g_st.h, g_st.v)
        blk.gpsimd(body)

    p3.close()
    ctx.close()
    return nc


def _make_mask():
    s = np.arange(128)[:, None]
    t = np.arange(512)[None, :]
    m = np.zeros((128, 2048), dtype=np.float32)
    for jj in range(4):
        m[:, jj * 512:(jj + 1) * 512] = (jj * 128 + s <= t).astype(np.float32)
    return m


def kernel(L0_attn_x, L0_attn_Wq, L0_attn_bq, L0_attn_Wk, L0_attn_bk, L0_attn_Wv, L0_attn_bv,
           L0_attn_Wo, L0_attn_bo, L0_ffn_w1, L0_ffn_w2, L0_ffn_w3):
    x = np.ascontiguousarray(np.asarray(L0_attn_x, dtype=np.float32))
    Wq = np.asarray(L0_attn_Wq, dtype=np.float32)
    Wk = np.asarray(L0_attn_Wk, dtype=np.float32)
    Wv = np.asarray(L0_attn_Wv, dtype=np.float32)
    bq = np.asarray(L0_attn_bq, dtype=np.float32)
    bk = np.asarray(L0_attn_bk, dtype=np.float32)
    bv = np.asarray(L0_attn_bv, dtype=np.float32)
    Wo = np.ascontiguousarray(np.asarray(L0_attn_Wo, dtype=np.float32))
    bo = np.asarray(L0_attn_bo, dtype=np.float32).reshape(1, C)
    w1 = np.asarray(L0_ffn_w1, dtype=np.float32)
    w2 = np.asarray(L0_ffn_w2, dtype=np.float32)
    w3 = np.asarray(L0_ffn_w3, dtype=np.float32)

    ident = np.eye(128, dtype=np.float32)
    ones = np.ones((128, 128), dtype=np.float32)
    mask = _make_mask()

    nc = build_nc()
    in_maps = []
    for c in range(NCORES):
        b = c // 4
        t0 = (c % 4) * MYT
        in_maps.append(dict(
            x_my=np.ascontiguousarray(x[b, t0:t0 + MYT, :]),
            wq_my=np.ascontiguousarray(Wq[2 * c:2 * c + 2]),
            wk_my=np.ascontiguousarray(Wk[2 * c:2 * c + 2]),
            wv_my=np.ascontiguousarray(Wv[2 * c:2 * c + 2]),
            bq_my=np.ascontiguousarray(bq[2 * c:2 * c + 2].T),
            bk_my=np.ascontiguousarray(bk[2 * c:2 * c + 2].T),
            bv_my=np.ascontiguousarray(bv[2 * c:2 * c + 2].T),
            wo_in=Wo,
            bo_in=bo,
            w1_my=np.ascontiguousarray(w1[RH * c:RH * (c + 1)]),
            w2_my=np.ascontiguousarray(w2[RH * c:RH * (c + 1)]),
            w3_my=np.ascontiguousarray(w3[256 * c:256 * (c + 1)]),
            ident_in=ident,
            ones_in=ones,
            mask_in=mask,
        ))
    res = run_bass_kernel_spmd(nc, in_maps, core_ids=list(range(NCORES)))
    out = np.empty((B, T, C), dtype=np.float32)
    for c in range(NCORES):
        b = c // 4
        t0 = (c % 4) * MYT
        out[b, t0:t0 + MYT, :] = np.asarray(res.results[c]["out_my"], dtype=np.float32)
    return out


# version 22
# speedup vs baseline: 1.0960x; 1.0131x over previous
import contextlib
import numpy as np
import ml_dtypes
import concourse.bass as bass
import concourse.mybir as mybir
from concourse.bass_utils import run_bass_kernel_spmd
from concourse.aot_env import allow_ici_all_to_all

F32 = mybir.dt.float32
BF16 = mybir.dt.bfloat16
AF = mybir.ActivationFunctionType
ALU = mybir.AluOpType
AX = mybir.AxisListType

NCORES = 8
B, T, C = 2, 8192, 2048
H, DH = 16, 128
HID, RH, PH = 5632, 704, 768
HPAD = NCORES * PH
MYT = 2048
NT = MYT // 128
SM_SCALE = float(DH) ** -0.5
RND_C = 1.5 * 2.0 ** 23
EPS_LN = 1e-5
EPS_Q = 1e-8
WCNT = float(HID * C)
WIN = 1024
NW = T // WIN
NTILE_W12 = 6


class Ctr:
    __slots__ = ("h", "v")
    def __init__(self, h):
        self.h = h
        self.v = 0

    def inc(self, ins, k=1):
        ins.then_inc(self.h, k)
        self.v += k
        return self.v


def w12_rows(j):
    r0 = j * 128
    return r0, min(RH, r0 + 128) - r0


def build_nc(debug_taps=False):
    nc = bass.Bass("TRN2", target_bir_lowering=False, detect_race_conditions=True)

    x_my = nc.dram_tensor("x_my", [MYT, C], F32, kind="ExternalInput")
    wq_my = nc.dram_tensor("wq_my", [2, C, DH], F32, kind="ExternalInput")
    wk_my = nc.dram_tensor("wk_my", [2, C, DH], F32, kind="ExternalInput")
    wv_my = nc.dram_tensor("wv_my", [2, C, DH], F32, kind="ExternalInput")
    bq_my = nc.dram_tensor("bq_my", [DH, 2], F32, kind="ExternalInput")
    bk_my = nc.dram_tensor("bk_my", [DH, 2], F32, kind="ExternalInput")
    bv_my = nc.dram_tensor("bv_my", [DH, 2], F32, kind="ExternalInput")
    wo_in = nc.dram_tensor("wo_in", [C, C], F32, kind="ExternalInput")
    bo_in = nc.dram_tensor("bo_in", [1, C], F32, kind="ExternalInput")
    w1_my = nc.dram_tensor("w1_my", [RH, C], F32, kind="ExternalInput")
    w2_my = nc.dram_tensor("w2_my", [RH, C], F32, kind="ExternalInput")
    w3_my = nc.dram_tensor("w3_my", [256, HID], F32, kind="ExternalInput")
    ident_in = nc.dram_tensor("ident_in", [128, 128], F32, kind="ExternalInput")
    ones_in = nc.dram_tensor("ones_in", [128, 128], F32, kind="ExternalInput")
    mask_in = nc.dram_tensor("mask_in", [128, 2048], F32, kind="ExternalInput")
    out_my = nc.dram_tensor("out_my", [MYT, C], F32, kind="ExternalOutput")

    xn_cq = [nc.dram_tensor(f"xn_cq{q}", [C, 512], BF16, kind="Internal") for q in range(4)]
    xn_aq = [nc.dram_tensor(f"xn_aq{q}", [NCORES * C, 512], BF16, kind="Internal") for q in range(4)]
    gam_c = nc.dram_tensor("gam_c", [1, 4], F32, kind="Internal")
    gam_s = nc.dram_tensor("gam_s", [1, 4], F32, kind="Internal")
    w1qc = nc.dram_tensor("w1qc", [RH, C], BF16, kind="Internal")
    w2qc = nc.dram_tensor("w2qc", [RH, C], BF16, kind="Internal")
    w3qc = nc.dram_tensor("w3qc", [256, HID], BF16, kind="Internal")
    w1q_all = nc.dram_tensor("w1q_all", [NCORES * RH, C], BF16, kind="Internal")
    w2q_all = nc.dram_tensor("w2q_all", [NCORES * RH, C], BF16, kind="Internal")
    w3q_all = nc.dram_tensor("w3q_all", [NCORES * 256, HID], BF16, kind="Internal")
    wo_bf = nc.dram_tensor("wo_bf", [C, C], BF16, kind="Internal")
    yt_c = nc.dram_tensor("yt_c", [2048, 2048], BF16, kind="Internal")
    yt_my = nc.dram_tensor("yt_my", [2048, 2048], BF16, kind="Internal")
    g_sc = nc.dram_tensor("g_sc", [MYT, C], BF16, kind="Internal")

    rg = [list(range(NCORES))]
    ctx = contextlib.ExitStack()

    def sem(name):
        return Ctr(ctx.enter_context(nc.semaphore(name)))

    def sb(name, shape, dt):
        return ctx.enter_context(nc.sbuf_tensor(name, shape, dt))

    ident_bf = sb("ident_bf", [128, 128], BF16)
    ident_f = sb("ident_f", [128, 128], F32)
    ones_f = sb("ones_f", [128, 128], F32)
    ones_bf = sb("ones_bf", [128, 1], BF16)
    stat = sb("stat", [128, 64], F32)
    grow = sb("grow", [1, 16], F32)
    rq_sb = sb("rq_sb", [128, 4], F32)
    negb = sb("negb", [128, 4], F32)
    bmax = sb("bmax", [1, 64], F32)
    biasT = sb("biasT", [128, 6], F32)

    psA = ctx.enter_context(nc.psum_tensor("psA", [128, 2048], F32))
    psB = ctx.enter_context(nc.psum_tensor("psB", [128, 2048], F32))

    s_ld = sem("s_ld")
    g_ld = sem("g_ld")
    s_tp = sem("s_tp")
    v_ms = sem("v_ms")
    a_ms = sem("a_ms")
    t_ms = sem("t_ms")
    g_st = sem("g_st")
    cc0 = sem("cc0"); cc2 = sem("cc2")
    ccq = [sem(f"ccq{q}") for q in range(4)]
    cc3a = sem("cc3a"); cc3b = sem("cc3b"); cc3c = sem("cc3c")
    g_c = sem("g_c")
    v_sf = sem("v_sf")
    a_sf = sem("a_sf")
    sx = [sem("sx0"), sem("sx1")]
    gxn = [sem("gxn0"), sem("gxn1")]
    gy = [sem("gy0"), sem("gy1")]
    gg = [sem("gg0"), sem("gg1")]
    sw3 = [sem(f"sw3{i}") for i in range(4)]
    g_yw = sem("g_yw")
    swp = [sem("swp0"), sem("swp1")]
    stw = [sem("stw0"), sem("stw1")]
    gq = [sem("gq0"), sem("gq1")]
    vq = sem("vq")
    vd = sem("vd")
    g_wl = sem("g_wl")
    swq = [sem("swq0"), sem("swq1")]
    gt3s = [sem("gt3s0"), sem("gt3s1")]
    sx3 = [sem("sx3a"), sem("sx3b")]
    go = [sem("go0"), sem("go1")]
    bar = sem("bar")

    blk = ctx.enter_context(nc.Block())
    barn = [0]

    def barrier():
        barn[0] += 1
        target = 5 * barn[0]
        caps = dict(sync=(s_ld, s_tp, sx[0], sx[1], sw3[0], sw3[1], sw3[2], sw3[3], sx3[0], sx3[1], swp[0], swp[1], stw[0], stw[1]),
                    gpsimd=(g_ld, g_st, g_c, gxn[0], gxn[1], gy[0], gy[1], gg[0], gg[1],
                            g_yw, gt3s[0], gt3s[1], go[0], go[1], gq[0], gq[1], g_wl),
                    tensor=(t_ms,), scalar=(a_ms, a_sf, swq[0], swq[1]), vector=(v_ms, v_sf, vq, vd))
        for name, dec in (("sync", blk.sync), ("gpsimd", blk.gpsimd), ("tensor", blk.tensor),
                          ("scalar", blk.scalar), ("vector", blk.vector)):
            def body(e, name=name):
                for s in caps[name]:
                    e.wait_ge(s.h, s.v)
                e.sem_inc(bar.h, 1)
                e.wait_ge(bar.h, target)
            dec(body)
        bar.v = target

    WQW = 2816
    WT2 = []
    for j in range(NTILE_W12):
        r0, nr = w12_rows(j)
        WT2.append((w1_my, r0, nr, 0, C, w1qc, r0, 0))
    for j in range(NTILE_W12):
        r0, nr = w12_rows(j)
        WT2.append((w2_my, r0, nr, 0, C, w2qc, r0, 1))
    for j in range(2):
        for hf in range(2):
            WT2.append((w3_my, j * 128, 128, hf * WQW, HID - hf * WQW if hf else WQW,
                        w3qc, j * 128, 2))
    NWT2 = len(WT2)

    pW = contextlib.ExitStack()
    def sbW(name, shape, dt):
        return pW.enter_context(nc.sbuf_tensor(name, shape, dt))
    wqa = sbW("wqa", [128, WQW], F32)
    qbf = [sbW("qbf0", [128, WQW], BF16), sbW("qbf1", [128, WQW], BF16)]
    wqkv = sbW("wqkv", [128, 6 * C], BF16)
    mask_bf = sbW("mask_bf", [128, 2048], BF16)

    pLN = contextlib.ExitStack()
    def sbL(name, shape, dt):
        return pLN.enter_context(nc.sbuf_tensor(name, shape, dt))
    xt = [sbL("xt0", [128, C], F32), sbL("xt1", [128, C], F32)]
    xsq = sbL("xsq", [128, C], F32)
    xnf = [sbL("xnf0", [128, C], F32), sbL("xnf1", [128, C], F32)]
    xn_st = [sbL("xnst0", [128, 16 * 512], BF16), sbL("xnst1", [128, 16 * 512], BF16)]

    def body(e):
        e.dma_start(out=ident_bf[:], in_=ident_in[:, :]).then_inc(g_c.h, 16)
        e.dma_start(out=ones_f[:], in_=ones_in[:, :]).then_inc(g_c.h, 16)
        e.dma_start(out=ones_bf[:], in_=ones_in[:, 0:1], allow_slow_non_contiguous=True).then_inc(g_c.h, 16)
        e.dma_start(out=biasT[:, 0:2], in_=bq_my[:, :]).then_inc(g_c.h, 16)
        e.dma_start(out=biasT[:, 2:4], in_=bk_my[:, :]).then_inc(g_c.h, 16)
        e.dma_start(out=biasT[:, 4:6], in_=bv_my[:, :]).then_inc(g_c.h, 16)
        e.dma_start(out=ident_f[:], in_=ident_in[:, :]).then_inc(g_c.h, 16)
        e.dma_start(out=mask_bf[:], in_=mask_in[:, :]).then_inc(g_ld.h, 16)
        g_ld.v += 16
        for idx, wsrc in enumerate((wq_my, wk_my, wv_my)):
            for hh in range(2):
                base = (idx * 2 + hh) * C
                for q in range(4):
                    e.dma_start(out=wqkv[:, base + q * 512:base + (q + 1) * 512],
                                in_=wsrc[hh, q * 512:(q + 1) * 512, :]
                                .rearrange("(a p) d -> p a d", p=128)
                                ).then_inc(g_ld.h, 16)
                    g_ld.v += 16
        wsrc0, r0, nr, c0, wd, _, _, _ = WT2[0]
        e.dma_start(out=wqa[0:nr, 0:wd], in_=wsrc0[r0:r0 + nr, c0:c0 + wd]
                    ).then_inc(g_wl.h, 16)
        g_wl.v += 16
    blk.gpsimd(body)
    g_c.v = 7 * 16
    GC_ALL = g_c.v
    GLD_QKV = g_ld.v

    VLN0 = v_ms.v
    ALN0 = a_ms.v
    TLN0 = t_ms.v
    GXN_B = (gxn[0].v, gxn[1].v)

    def body(e):
        for i in range(NT):
            if i >= 2:
                e.wait_ge(v_ms.h, VLN0 + 6 * (i - 2) + 4)
            e.dma_start(out=xt[i % 2][:], in_=x_my[i * 128:(i + 1) * 128, :]
                        ).then_inc(sx[i % 2].h, 16)
            sx[i % 2].v += 16
    blk.sync(body)

    def body(e):
        for i in range(NT):
            e.wait_ge(sx[i % 2].h, 16 * (i // 2 + 1))
            if i >= 1:
                e.wait_ge(v_ms.h, VLN0 + 6 * (i - 1) + 4)
            e.activation(xsq[:], xt[i % 2][:], AF.Square).then_inc(a_ms.h, 1)
            e.wait_ge(v_ms.h, VLN0 + 6 * i + 3)
            e.activation(stat[:, 56:57], stat[:, 16 + i:17 + i], AF.Ln).then_inc(a_ms.h, 1)
            e.wait_ge(a_ms.h, ALN0 + 5 * i + 2)
            e.activation(stat[:, 52 + i % 2:53 + i % 2], stat[:, 56:57], AF.Exp,
                         scale=-0.5).then_inc(a_ms.h, 1)
            di = i // 4
            psX = psA if i % 2 == 0 else psB
            for q in (0, 1):
                e.wait_ge(t_ms.h, TLN0 + 16 * i + 4 * (q + 1))
                if di >= 2 and i % 4 == 0 and q == 0:
                    e.wait_ge(gxn[di % 2].h, GXN_B[di % 2] + 64 * (di // 2))
                e.activation(xn_st[di % 2][:, (i % 4) * 2048 + q * 512:
                                           (i % 4) * 2048 + (q + 1) * 512],
                             psX[:, q * 512:(q + 1) * 512], AF.Identity,
                             bias=stat[:, 60:61]).then_inc(a_ms.h, 1)
    blk.scalar(body)
    a_ms.v += 5 * NT

    def body(e):
        sfb = v_sf.v
        e.memset(stat[:, 60:61], 0.0).then_inc(v_sf.h, 1)
        sfb += 1
        for i in range(NT):
            e.wait_ge(sx[i % 2].h, 16 * (i // 2 + 1))
            e.reduce_sum(stat[:, i:i + 1], xt[i % 2][:], axis=AX.X).then_inc(v_ms.h, 1)
            e.wait_ge(a_ms.h, ALN0 + 5 * i + 1)
            e.reduce_sum(stat[:, 16 + i:17 + i], xsq[:], axis=AX.X).then_inc(v_ms.h, 1)
            e.wait_ge(v_ms.h, VLN0 + 6 * i + 2)
            e.tensor_scalar(stat[:, i:i + 1], stat[:, i:i + 1], -1.0 / C, None,
                            ALU.mult).then_inc(v_sf.h, 1)
            e.wait_ge(v_sf.h, sfb + 4 * i + 1)
            e.tensor_tensor(stat[:, 54:55], stat[:, i:i + 1], stat[:, i:i + 1],
                            ALU.mult).then_inc(v_sf.h, 1)
            e.tensor_scalar(stat[:, 16 + i:17 + i], stat[:, 16 + i:17 + i], 1.0 / C, EPS_LN,
                            ALU.mult, ALU.add).then_inc(v_sf.h, 1)
            e.wait_ge(v_sf.h, sfb + 4 * i + 3)
            e.tensor_tensor(stat[:, 16 + i:17 + i], stat[:, 16 + i:17 + i], stat[:, 54:55],
                            ALU.subtract).then_inc(v_ms.h, 1)
            e.wait_ge(a_ms.h, ALN0 + 5 * i + 3)
            e.tensor_scalar(xsq[:], xt[i % 2][:], stat[:, i:i + 1], None,
                            ALU.add).then_inc(v_sf.h, 1)
            e.wait_ge(v_sf.h, sfb + 4 * i + 4)
            if i >= 2:
                e.wait_ge(t_ms.h, TLN0 + 16 * (i - 1))
            e.tensor_scalar(xnf[i % 2][:], xsq[:], stat[:, 52 + i % 2:53 + i % 2], None,
                            ALU.mult).then_inc(v_ms.h, 1)
            di = i // 4
            psX = psA if i % 2 == 0 else psB
            for q in (2, 3):
                e.wait_ge(t_ms.h, TLN0 + 16 * i + 4 * (q + 1))
                if di >= 2 and i % 4 == 0 and q == 2:
                    e.wait_ge(gxn[di % 2].h, GXN_B[di % 2] + 64 * (di // 2))
                e.tensor_copy(xn_st[di % 2][:, (i % 4) * 2048 + q * 512:
                                            (i % 4) * 2048 + (q + 1) * 512],
                              psX[:, q * 512:(q + 1) * 512]).then_inc(v_ms.h, 1)
    blk.vector(body)
    v_ms.v += 6 * NT
    v_sf.v += 4 * NT + 1

    def body(e):
        e.wait_ge(g_c.h, GC_ALL)
        for i in range(NT):
            e.wait_ge(v_ms.h, VLN0 + 6 * i + 4)
            psX = psA if i % 2 == 0 else psB
            for q in range(4):
                if i >= 2:
                    if q < 2:
                        e.wait_ge(a_ms.h, ALN0 + 5 * (i - 2) + 4 + q)
                    else:
                        e.wait_ge(v_ms.h, VLN0 + 6 * (i - 2) + 3 + q)
                for j in range(4):
                    cc = 4 * q + j
                    e.transpose(psX[:, q * 512 + j * 128:q * 512 + (j + 1) * 128],
                                xnf[i % 2][:, cc * 128:(cc + 1) * 128],
                                ident_f[:]).then_inc(t_ms.h, 1)
    blk.tensor(body)
    t_ms.v += 16 * NT

    def body(e):
        for di in range(NT // 4):
            xo = xn_cq[di].ap().rearrange("(a p) t -> p a t", p=128)
            for f in range(4):
                e.wait_ge(a_ms.h, ALN0 + 5 * (4 * di + f + 1))
                e.wait_ge(v_ms.h, VLN0 + 6 * (4 * di + f + 1))
                e.dma_start(out=xo[:, :, f * 128:(f + 1) * 128],
                            in_=xn_st[di % 2][:, f * 2048:(f + 1) * 2048]
                            .rearrange("p (a t) -> p a t", a=16)
                            ).then_inc(gxn[di % 2].h, 16)
                gxn[di % 2].v += 16
            e.wait_ge(gxn[di % 2].h, gxn[di % 2].v)
            e.collective_compute("AllGather", ALU.bypass, replica_groups=rg,
                                 ins=[xn_cq[di].ap().opt()],
                                 outs=[xn_aq[di].ap().opt()]).then_inc(ccq[di].h)
            ccq[di].v = 1
    blk.gpsimd(body)

    barrier()
    pLN.close()

    p1 = contextlib.ExitStack()
    def sb1(name, shape, dt):
        return p1.enter_context(nc.sbuf_tensor(name, shape, dt))
    xnT = [sb1("xnTa", [128, 16 * 512], BF16), sb1("xnTb", [128, 16 * 512], BF16)]
    qT2 = sb1("qT2", [128, 2 * T], BF16)
    kT2 = sb1("kT2", [128, 2 * T], BF16)
    v2 = sb1("v2", [128, 2 * T], BF16)
    vTs = sb1("vTs", [128, 512], F32)
    pT = [sb1(f"pT{i}", [128, 512], BF16) for i in range(8)]
    acc2 = [sb1("acc0", [128, 512], F32), sb1("acc1", [128, 512], F32)]
    accT_bf = sb1("accT", [128, 512], BF16)
    rden_sb = sb1("rden_sb", [1, 512], F32)
    dtmp = sb1("dtmp", [1, 1024], F32)
    bc_sb = sb1("bc_sb", [128, 512], F32)
    yn_bf = [sb1(f"yn{i}", [128, 512], BF16) for i in range(2)]

    VT_B0 = v_ms.v
    def body(e):
        for k, (wsrc, r0, nr, c0, wd, dst, dr, wid) in enumerate(WT2):
            if k == 0:
                continue
            e.wait_ge(v_ms.h, VT_B0 + 9 * k)
            e.dma_start(out=wqa[0:nr, 0:wd], in_=wsrc[r0:r0 + nr, c0:c0 + wd]
                        ).then_inc(g_wl.h, 16)
            g_wl.v += 16
    blk.gpsimd(body)

    def body(e):
        e.dma_start(out=wo_bf[:, :], in_=wo_in[:, :]).then_inc(g_st.h, 16)
    blk.gpsimd(body)
    g_st.v += 16
    WO_CAST_G = g_st.v
    GLDQ0 = g_wl.v
    GQ_BASE = (gq[0].v, gq[1].v)
    V_SF_FOLD = None
    V_FOLD = None
    QSH = ({qb: qb - 6 for qb in range(6, 16)}, {qb: 10 + qb for qb in range(6)})
    RSH = ({qb: qb - 5 for qb in range(5, 16)}, {qb: 11 + qb for qb in range(5)})

    for b in range(B):
        PWIN = 512
        NPW = T // PWIN
        projT0 = t_ms.v
        projA0 = a_ms.v
        vtV0 = v_ms.v
        STP0 = s_tp.v

        STW_BASE = (stw[0].v, stw[1].v)
        worder = ([r * 4 + q for q in range(4) for r in range(4)] if b == 0
                  else list(range(NPW)))
        def body(e, b=b, projT0=projT0, worder=worder):
            for p in range(NPW):
                w = worder[p]
                if p <= 1:
                    if b == 1:
                        e.wait_ge(t_ms.h, projT0)
                else:
                    e.wait_ge(t_ms.h, projT0 + (p - 2) * 14 + 6)
                if b == 0 and p % 4 == 0:
                    e.wait_ge(ccq[p // 4].h, 1)
                gtok = b * T + w * PWIN
                r = gtok // MYT
                cq = (gtok % MYT) // PWIN
                for cchunk in range(16):
                    src = xn_aq[cq][r * C + cchunk * 128: r * C + (cchunk + 1) * 128, :]
                    e.dma_start(out=xnT[p % 2][:, cchunk * PWIN:(cchunk + 1) * PWIN], in_=src
                                ).then_inc(stw[p % 2].h, 16)
                stw[p % 2].v += 256
        blk.sync(body)

        def body(e, b=b, projT0=projT0, projA0=projA0, vtV0=vtV0, STW_BASE=STW_BASE,
                 worder=worder):
            if b == 0:
                e.wait_ge(g_ld.h, GLD_QKV)
            e.wait_ge(a_ms.h, projA0)
            e.wait_ge(v_ms.h, vtV0)
            for p in range(NPW):
                w = worder[p]
                e.wait_ge(stw[p % 2].h, STW_BASE[p % 2] + 256 * (p // 2 + 1))
                for rnd in range(6):
                    gr = p * 6 + rnd
                    qtr = rnd % 4
                    ps = psA[:, qtr * 512:(qtr + 1) * 512]
                    if gr >= 4:
                        e.wait_ge(a_ms.h, projA0 + gr - 3)
                    kind, hh = divmod(rnd, 2)
                    last = None
                    for cchunk in range(16):
                        lhsT = wqkv[:, (kind * 2 + hh) * C + cchunk * 128:
                                    (kind * 2 + hh) * C + (cchunk + 1) * 128]
                        last = e.matmul(ps, lhsT,
                                 xnT[p % 2][:, cchunk * PWIN:(cchunk + 1) * PWIN],
                                 start=(cchunk == 0), stop=(cchunk == 15))
                    last.then_inc(t_ms.h, 1)
                for hh in range(2):
                    e.wait_ge(a_ms.h, projA0 + p * 6 + 4 + hh + 1)
                    for j in range(4):
                        tki = p * 8 + hh * 4 + j
                        if tki >= 2:
                            tp = tki - 2
                            idx = 9 * (tp // 8) + tp % 8 + 1 if b == 0 else tp + 1
                            e.wait_ge(v_ms.h, vtV0 + idx)
                        psT = psB[:, 0:128] if tki % 2 == 0 else psB[:, 512:640]
                        e.transpose(psT, vTs[:, j * 128:(j + 1) * 128],
                                    ident_f[:]).then_inc(t_ms.h, 1)
        blk.tensor(body)
        t_ms.v += NPW * 14

        def body(e, b=b, projT0=projT0, projA0=projA0, worder=worder):
            for p in range(NPW):
                w = worder[p]
                for rnd in range(6):
                    kind, hh = divmod(rnd, 2)
                    e.wait_ge(t_ms.h, projT0 + p * 14 + rnd + 1)
                    if kind == 2 and hh == 1:
                        e.wait_ge(t_ms.h, projT0 + p * 14 + 6 + 4)
                    bcol = biasT[:, kind * 2 + hh:kind * 2 + hh + 1]
                    if kind == 0:
                        dst = qT2[:, hh * T + w * PWIN: hh * T + (w + 1) * PWIN]
                    elif kind == 1:
                        dst = kT2[:, hh * T + w * PWIN: hh * T + (w + 1) * PWIN]
                    else:
                        dst = vTs[:, 0:PWIN]
                    qtr = rnd % 4
                    ps = psA[:, qtr * 512:(qtr + 1) * 512]
                    e.activation(dst, ps, AF.Identity, bias=bcol).then_inc(a_ms.h, 1)
        blk.scalar(body)
        a_ms.v += NPW * 6

        def body(e, b=b, projT0=projT0, worder=worder):
            for p in range(NPW):
                w = worder[p]
                for hh in range(2):
                    for j in range(4):
                        tki = p * 8 + hh * 4 + j
                        psT = psB[:, 0:128] if tki % 2 == 0 else psB[:, 512:640]
                        e.wait_ge(t_ms.h, projT0 + p * 14 + 6 + hh * 4 + j + 1)
                        kb = w * 4 + j
                        e.tensor_copy(v2[:, hh * T + kb * 128:hh * T + (kb + 1) * 128],
                                      psT).then_inc(v_ms.h, 1)
                if b == 0:
                    k = p
                    wsrc, r0, nr, c0, wd, dst, dr, wid = WT2[k]
                    e.wait_ge(g_wl.h, (k + 1) * 16)
                    e.reduce_sum(stat[0:nr, 32 + k:33 + k], wqa[0:nr, 0:wd], axis=AX.X,
                                 apply_absolute_value=True).then_inc(v_ms.h, 1)
                    if nr < 128:
                        e.memset(stat[64:128, 32 + k:33 + k], 0.0)
            if b == 0:
                e.wait_ge(v_ms.h, vtV0 + NPW * 9)
                e.reduce_sum(stat[:, 48:49], stat[:, 32:38], axis=AX.X).then_inc(v_sf.h, 1)
                e.reduce_sum(stat[:, 49:50], stat[:, 38:44], axis=AX.X).then_inc(v_sf.h, 1)
                e.memset(stat[:, 51:52], 0.0).then_inc(v_sf.h, 1)
                e.reduce_sum(stat[:, 50:51], stat[:, 44:48], axis=AX.X).then_inc(v_ms.h, 1)
        blk.vector(body)
        if b == 0:
            assert vtV0 == VT_B0
            v_ms.v += NPW * 9 + 1
            v_sf.v += 3
            V_SF_FOLD = v_sf.v
            V_FOLD = v_ms.v
        else:
            v_ms.v += NPW * 8
        if b == 0:
            VTE_INS = v_ms.v
            def body(e, VTE_INS=VTE_INS):
                e.wait_ge(v_ms.h, V_FOLD)
                e.wait_ge(v_sf.h, V_SF_FOLD)
                e.wait_ge(v_ms.h, VTE_INS)
                e.matmul(psB[0:1, 0:4], ones_f[:, 0:1], stat[:, 48:52], start=True,
                         stop=True).then_inc(t_ms.h, 1)
            blk.tensor(body)
            t_ms.v += 1
            T_GAM = t_ms.v
            def body(e, T_GAM=T_GAM):
                e.wait_ge(t_ms.h, T_GAM)
                e.tensor_copy(grow[0:1, 0:4], psB[0:1, 0:4]).then_inc(v_ms.h, 1)
            blk.vector(body)
            v_ms.v += 1
            V_GROW = v_ms.v
            def body(e, V_GROW=V_GROW):
                e.wait_ge(v_ms.h, V_GROW)
                e.dma_start(out=gam_c[:, :], in_=grow[0:1, 0:4]).then_inc(g_st.h, 16)
                e.wait_ge(g_st.h, g_st.v + 16)
                e.collective_compute("AllReduce", ALU.add, replica_groups=rg,
                                     ins=[gam_c.ap().opt()],
                                     outs=[gam_s.ap().opt()]).then_inc(cc0.h)
                e.wait_ge(cc0.h, 1)
                e.dma_start(out=grow[0:1, 8:12], in_=gam_s[:, :]).then_inc(g_ld.h, 16)
            blk.gpsimd(body)
            g_st.v += 16
            g_ld.v += 16
            GLD_GAMS = g_ld.v
            cc0.v = 1
            def body(e, GLD_GAMS=GLD_GAMS):
                e.wait_ge(g_ld.h, GLD_GAMS)
                e.tensor_scalar(grow[0:1, 12:16], grow[0:1, 8:12], 1.0 / WCNT, EPS_Q,
                                ALU.mult, ALU.add).then_inc(v_sf.h, 1)
                e.wait_ge(v_sf.h, v_sf.v + 1)
                e.reciprocal(grow[0:1, 4:8], grow[0:1, 12:16]).then_inc(v_ms.h, 1)
            blk.vector(body)
            v_ms.v += 1
            v_sf.v += 1
            V_RECIP = v_ms.v
            def body(e, V_RECIP=V_RECIP):
                e.wait_ge(v_ms.h, V_RECIP)
                e.matmul(psB[:, 8:12], ones_f[0:1, :], grow[0:1, 4:8], start=True,
                         stop=True).then_inc(t_ms.h, 1)
            blk.tensor(body)
            t_ms.v += 1
            T_RQ = t_ms.v
            def body(e, T_RQ=T_RQ):
                e.wait_ge(t_ms.h, T_RQ)
                e.tensor_copy(rq_sb[:], psB[:, 8:12]).then_inc(v_ms.h, 1)
            blk.vector(body)
            v_ms.v += 1
            V_RQ = v_ms.v

        PROJ_A_END = a_ms.v
        VT_V_END = v_ms.v

        for hh in range(2):
            pair = b * 2 + hh
            TKD = t_ms.v
            def body(e, hh=hh, TKD=TKD):
                e.wait_ge(a_ms.h, PROJ_A_END)
                e.wait_ge(t_ms.h, TKD)
                e.tensor_tensor(xnT[0][:, 0:T], qT2[:, hh * T:(hh + 1) * T],
                                qT2[:, hh * T:(hh + 1) * T], ALU.mult).then_inc(v_ms.h, 1)
            blk.vector(body)
            v_ms.v += 1
            QSQ_V = v_ms.v

            BQ_T0 = t_ms.v
            def body(e, QSQ_V=QSQ_V):
                e.wait_ge(v_ms.h, QSQ_V)
                for ch in range(16):
                    if ch >= 2:
                        e.wait_ge(v_ms.h, QSQ_V + ch - 1)
                    ps = psB[0:1, 1536:2048] if ch % 2 == 0 else psB[0:1, 1024:1536]
                    e.matmul(ps, ones_bf[:, 0:1], xnT[0][:, ch * 512:(ch + 1) * 512],
                             start=True, stop=True).then_inc(t_ms.h, 1)
            blk.tensor(body)
            t_ms.v += 16

            def body(e, BQ_T0=BQ_T0, pair=pair, QSQ_V=QSQ_V):
                for ch in range(16):
                    ps = psB[0:1, 1536:2048] if ch % 2 == 0 else psB[0:1, 1024:1536]
                    e.wait_ge(t_ms.h, BQ_T0 + ch + 1)
                    e.reduce_max(bmax[0:1, ch:ch + 1], ps, axis=AX.X).then_inc(v_ms.h, 1)
                e.wait_ge(v_ms.h, QSQ_V + 16)
                e.reduce_max(bmax[0:1, 32 + pair:33 + pair], bmax[0:1, 0:16],
                             axis=AX.X).then_inc(v_ms.h, 1)
            blk.vector(body)
            v_ms.v += 17

            QD = t_ms.v
            def body(e, hh=hh, QD=QD):
                e.wait_ge(t_ms.h, QD)
                e.tensor_tensor(xnT[0][:, 0:T], kT2[:, hh * T:(hh + 1) * T],
                                kT2[:, hh * T:(hh + 1) * T], ALU.mult).then_inc(v_ms.h, 1)
            blk.vector(body)
            v_ms.v += 1
            KSQ_V = v_ms.v

            BK_T0 = t_ms.v
            def body(e, KSQ_V=KSQ_V):
                e.wait_ge(v_ms.h, KSQ_V)
                for ch in range(16):
                    if ch >= 2:
                        e.wait_ge(v_ms.h, KSQ_V + ch - 1)
                    ps = psB[0:1, 1536:2048] if ch % 2 == 0 else psB[0:1, 1024:1536]
                    e.matmul(ps, ones_bf[:, 0:1], xnT[0][:, ch * 512:(ch + 1) * 512],
                             start=True, stop=True).then_inc(t_ms.h, 1)
            blk.tensor(body)
            t_ms.v += 16

            def body(e, BK_T0=BK_T0, pair=pair, KSQ_V=KSQ_V):
                for ch in range(16):
                    ps = psB[0:1, 1536:2048] if ch % 2 == 0 else psB[0:1, 1024:1536]
                    e.wait_ge(t_ms.h, BK_T0 + ch + 1)
                    e.reduce_max(bmax[0:1, ch:ch + 1], ps, axis=AX.X).then_inc(v_ms.h, 1)
                e.wait_ge(v_ms.h, KSQ_V + 16)
                e.reduce_max(bmax[0:1, 40 + pair:41 + pair], bmax[0:1, 0:16],
                             axis=AX.X).then_inc(v_sf.h, 1)
                e.wait_ge(v_sf.h, v_sf.v + 1)
                e.tensor_tensor(bmax[0:1, 48 + pair:49 + pair], bmax[0:1, 32 + pair:33 + pair],
                                bmax[0:1, 40 + pair:41 + pair], ALU.mult).then_inc(v_ms.h, 1)
            blk.vector(body)
            v_ms.v += 17
            v_sf.v += 1
            PRD_V = v_ms.v

            def body(e, pair=pair, PRD_V=PRD_V):
                e.wait_ge(v_ms.h, PRD_V)
                e.activation(bmax[0:1, 56 + pair:57 + pair], bmax[0:1, 48 + pair:49 + pair],
                             AF.Ln).then_inc(a_sf.h, 1)
                e.wait_ge(a_sf.h, a_sf.v + 1)
                e.activation(bmax[0:1, 48 + pair:49 + pair], bmax[0:1, 56 + pair:57 + pair],
                             AF.Exp, scale=0.5).then_inc(a_ms.h, 1)
            blk.scalar(body)
            a_ms.v += 1
            a_sf.v += 1
            SQ_A = a_ms.v

            def body(e, pair=pair, SQ_A=SQ_A):
                e.wait_ge(a_ms.h, SQ_A)
                e.tensor_scalar(bmax[0:1, 56 + pair:57 + pair], bmax[0:1, 48 + pair:49 + pair],
                                -SM_SCALE, None, ALU.mult).then_inc(v_ms.h, 1)
            blk.vector(body)
            v_ms.v += 1
            NB_V = v_ms.v

            def body(e, NB_V=NB_V, pair=pair):
                e.wait_ge(v_ms.h, NB_V)
                e.matmul(psB[:, 1024:1025], ones_f[0:1, :], bmax[0:1, 56 + pair:57 + pair],
                         start=True, stop=True).then_inc(t_ms.h, 1)
            blk.tensor(body)
            t_ms.v += 1
            NB_T = t_ms.v

            def body(e, NB_T=NB_T, pair=pair):
                e.wait_ge(t_ms.h, NB_T)
                e.tensor_copy(negb[:, pair:pair + 1], psB[:, 1024:1025]).then_inc(v_ms.h, 1)
            blk.vector(body)
            v_ms.v += 1
            NEGB_V = v_ms.v

            qk_t0 = t_ms.v
            ae_a0 = a_ms.v
            acc_v0 = v_ms.v
            vd0 = vd.v
            NKBS = [4 * (q + 1) for q in range(16)]
            klist = [(qb, j) for qb in range(16) for j in range(NKBS[qb])]
            CUM = []
            _c = 0
            for q in range(16):
                _c += NKBS[q]
                CUM.append(_c)
            NKI = len(klist)
            DLAG = 3
            NSL = 5
            NPT = 8
            def vb(qb):
                return acc_v0 + 6 * qb
            def mask_inc(ki):
                qb, j = klist[ki]
                return vb(qb) + (j - (NKBS[qb] - 4)) + 1
            def is_diag(ki):
                qb, j = klist[ki]
                return j >= NKBS[qb] - 4
            def acci(ki, vd0=vd0, klist=klist):
                return vd0 + ki + klist[ki][0] + 1
            def cmbi(qb, vd0=vd0, CUM=CUM):
                return vd0 + CUM[qb] + qb + 1
            sched = []
            for ki in range(NKI + DLAG):
                if ki < NKI:
                    sched.append(("qk", ki))
                kj = ki - DLAG
                if kj >= 0:
                    sched.append(("pv", kj))
                    qb, j = klist[kj]
                    if j == NKBS[qb] - 1:
                        sched.append(("dn", qb))
                        sched.append(("bc", qb))
            qkI, pvI, dnI, bcI = {}, {}, {}, {}
            for idx, (kind, a) in enumerate(sched):
                {"qk": qkI, "pv": pvI, "dn": dnI, "bc": bcI}[kind][a] = qk_t0 + idx + 1
            def st_slot(ki):
                s = ki % NSL
                return psA[:, s * 512:(s + 1) * 512] if s < 4 else psB[:, 1024:1536]

            def body(e, hh=hh, pair=pair, NEGB_V=NEGB_V, VT_V_END=VT_V_END,
                     vb=vb, mask_inc=mask_inc, st_slot=st_slot, ae_a0=ae_a0,
                     sched=sched, cmbi=cmbi):
                e.wait_ge(v_ms.h, NEGB_V)
                e.wait_ge(v_ms.h, VT_V_END)
                for kind, a in sched:
                    if kind == "qk":
                        ki = a
                        qb, j = klist[ki]
                        if ki >= NSL:
                            e.wait_ge(a_ms.h, ae_a0 + ki - NSL + 1)
                        e.matmul(st_slot(ki),
                                 kT2[:, hh * T + j * 128:hh * T + (j + 1) * 128],
                                 qT2[:, hh * T + qb * 512:hh * T + (qb + 1) * 512],
                                 start=True, stop=True).then_inc(t_ms.h, 1)
                    elif kind == "pv":
                        ki = a
                        qb, j = klist[ki]
                        nkb = NKBS[qb]
                        e.wait_ge(a_ms.h, ae_a0 + ki + 1)
                        if j >= nkb - 4:
                            e.wait_ge(v_ms.h, mask_inc(ki))
                        if j == 0 and qb >= 1:
                            e.wait_ge(v_ms.h, vb(qb))
                        buf = pT[ki % NPT]
                        e.matmul(psB[:, 0:512],
                                 v2[:, hh * T + j * 128:hh * T + (j + 1) * 128],
                                 buf[:], start=(j == 0), stop=(j == nkb - 1)
                                 ).then_inc(t_ms.h, 1)
                    elif kind == "dn":
                        qb = a
                        e.wait_ge(vd.h, cmbi(qb))
                        e.matmul(psB[0:1, 512:1024], ones_bf[:, 0:1], accT_bf[:],
                                 start=True, stop=True).then_inc(t_ms.h, 1)
                    else:
                        qb = a
                        e.wait_ge(v_ms.h, vb(qb) + 5)
                        e.matmul(psB[:, 1536:2048], ones_f[0:1, :], rden_sb[0:1, :],
                                 start=True, stop=True).then_inc(t_ms.h, 1)
            blk.tensor(body)
            t_ms.v = qk_t0 + len(sched)

            def body(e, pair=pair, qkI=qkI, pvI=pvI, mask_inc=mask_inc,
                     is_diag=is_diag, st_slot=st_slot, ae_a0=ae_a0, acci=acci):
                for ki in range(NKI):
                    e.wait_ge(t_ms.h, qkI[ki])
                    if ki >= NPT:
                        e.wait_ge(t_ms.h, pvI[ki - NPT])
                        e.wait_ge(vd.h, acci(ki - NPT))
                        if is_diag(ki - NPT):
                            e.wait_ge(v_ms.h, mask_inc(ki - NPT))
                    e.activation(pT[ki % NPT][:], st_slot(ki), AF.Exp,
                                 bias=negb[:, pair:pair + 1],
                                 scale=SM_SCALE).then_inc(a_ms.h, 1)
            blk.scalar(body)
            a_ms.v = ae_a0 + NKI

            gy_base = (gy[0].v, gy[1].v)
            VSF_ATT = v_sf.v
            def body(e, b=b, hh=hh, pair=pair, ae_a0=ae_a0, vb=vb, dnI=dnI, bcI=bcI,
                     gy_base=gy_base, VSF_ATT=VSF_ATT, acci=acci,
                     cmbi=cmbi):
                ki = 0
                for qb in range(16):
                    nkb = NKBS[qb]
                    for j in range(nkb):
                        e.wait_ge(a_ms.h, ae_a0 + ki + 1)
                        if j >= nkb - 4:
                            dj = j - (nkb - 4)
                            e.tensor_tensor(pT[ki % NPT][:], pT[ki % NPT][:],
                                            mask_bf[:, dj * 512:(dj + 1) * 512],
                                            ALU.mult).then_inc(v_ms.h, 1)
                        if j < 2:
                            if qb >= 1:
                                e.wait_ge(vd.h, cmbi(qb - 1))
                            e.tensor_copy(acc2[j][:], pT[ki % NPT][:]).then_inc(vd.h, 1)
                        else:
                            e.wait_ge(vd.h, acci(ki - 2))
                            e.tensor_tensor(acc2[j % 2][:], acc2[j % 2][:],
                                            pT[ki % NPT][:], ALU.add).then_inc(vd.h, 1)
                        ki += 1
                    e.wait_ge(vd.h, acci(ki - 1))
                    e.tensor_tensor(accT_bf[:], acc2[0][:], acc2[1][:],
                                    ALU.add).then_inc(vd.h, 1)
                    e.wait_ge(t_ms.h, dnI[qb])
                    e.tensor_copy(dtmp[0:1, 0:512], psB[0:1, 512:1024]).then_inc(v_sf.h, 1)
                    e.wait_ge(v_sf.h, VSF_ATT + 2 * qb + 1)
                    e.reciprocal(rden_sb[0:1, :], dtmp[0:1, 0:512]).then_inc(v_ms.h, 1)
                    e.wait_ge(t_ms.h, bcI[qb])
                    e.tensor_copy(bc_sb[:], psB[:, 1536:2048]).then_inc(v_sf.h, 1)
                    e.wait_ge(gy[qb % 2].h, gy_base[qb % 2] + 16 * (qb // 2))
                    e.wait_ge(v_sf.h, VSF_ATT + 2 * qb + 2)
                    e.tensor_tensor(yn_bf[qb % 2][:], psB[:, 0:512], bc_sb[:],
                                    ALU.mult).then_inc(v_ms.h, 1)
                    k = QSH[hh].get(qb) if b == 0 else None
                    if k is not None:
                        wsrc, r0, nr, c0, wd, dst, dr, wid = WT2[k]
                        e.wait_ge(g_wl.h, GLDQ0 + (k + 1) * 16)
                        if k == 0:
                            e.wait_ge(v_ms.h, V_RQ)
                        col = rq_sb[0:nr, wid:wid + 1]
                        e.tensor_scalar(wqa[0:nr, 0:wd], wqa[0:nr, 0:wd], col, RND_C,
                                        ALU.mult, ALU.add).then_inc(vq.h, 1)
                        e.wait_ge(vq.h, 3 * k + 1)
                        e.tensor_scalar(wqa[0:nr, 0:wd], wqa[0:nr, 0:wd], -RND_C, 1.0,
                                        ALU.add, ALU.min).then_inc(vq.h, 1)
                        e.wait_ge(vq.h, 3 * k + 2)
                        if k >= 2:
                            e.wait_ge(gq[k % 2].h, GQ_BASE[k % 2] + 16 * (k // 2))
                        e.tensor_scalar(qbf[k % 2][0:nr, 0:wd], wqa[0:nr, 0:wd], -1.0, None,
                                        ALU.max).then_inc(vq.h, 1)
            blk.vector(body)
            v_ms.v = vb(16)
            v_sf.v += 32
            vd.v = cmbi(15)
            if b == 0:
                vq.v += 3 * len(QSH[hh])

            def body(e, b=b, hh=hh, vb=vb):
                for qb in range(16):
                    e.wait_ge(v_ms.h, vb(qb) + 6)
                    jdst = 4 * b + qb // 4
                    row = 256 * jdst + 128 * hh
                    col = (qb % 4) * 512
                    e.dma_start(out=yt_c[row:row + 128, col:col + 512],
                                in_=yn_bf[qb % 2][:]).then_inc(gy[qb % 2].h, 16)
                    gy[qb % 2].v += 16
                    rk = RSH[hh].get(qb) if b == 0 else None
                    if rk is not None:
                        wsrc, r0, nr, c0, wd, dst, dr, wid = WT2[rk]
                        if rk == 0:
                            e.wait_ge(v_ms.h, V_FOLD)
                        else:
                            e.wait_ge(vq.h, 3 * (rk - 1) + 3)
                        e.dma_start(out=wqa[0:nr, 0:wd], in_=wsrc[r0:r0 + nr, c0:c0 + wd]
                                    ).then_inc(g_wl.h, 16)
                        g_wl.v += 16
                        if rk >= 1:
                            ws2, r02, nr2, c02, wd2, dst2, dr2, wid2 = WT2[rk - 1]
                            e.dma_start(out=dst2[dr2:dr2 + nr2, c02:c02 + wd2],
                                        in_=qbf[(rk - 1) % 2][0:nr2, 0:wd2]
                                        ).then_inc(gq[(rk - 1) % 2].h, 16)
                            gq[(rk - 1) % 2].v += 16
                        if rk == 6:
                            e.wait_ge(gq[0].h, GQ_BASE[0] + 48)
                            e.wait_ge(gq[1].h, GQ_BASE[1] + 48)
                            e.collective_compute("AllGather", ALU.bypass, replica_groups=rg,
                                                 ins=[w1qc.ap().opt()],
                                                 outs=[w1q_all.ap().opt()]).then_inc(cc3a.h)
                        if rk == 12:
                            e.wait_ge(gq[0].h, GQ_BASE[0] + 96)
                            e.wait_ge(gq[1].h, GQ_BASE[1] + 96)
                            e.collective_compute("AllGather", ALU.bypass, replica_groups=rg,
                                                 ins=[w2qc.ap().opt()],
                                                 outs=[w2q_all.ap().opt()]).then_inc(cc3b.h)
            blk.gpsimd(body)

        if b == 0 and hh == 1:
            def body(e):
                k = NWT2 - 1
                (ws2, r02, nr2, c02, wd2, dst2, dr2, wid2) = WT2[k]
                e.wait_ge(vq.h, 3 * k + 3)
                e.dma_start(out=dst2[dr2:dr2 + nr2, c02:c02 + wd2],
                            in_=qbf[k % 2][0:nr2, 0:wd2]).then_inc(gq[k % 2].h, 16)
                gq[k % 2].v += 16
                e.wait_ge(gq[0].h, gq[0].v)
                e.wait_ge(gq[1].h, gq[1].v)
                e.collective_compute("AllGather", ALU.bypass, replica_groups=rg,
                                     ins=[w3qc.ap().opt()], outs=[w3q_all.ap().opt()]).then_inc(cc3c.h)
            blk.gpsimd(body)
            cc3a.v = cc3b.v = cc3c.v = 1

    def body(e):
        e.wait_ge(gy[0].h, gy[0].v)
        e.wait_ge(gy[1].h, gy[1].v)
        with allow_ici_all_to_all():
            e.collective_compute("AllToAll", ALU.bypass, replica_groups=rg,
                                 ins=[yt_c.ap().opt()], outs=[yt_my.ap().opt()]).then_inc(cc2.h)
    blk.gpsimd(body)
    cc2.v = 1

    barrier()
    p1.close()
    pW.close()

    p2 = contextlib.ExitStack()
    def sb2(name, shape, dt):
        return p2.enter_context(nc.sbuf_tensor(name, shape, dt))
    NHT = HID // 128
    NHQ = NHT // 4
    hidT = sb2("hidT", [128, NHT * WIN], BF16)
    xnT2 = sb2("xnT2", [128, 16 * WIN], BF16)
    wp1 = [sb2("wp1a", [128, 16 * 512], BF16), sb2("wp1b", [128, 16 * 512], BF16)]
    wp2 = [sb2("wp2a", [128, 16 * 512], BF16), sb2("wp2b", [128, 16 * 512], BF16)]
    w3t = [sb2(f"w3t{i}", [128, 512], BF16) for i in range(4)]
    sgm = sb2("sgm", [128, WIN], F32)
    gbf = [sb2(f"gbf{i}", [128, 512], BF16) for i in range(2)]

    W3_T_PREV = [0]
    X1_PREV = [0]
    for half in range(2):
        STP_H = s_tp.v
        X1P_H = X1_PREV[0]
        X1T0 = t_ms.v
        HIDV0 = v_ms.v
        SGA0 = a_ms.v

        SWP_BASE = (swp[0].v, swp[1].v)
        SWQ_BASE = (swq[0].v, swq[1].v)
        def body(e, half=half, X1T0=X1T0, X1P=X1P_H):
            e.wait_ge(cc3a.h, 1)
            e.wait_ge(cc3b.h, 1)
            for hq in range(NHQ):
                if hq >= 2:
                    e.wait_ge(t_ms.h, X1T0 + (hq - 1) * 4)
                for cchunk in range(16):
                    e.dma_start(out=wp1[hq % 2][:, cchunk * 512:(cchunk + 1) * 512],
                                in_=w1q_all[hq * 512:(hq + 1) * 512,
                                            cchunk * 128:(cchunk + 1) * 128],
                                transpose=True).then_inc(swp[hq % 2].h, 16)
                    e.dma_start(out=wp2[hq % 2][:, cchunk * 512:(cchunk + 1) * 512],
                                in_=w2q_all[hq * 512:(hq + 1) * 512,
                                            cchunk * 128:(cchunk + 1) * 128],
                                transpose=True).then_inc(swp[hq % 2].h, 16)
                swp[hq % 2].v += 512
        blk.sync(body)

        def body(e, half=half, X1P=X1P_H):
            if half == 1:
                e.wait_ge(t_ms.h, X1P)
            for cchunk in range(16):
                for sub in range(2):
                    src = xn_cq[2 * half + sub][cchunk * 128:(cchunk + 1) * 128, :]
                    e.dma_start(out=xnT2[:, cchunk * WIN + sub * 512:
                                         cchunk * WIN + (sub + 1) * 512], in_=src
                                ).then_inc(swq[0].h, 16)
        blk.scalar(body)
        swq[0].v += 512

        def body(e, half=half, STP_H=STP_H, SWP_BASE=SWP_BASE, HIDV0=HIDV0,
                 W3P=W3_T_PREV[0]):
            e.wait_ge(swq[0].h, SWQ_BASE[0] + 512)
            if half == 1:
                e.wait_ge(v_ms.h, HIDV0)
            for hq in range(NHQ):
                e.wait_ge(swp[hq % 2].h, SWP_BASE[hq % 2] + 512 * (hq // 2 + 1))
                for ht in range(4):
                    hti = hq * 4 + ht
                    pspar = psA if hti % 2 == 0 else psB
                    x1ps = pspar[:, 0:1024]
                    x2ps = pspar[:, 1024:2048]
                    if hti >= 2:
                        e.wait_ge(v_ms.h, HIDV0 + hti - 1)
                    last = None
                    for cchunk in range(16):
                        lhs1 = wp1[hq % 2][:, cchunk * 512 + ht * 128: cchunk * 512 + ht * 128 + 128]
                        lhs2 = wp2[hq % 2][:, cchunk * 512 + ht * 128: cchunk * 512 + ht * 128 + 128]
                        for nn in range(2):
                            e.matmul(x1ps[:, nn * 512:(nn + 1) * 512], lhs1,
                                     xnT2[:, cchunk * WIN + nn * 512:cchunk * WIN + (nn + 1) * 512],
                                     start=(cchunk == 0), stop=(cchunk == 15))
                        for nn in range(2):
                            last = e.matmul(x2ps[:, nn * 512:(nn + 1) * 512], lhs2,
                                     xnT2[:, cchunk * WIN + nn * 512:cchunk * WIN + (nn + 1) * 512],
                                     start=(cchunk == 0), stop=(cchunk == 15))
                    last.then_inc(t_ms.h, 1)
        blk.tensor(body)
        t_ms.v += NHT
        X1_PREV[0] = t_ms.v

        def wp2load(e, hq):
            for cchunk in range(16):
                e.dma_start(out=wp2[hq % 2][:, cchunk * 512:(cchunk + 1) * 512],
                            in_=w2q_all[hq * 512:(hq + 1) * 512,
                                        cchunk * 128:(cchunk + 1) * 128],
                            transpose=True).then_inc(swq[hq % 2].h, 16)
            swq[hq % 2].v += 256

        def body(e, half=half, X1T0=X1T0, HIDV0=HIDV0, wp2load=wp2load,
                 X1P=X1P_H):
            for hti in range(NHT):
                pspar = psA if hti % 2 == 0 else psB
                e.wait_ge(t_ms.h, X1T0 + hti + 1)
                if hti >= 1:
                    e.wait_ge(v_ms.h, HIDV0 + hti)
                e.activation(sgm[:], pspar[:, 0:1024], AF.Sigmoid).then_inc(a_ms.h, 1)
        blk.scalar(body)
        a_ms.v += NHT

        def body(e, SGA0=SGA0, HIDV0=HIDV0):
            for hti in range(NHT):
                pspar = psA if hti % 2 == 0 else psB
                e.wait_ge(a_ms.h, SGA0 + hti + 1)
                e.tensor_tensor(hidT[:, hti * WIN:(hti + 1) * WIN], sgm[:],
                                pspar[:, 1024:2048], ALU.mult).then_inc(v_ms.h, 1)
        blk.vector(body)
        v_ms.v += NHT
        HID_END_V = v_ms.v

        W3T0 = t_ms.v
        W3A0 = a_ms.v
        W3V0 = v_ms.v

        SW3_BASE = tuple(s.v for s in sw3)
        def body(e, W3T0=W3T0, HID_END_V=HID_END_V):
            e.wait_ge(v_ms.h, HID_END_V)
            e.wait_ge(cc3c.h, 1)
            for cq in range(4):
                for hc in range(NHT):
                    idx = cq * NHT + hc
                    if idx >= 4:
                        e.wait_ge(t_ms.h, W3T0 + idx - 3)
                    slot = idx % 4
                    e.dma_start(out=w3t[slot][:],
                                in_=w3q_all[cq * 512:(cq + 1) * 512,
                                            hc * 128:(hc + 1) * 128],
                                transpose=True).then_inc(sw3[slot].h, 16)
                    sw3[slot].v += 16
        blk.sync(body)

        def body(e, W3T0=W3T0, W3V0=W3V0, SW3_BASE=SW3_BASE):
            for cq in range(4):
                for hc in range(NHT):
                    idx = cq * NHT + hc
                    slot = idx % 4
                    e.wait_ge(sw3[slot].h, SW3_BASE[slot] + 16 * (idx // 4 + 1))
                    last = None
                    for tt in range(8):
                        if hc == 0 and cq > 0:
                            e.wait_ge(v_ms.h, W3V0 + (cq - 1) * 8 + tt + 1)
                        ps = (psA if tt < 4 else psB)[:, (tt % 4) * 512:(tt % 4 + 1) * 512]
                        last = e.matmul(ps, hidT[:, hc * WIN + tt * 128: hc * WIN + (tt + 1) * 128],
                                 w3t[slot][:, :], start=(hc == 0), stop=(hc == NHT - 1))
                    last.then_inc(t_ms.h, 1)
        blk.tensor(body)
        t_ms.v += 4 * NHT
        W3_T_PREV[0] = t_ms.v

        def body(e, W3T0=W3T0, W3V0=W3V0):
            for cq in range(4):
                e.wait_ge(t_ms.h, W3T0 + (cq + 1) * NHT)
                for tt in range(8):
                    idx = cq * 8 + tt
                    if idx >= 1:
                        e.wait_ge(v_ms.h, W3V0 + idx)
                    ps = (psA if tt < 4 else psB)[:, (tt % 4) * 512:(tt % 4 + 1) * 512]
                    e.activation(sgm[:, 0:512], ps, AF.Sigmoid).then_inc(a_ms.h, 1)
        blk.scalar(body)
        a_ms.v += 32

        GG_BASE = (gg[0].v, gg[1].v)
        def body(e, W3A0=W3A0, W3V0=W3V0, GG_BASE=GG_BASE):
            for cq in range(4):
                for tt in range(8):
                    idx = cq * 8 + tt
                    ps = (psA if tt < 4 else psB)[:, (tt % 4) * 512:(tt % 4 + 1) * 512]
                    e.wait_ge(a_ms.h, W3A0 + idx + 1)
                    e.wait_ge(gg[idx % 2].h, GG_BASE[idx % 2] + 16 * (idx // 2))
                    e.tensor_tensor(gbf[idx % 2][:], sgm[:, 0:512], ps,
                                    ALU.mult).then_inc(v_ms.h, 1)
        blk.vector(body)
        v_ms.v += 32

        def body(e, half=half, W3V0=W3V0):
            for cq in range(4):
                for tt in range(8):
                    idx = cq * 8 + tt
                    e.wait_ge(v_ms.h, W3V0 + idx + 1)
                    e.dma_start(out=g_sc[half * WIN + tt * 128: half * WIN + (tt + 1) * 128,
                                         cq * 512:(cq + 1) * 512],
                                in_=gbf[idx % 2][:]).then_inc(gg[idx % 2].h, 16)
                    gg[idx % 2].v += 16
        blk.gpsimd(body)

    barrier()
    p2.close()

    p3 = contextlib.ExitStack()
    def sb3(name, shape, dt):
        return p3.enter_context(nc.sbuf_tensor(name, shape, dt))
    yT_sb = sb3("yT_sb", [128, 16 * 2048], BF16)
    wo_sb = sb3("wo_sb", [128, 16 * 2048], BF16)
    bo_row = sb3("bo_row", [1, 2048], F32)
    bobc = sb3("bobc", [128, 2048], F32)
    xt3 = [sb3("xt3a", [128, 2048], F32), sb3("xt3b", [128, 2048], F32)]
    gt3 = [sb3("gt3a", [128, 2048], BF16), sb3("gt3b", [128, 2048], BF16)]
    s_bf = [sb3("sbfa", [128, 2048], BF16), sb3("sbfb", [128, 2048], BF16)]
    sg_f = sb3("sg_f", [128, 2048], F32)

    P3G0 = g_ld.v
    P3S0 = s_ld.v
    P3T0 = t_ms.v
    P3V0 = v_ms.v
    P3GS0 = g_st.v

    def body(e):
        e.wait_ge(g_st.h, WO_CAST_G)
        e.dma_start(out=wo_sb[:, 0:8192],
                    in_=wo_bf[0:512, :]
                    .rearrange("(a p) n -> p a n", p=128)).then_inc(g_yw.h, 16)
        e.wait_ge(cc2.h, 1)
        for q in range(2, 4):
            e.dma_start(out=yT_sb[:, q * 8192:(q + 1) * 8192],
                        in_=yt_my[q * 512:(q + 1) * 512, :]
                        .rearrange("(a p) t -> p a t", p=128)).then_inc(g_yw.h, 16)
    blk.gpsimd(body)
    g_yw.v += 3 * 16
    YWO_G = g_yw.v

    SWQ3_BASE = swq[1].v
    def body(e):
        e.wait_ge(g_st.h, WO_CAST_G)
        for q in range(1, 4):
            e.dma_start(out=wo_sb[:, q * 8192:(q + 1) * 8192],
                        in_=wo_bf[q * 512:(q + 1) * 512, :]
                        .rearrange("(a p) n -> p a n", p=128)).then_inc(swq[1].h, 16)
        for tt in range(16):
            if tt >= 2:
                e.wait_ge(v_ms.h, P3V0 + 1 + 3 * (tt - 2) + 3)
            e.dma_start(out=gt3[tt % 2][:], in_=g_sc[tt * 128:(tt + 1) * 128, :]
                        ).then_inc(gt3s[tt % 2].h, 16)
            gt3s[tt % 2].v += 16
    blk.scalar(body)
    swq[1].v += 3 * 16

    def body(e):
        e.dma_start(out=bo_row[:], in_=bo_in[:, :]).then_inc(s_ld.h, 16)
        s_ld.v += 16
        e.wait_ge(cc2.h, 1)
        for q in range(2):
            e.dma_start(out=yT_sb[:, q * 8192:(q + 1) * 8192],
                        in_=yt_my[q * 512:(q + 1) * 512, :]
                        .rearrange("(a p) t -> p a t", p=128)).then_inc(s_ld.h, 16)
        s_ld.v += 2 * 16
        for tt in range(16):
            if tt >= 2:
                e.wait_ge(go[tt % 2].h, 16 * (tt // 2))
            e.dma_start(out=xt3[tt % 2][:], in_=x_my[tt * 128:(tt + 1) * 128, :]
                        ).then_inc(sx3[tt % 2].h, 16)
            sx3[tt % 2].v += 16
    blk.sync(body)

    def body(e):
        e.wait_ge(s_ld.h, P3S0 + 16)
        last = None
        for nn in range(4):
            last = e.matmul(psB[:, nn * 512:(nn + 1) * 512], ones_f[0:1, :],
                     bo_row[0:1, nn * 512:(nn + 1) * 512], start=True, stop=True)
        last.then_inc(t_ms.h, 1)
        e.wait_ge(g_yw.h, YWO_G)
        e.wait_ge(swq[1].h, SWQ3_BASE + 3 * 16)
        e.wait_ge(s_ld.h, P3S0 + 3 * 16)
        for tt in range(16):
            pspar = psA if tt % 2 == 0 else psB
            if tt == 1:
                e.wait_ge(v_ms.h, P3V0 + 1)
            if tt >= 2:
                e.wait_ge(v_ms.h, P3V0 + 1 + 3 * (tt - 2) + 1)
            last = None
            for cchunk in range(16):
                lhsT = yT_sb[:, cchunk * 2048 + tt * 128: cchunk * 2048 + tt * 128 + 128]
                for nn in range(4):
                    last = e.matmul(pspar[:, nn * 512:(nn + 1) * 512], lhsT,
                             wo_sb[:, cchunk * 2048 + nn * 512: cchunk * 2048 + (nn + 1) * 512],
                             start=(cchunk == 0), stop=(cchunk == 15))
            last.then_inc(t_ms.h, 1)
    blk.tensor(body)
    t_ms.v += 17

    def body(e):
        e.wait_ge(t_ms.h, P3T0 + 1)
        e.tensor_copy(bobc[:], psB[:, 0:2048]).then_inc(v_ms.h, 1)
        e.wait_ge(v_ms.h, P3V0 + 1)
        sfb3 = v_sf.v
        for tt in range(16):
            pspar = psA if tt % 2 == 0 else psB
            e.wait_ge(t_ms.h, P3T0 + tt + 2)
            e.tensor_tensor(s_bf[tt % 2][:], pspar[:, 0:2048], bobc[:],
                            ALU.add).then_inc(v_ms.h, 1)
            e.wait_ge(gt3s[tt % 2].h, 16 * (tt // 2 + 1))
            e.wait_ge(v_ms.h, P3V0 + 1 + 3 * tt + 1)
            e.tensor_tensor(sg_f[:], s_bf[tt % 2][:], gt3[tt % 2][:],
                            ALU.mult).then_inc(v_sf.h, 1)
            e.wait_ge(sx3[tt % 2].h, 16 * (tt // 2 + 1))
            e.wait_ge(v_sf.h, sfb3 + tt + 1)
            e.tensor_tensor(xt3[tt % 2][:], sg_f[:], xt3[tt % 2][:],
                            ALU.add).then_inc(v_ms.h, 2)
    blk.vector(body)
    v_ms.v += 1 + 3 * 16
    v_sf.v += 16

    def body(e):
        for tt in range(16):
            e.wait_ge(v_ms.h, P3V0 + 1 + 3 * tt + 3)
            e.dma_start(out=out_my[tt * 128:(tt + 1) * 128, :], in_=xt3[tt % 2][:]
                        ).then_inc(go[tt % 2].h, 16)
            go[tt % 2].v += 16
        e.wait_ge(go[0].h, go[0].v)
        e.wait_ge(go[1].h, go[1].v)
    blk.gpsimd(body)

    if debug_taps:
        tap_list = [("t_xn", xn_cq[0], [C, 512], BF16), ("t_gam", gam_s, [1, 4], F32),
                    ("t_w1q", w1qc, [RH, C], BF16), ("t_w3q", w3qc, [256, HID], BF16),
                    ("t_ytc", yt_c, [2048, 2048], BF16), ("t_ytm", yt_my, [2048, 2048], BF16),
                    ("t_g", g_sc, [MYT, C], BF16), ("t_xna", xn_aq[0], [NCORES * C, 512], BF16)]
        tap_out = {nm: nc.dram_tensor(nm, shp, dtp, kind="ExternalOutput")
                   for nm, _, shp, dtp in tap_list}
        def body(e):
            e.wait_ge(go[0].h, go[0].v)
            e.wait_ge(go[1].h, go[1].v)
            for nm, t, shp, dtp in tap_list:
                nrow = shp[0]
                step = max(1, 8192 * 2048 // max(1, int(np.prod(shp)))) if False else 512
                r = 0
                while r < nrow:
                    rr = min(step, nrow - r)
                    e.dma_start(out=tap_out[nm][r:r + rr, :], in_=t[r:r + rr, :]
                                ).then_inc(g_st.h, 16)
                    g_st.v += 16
                    r += rr
            e.wait_ge(g_st.h, g_st.v)
        blk.gpsimd(body)

    p3.close()
    ctx.close()
    return nc


def _make_mask():
    s = np.arange(128)[:, None]
    t = np.arange(512)[None, :]
    m = np.zeros((128, 2048), dtype=np.float32)
    for jj in range(4):
        m[:, jj * 512:(jj + 1) * 512] = (jj * 128 + s <= t).astype(np.float32)
    return m


def kernel(L0_attn_x, L0_attn_Wq, L0_attn_bq, L0_attn_Wk, L0_attn_bk, L0_attn_Wv, L0_attn_bv,
           L0_attn_Wo, L0_attn_bo, L0_ffn_w1, L0_ffn_w2, L0_ffn_w3):
    x = np.ascontiguousarray(np.asarray(L0_attn_x, dtype=np.float32))
    Wq = np.asarray(L0_attn_Wq, dtype=np.float32)
    Wk = np.asarray(L0_attn_Wk, dtype=np.float32)
    Wv = np.asarray(L0_attn_Wv, dtype=np.float32)
    bq = np.asarray(L0_attn_bq, dtype=np.float32)
    bk = np.asarray(L0_attn_bk, dtype=np.float32)
    bv = np.asarray(L0_attn_bv, dtype=np.float32)
    Wo = np.ascontiguousarray(np.asarray(L0_attn_Wo, dtype=np.float32))
    bo = np.asarray(L0_attn_bo, dtype=np.float32).reshape(1, C)
    w1 = np.asarray(L0_ffn_w1, dtype=np.float32)
    w2 = np.asarray(L0_ffn_w2, dtype=np.float32)
    w3 = np.asarray(L0_ffn_w3, dtype=np.float32)

    ident = np.eye(128, dtype=np.float32)
    ones = np.ones((128, 128), dtype=np.float32)
    mask = _make_mask()

    nc = build_nc()
    in_maps = []
    for c in range(NCORES):
        b = c // 4
        t0 = (c % 4) * MYT
        in_maps.append(dict(
            x_my=np.ascontiguousarray(x[b, t0:t0 + MYT, :]),
            wq_my=np.ascontiguousarray(Wq[2 * c:2 * c + 2]),
            wk_my=np.ascontiguousarray(Wk[2 * c:2 * c + 2]),
            wv_my=np.ascontiguousarray(Wv[2 * c:2 * c + 2]),
            bq_my=np.ascontiguousarray(bq[2 * c:2 * c + 2].T),
            bk_my=np.ascontiguousarray(bk[2 * c:2 * c + 2].T),
            bv_my=np.ascontiguousarray(bv[2 * c:2 * c + 2].T),
            wo_in=Wo,
            bo_in=bo,
            w1_my=np.ascontiguousarray(w1[RH * c:RH * (c + 1)]),
            w2_my=np.ascontiguousarray(w2[RH * c:RH * (c + 1)]),
            w3_my=np.ascontiguousarray(w3[256 * c:256 * (c + 1)]),
            ident_in=ident,
            ones_in=ones,
            mask_in=mask,
        ))
    res = run_bass_kernel_spmd(nc, in_maps, core_ids=list(range(NCORES)))
    out = np.empty((B, T, C), dtype=np.float32)
    for c in range(NCORES):
        b = c // 4
        t0 = (c % 4) * MYT
        out[b, t0:t0 + MYT, :] = np.asarray(res.results[c]["out_my"], dtype=np.float32)
    return out
